# version 04
# baseline (speedup 1.0000x reference)
import numpy as np
import ml_dtypes
import concourse.bass as bass
import concourse.mybir as mybir
from concourse.bass_utils import run_bass_kernel_spmd
from contextlib import ExitStack

F32 = mybir.dt.float32
BF16 = mybir.dt.bfloat16
AF = mybir.ActivationFunctionType
ALU = mybir.AluOpType

STREAMS = ("pe", "act", "dve", "pool", "sync")


class _Op:
    __slots__ = ("idx", "stream", "ctr", "fn", "reads", "writes", "deps",
                 "waits", "inc_val", "tag")

    def __init__(self, idx, stream, ctr, fn, reads, writes, tag):
        self.idx = idx
        self.stream = stream
        self.ctr = ctr
        self.fn = fn
        self.reads = reads
        self.writes = writes
        self.deps = []
        self.waits = []
        self.inc_val = None
        self.tag = tag


class Sched:
    def __init__(self):
        self.ops = []
        self.last_write = {}
        self.readers = {}
        self.last_on_ctr = {}

    def op(self, stream, fn, reads=(), writes=(), dma=False, tag="",
           chan="d", serial=False):
        ctr = (stream + ":" + chan) if dma else stream
        idx = len(self.ops)
        o = _Op(idx, stream, ctr, fn, tuple(reads), tuple(writes), tag)
        deps = set()
        for r in o.reads:
            j = self.last_write.get(r)
            if j is not None:
                deps.add(j)
        for w in o.writes:
            j = self.last_write.get(w)
            if j is not None:
                deps.add(j)
            for rd in self.readers.get(w, ()):
                deps.add(rd)
        if dma and serial:
            p = self.last_on_ctr.get(ctr)
            if p is not None:
                deps.add(p)
        deps.discard(idx)
        o.deps = sorted(deps)
        self.ops.append(o)
        for r in o.reads:
            self.readers.setdefault(r, []).append(idx)
        for w in o.writes:
            self.last_write[w] = idx
            self.readers[w] = []
        self.last_on_ctr[ctr] = idx
        return idx

    def pe(self, fn, reads=(), writes=(), tag=""):
        return self.op("pe", fn, reads, writes, tag=tag)

    def act(self, fn, reads=(), writes=(), tag=""):
        return self.op("act", fn, reads, writes, tag=tag)

    def dve(self, fn, reads=(), writes=(), tag=""):
        return self.op("dve", fn, reads, writes, tag=tag)

    def pool(self, fn, reads=(), writes=(), tag=""):
        return self.op("pool", fn, reads, writes, tag=tag)

    def dma(self, queue, fn, reads=(), writes=(), tag="", chan="d",
            serial=False):
        return self.op(queue, fn, reads, writes, dma=True, tag=tag,
                       chan=chan, serial=serial)

    def _resolve(self):
        n = len(self.ops)
        ctrs = sorted({o.ctr for o in self.ops})
        cidx = {c: i for i, c in enumerate(ctrs)}
        nc_ = len(ctrs)
        seq = [0] * n
        cnt = {c: 0 for c in ctrs}
        for o in self.ops:
            cnt[o.ctr] += 1
            seq[o.idx] = cnt[o.ctr]
        ZERO = tuple([0] * nc_)

        def vmax(a, b):
            return tuple(x if x >= y else y for x, y in zip(a, b))

        start_know = [ZERO] * n
        complete = [ZERO] * n
        prev_on_stream = {}
        for o in self.ops:
            i = o.idx
            know = ZERO
            p = prev_on_stream.get(o.stream)
            if p is not None:
                po = self.ops[p]
                know = start_know[p] if (":" in po.ctr) else complete[p]
            for d in o.deps:
                if know[cidx[self.ops[d].ctr]] >= seq[d]:
                    continue
                do = self.ops[d]
                if do.stream == o.stream and not (":" in do.ctr) \
                        and not (":" in o.ctr):
                    know = vmax(know, complete[d])
                    continue
                o.waits.append(d)
                know = vmax(know, complete[d])
            start_know[i] = know
            me = list(know)
            j = cidx[o.ctr]
            if seq[o.idx] > me[j]:
                me[j] = seq[o.idx]
            complete[i] = tuple(me)
            prev_on_stream[o.stream] = i
        marked = set()
        for o in self.ops:
            if (":" in o.ctr):
                marked.add(o.idx)
            for d in o.waits:
                marked.add(d)
        val = {}
        mcnt = {c: 0 for c in ctrs}
        for o in self.ops:
            if o.idx in marked:
                mcnt[o.ctr] += 1
                step = 16 if (":" in o.ctr) else 1
                o.inc_val = mcnt[o.ctr] * step
                val[o.idx] = o.inc_val
        for o in self.ops:
            best = {}
            for d in o.waits:
                c, v = self.ops[d].ctr, val[d]
                if best.get(c, -1) < v:
                    best[c] = v
            o.waits = sorted(best.items())
        return ctrs

    def emit(self, nc, es):
        ctrs = self._resolve()
        sems = {c: es.enter_context(nc.semaphore("sem_" + c)) for c in ctrs}
        by_stream = {s: [] for s in STREAMS}
        for o in self.ops:
            by_stream[o.stream].append(o)
        stats = {
            "n_ops": len(self.ops),
            "n_waits": sum(len(o.waits) for o in self.ops),
            "n_incs": sum(1 for o in self.ops if o.inc_val is not None),
        }

        def make_emitter(stream):
            def emitter(eng):
                for o in by_stream[stream]:
                    for c, v in o.waits:
                        eng.wait_ge(sems[c], v)
                    inst = o.fn(eng)
                    if o.inc_val is not None:
                        if inst is None:
                            raise RuntimeError(
                                f"op {o.idx} ({o.tag}) needs inc, fn gave None")
                        inst.then_inc(sems[o.ctr],
                                      16 if (":" in o.ctr) else 1)
            return emitter

        return {st: make_emitter(st) for st in STREAMS}, stats


S = 4096
D = 1024
NCH = 8
NQ = 4
QT = S // NQ
DFF = 4096
NEXP = 8

TRI_C = 0
B2SB_C = 2048
WRT_C = 3072
CON_C = 3584
BASE_A = 4096
XN_C = BASE_A
QTB_C = XN_C + 8 * 4096
KTB_C = QTB_C + 4096
O2P_C = KTB_C + 4096
V2A_C = O2P_C + 4096
PB_C = V2A_C + 4224
WQS_C = PB_C + 2048
WKS_C = WQS_C + 2048
WVS_C = WKS_C + 2048
TOP_A = WVS_C + 2048
XN2_C = BASE_A
W1S_C = XN2_C + 8 * 1024
W2S_C = W1S_C + 16384
H_C = W2S_C + 16384
H0_C = H_C + 4096
O2H_C = H0_C + 1024
WOS_C = O2H_C + 4096
MKB_C = WOS_C + 8192
MK8_C = MKB_C + 1024
MRE_C = MK8_C + 1024
TOP_B = MRE_C + 1024
XB_C = TOP_B
XQ_C = XB_C + 1024
NB = XQ_C + 1024

CP_C = 0
OUTB_C = 16
BR8_C = 24
ONF_C = 128
ROW_C = 512
XF_C = ROW_C + 6 * 512
T0_C = XF_C + 4 * 512
XO_C = T0_C + 1024
LGT_C = XO_C + 1024
TMA_C = LGT_C + 1024
TMB_C = XO_C
XA_C = TMA_C + 1024
B1S_C = XA_C + 8 * 1024
NF = B1S_C + 1024


def _pages(arena, c0, c1):
    return [(arena, p) for p in range(c0 // 512, (c1 + 511) // 512)]


def build_program(mode="full"):
    nc = bass.Bass("TRN2", target_bir_lowering=False, debug=False)
    xT = nc.dram_tensor("xT", [D, S], F32, kind="ExternalInput").ap()
    wq = nc.dram_tensor("wq", [D, D], F32, kind="ExternalInput").ap()
    wk = nc.dram_tensor("wk", [D, D], F32, kind="ExternalInput").ap()
    wv = nc.dram_tensor("wv", [D, D], F32, kind="ExternalInput").ap()
    wo = nc.dram_tensor("wo", [D, D], F32, kind="ExternalInput").ap()
    outb = nc.dram_tensor("outb", [128, 8], F32, kind="ExternalInput").ap()
    wrt = nc.dram_tensor("wrt", [D, NEXP], F32, kind="ExternalInput").ap()
    br8 = nc.dram_tensor("br8", [NEXP, 1], F32, kind="ExternalInput").ap()
    w1t = nc.dram_tensor("w1t", [NEXP, D, DFF], F32, kind="ExternalInput").ap()
    b1c = nc.dram_tensor("b1c", [NEXP, 128, 32], F32, kind="ExternalInput").ap()
    w2t = nc.dram_tensor("w2t", [NEXP, DFF, D], F32, kind="ExternalInput").ap()
    b2e = nc.dram_tensor("b2e", [NEXP, D], F32, kind="ExternalInput").ap()
    tri = nc.dram_tensor("tri", [128, 2048], BF16, kind="ExternalInput").ap()
    onesb = nc.dram_tensor("onesb", [128, 2], BF16, kind="ExternalInput").ap()
    o2s = nc.dram_tensor("o2s", [D, S], BF16, kind="Internal").ap()
    outT = nc.dram_tensor("outT", [D, S], F32, kind="ExternalOutput").ap()
    if mode == "dbg":
        dbg_xac = nc.dram_tensor("dbg_xac", [D, S], F32, kind="ExternalOutput").ap()
        dbg_xn2 = nc.dram_tensor("dbg_xn2", [D, S], BF16, kind="ExternalOutput").ap()
        dbg_lgt = nc.dram_tensor("dbg_lgt", [8, S], F32, kind="ExternalOutput").ap()
        dbg_mk8 = nc.dram_tensor("dbg_mk8", [8, S], BF16, kind="ExternalOutput").ap()

    s = Sched()
    with ExitStack() as es:
        AB = es.enter_context(nc.sbuf_tensor("AB", [128, NB], BF16))
        AF32 = es.enter_context(nc.sbuf_tensor("AF32", [128, NF], F32))
        scoreP = es.enter_context(nc.psum_tensor("scoreP", [128, 2048], F32))
        pA = es.enter_context(nc.psum_tensor("pA", [128, 512], F32))
        pB = es.enter_context(nc.psum_tensor("pB", [128, 512], F32))
        pC = es.enter_context(nc.psum_tensor("pC", [128, 512], F32))
        pD = es.enter_context(nc.psum_tensor("pD", [128, 512], F32))
        def ps_res(t, off=0):
            if t is scoreP:
                return [("PS", off // 512)]
            return [("PS", {id(pA): 4, id(pB): 5, id(pC): 6, id(pD): 7}[id(t)])]

        def bf(c0, w, r0=0, rn=128):
            return AB[r0:rn, c0:c0 + w]

        def f32(c0, w, r0=0, rn=128):
            return AF32[r0:rn, c0:c0 + w]

        def bpg(c0, w):
            return _pages("B", c0, c0 + w)

        def fpg(c0, w):
            return _pages("F", c0, c0 + w)

        s.dma("sync", lambda e: e.dma_start(bf(TRI_C, 2048), tri),
              writes=bpg(TRI_C, 2048), tag="tri", chan="ini", serial=True)
        s.dma("sync", lambda e: e.dma_start(bf(CON_C, 2, 0, 128), onesb),
              writes=bpg(CON_C, 512), tag="ones", chan="ini", serial=True)
        s.dma("sync", lambda e: e.dma_start(f32(OUTB_C, 8), outb),
              writes=fpg(0, 512), tag="outb", chan="ini", serial=True)
        s.dma("sync", lambda e: e.dma_start(f32(BR8_C, 1, 0, 8), br8),
              writes=fpg(0, 512), tag="br8", chan="ini", serial=True)
        s.dma("pool", lambda e: e.dma_start(bf(B2SB_C, 1024, 0, 8), b2e),
              writes=bpg(B2SB_C, 1024), tag="b2", chan="ini", serial=True)
        s.dma("pool", lambda e: e.dma_start(
            bf(WRT_C, 64).rearrange("p (cc e) -> p cc e", e=8),
            wrt.rearrange("(cc p) e -> p cc e", p=128)),
              writes=bpg(WRT_C, 512), tag="wrt", chan="ini", serial=True)
        s.dve(lambda e: e.memset(f32(CP_C, 1), 0.125), writes=fpg(0, 512))
        s.dve(lambda e: e.memset(f32(CP_C + 1, 1), -12.0), writes=fpg(0, 512))
        s.dve(lambda e: e.memset(f32(CP_C + 2, 1), 1e-5), writes=fpg(0, 512))
        s.dve(lambda e: e.memset(f32(CP_C + 3, 1), 1.0 / 1024.0), writes=fpg(0, 512))
        s.dve(lambda e: e.memset(f32(CP_C + 4, 1), -1.0), writes=fpg(0, 512))
        s.dve(lambda e: e.memset(f32(ONF_C, 128, 0, 1), 1.0), writes=fpg(ONF_C, 128))
        CPQ = fpg(0, 512)
        qscale = f32(CP_C, 1)
        ebias = f32(CP_C + 1, 1)
        lneps = f32(CP_C + 2, 1, 0, 1)
        inv1024 = f32(CP_C + 3, 1, 0, 1)
        neg1 = f32(CP_C + 4, 1, 0, 1)
        onesf = f32(ONF_C, 128, 0, 1)
        onescol_b = bf(CON_C, 1, 0, 128)
        onesr = nc.dram_tensor("onesr", [1, 128], BF16, kind="ExternalInput").ap()
        s.dma("sync", lambda e: e.dma_start(bf(CON_C + 2, 128, 0, 1), onesr),
              writes=bpg(CON_C, 512), tag="onesr", chan="ini", serial=True)
        onesrow_b = bf(CON_C + 2, 128, 0, 1)

        for tt in range(8):
            ts = slice(tt * 512, (tt + 1) * 512)
            for cc in range(NCH):
                xfc = XF_C + (cc % 2) * 512
                xfv = f32(xfc, 512)
                s.dma("sync", lambda e, cc=cc, ts=ts, xfv=xfv: e.dma_start(
                    xfv, xT[cc * 128:(cc + 1) * 128, ts]),
                    writes=fpg(xfc, 512), tag="lnx", chan="xs")
                xbv = bf(XB_C + (cc % 2) * 512, 512)
                s.pool(lambda e, xfv=xfv, xbv=xbv: e.tensor_copy(xbv, xfv),
                       reads=fpg(xfc, 512),
                       writes=bpg(XB_C + (cc % 2) * 512, 512), tag="lncast")
                xqv = bf(XQ_C + (cc % 2) * 512, 512)
                s.act(lambda e, xbv=xbv, xqv=xqv: e.activation(xqv, xbv, AF.Square),
                      reads=bpg(XB_C + (cc % 2) * 512, 512),
                      writes=bpg(XQ_C + (cc % 2) * 512, 512), tag="lnsq")
                s.pe(lambda e, xbv=xbv, cc=cc: e.matmul(
                        pC[0:1, :], onescol_b, xbv, start=(cc == 0), stop=(cc == 7)),
                     reads=bpg(XB_C + (cc % 2) * 512, 512) + bpg(CON_C, 512),
                     writes=ps_res(pC), tag="lns1")
                s.pe(lambda e, xqv=xqv, cc=cc: e.matmul(
                        pD[0:1, :], onescol_b, xqv, start=(cc == 0), stop=(cc == 7)),
                     reads=bpg(XQ_C + (cc % 2) * 512, 512) + bpg(CON_C, 512),
                     writes=ps_res(pD), tag="lns2")
            r = [f32(ROW_C + i * 512, 512, 0, 1) for i in range(6)]
            RP = [fpg(ROW_C + i * 512, 512) for i in range(6)]
            s.dve(lambda e, r=r: e.tensor_scalar(r[0], pC[0:1, :], inv1024, None, ALU.mult),
                  reads=ps_res(pC) + CPQ, writes=RP[0], tag="lnm")
            s.dve(lambda e, r=r: e.tensor_scalar(r[1], pD[0:1, :], inv1024, None, ALU.mult),
                  reads=ps_res(pD) + CPQ, writes=RP[1], tag="lnq")
            s.dve(lambda e, r=r: e.tensor_tensor(r[2], r[0], r[0], ALU.mult),
                  reads=RP[0], writes=RP[2], tag="lnm2")
            s.dve(lambda e, r=r: e.tensor_tensor(r[2], r[1], r[2], ALU.subtract),
                  reads=RP[1] + RP[2], writes=RP[2], tag="lnvar")
            s.act(lambda e, r=r: e.activation(r[3], r[2], AF.Sqrt, bias=lneps),
                  reads=RP[2] + CPQ, writes=RP[3], tag="lnsd")
            s.dve(lambda e, r=r: e.reciprocal(r[4], r[3]),
                  reads=RP[3], writes=RP[4], tag="lnrstd")
            s.dve(lambda e, r=r: e.scalar_tensor_tensor(r[5], r[0], neg1, r[4],
                                                        ALU.mult, ALU.mult),
                  reads=RP[0] + RP[4] + CPQ, writes=RP[5], tag="lnb")
            s.pe(lambda e, r=r: e.matmul(pA[:, :], onesf, r[4], start=True, stop=True),
                 reads=RP[4] + fpg(ONF_C, 128), writes=ps_res(pA), tag="lnbca")
            s.pe(lambda e, r=r: e.matmul(pB[:, :], onesf, r[5], start=True, stop=True),
                 reads=RP[5] + fpg(ONF_C, 128), writes=ps_res(pB), tag="lnbcb")
            for cc in range(NCH):
                xfc = XF_C + (2 + cc % 2) * 512
                xfv = f32(xfc, 512)
                s.dma("sync", lambda e, cc=cc, ts=ts, xfv=xfv: e.dma_start(
                    xfv, xT[cc * 128:(cc + 1) * 128, ts]),
                    writes=fpg(xfc, 512), tag="lnx2", chan="xs")
                t0v = f32(T0_C + (cc % 2) * 512, 512)
                s.dve(lambda e, xfv=xfv, t0v=t0v: e.tensor_tensor(t0v, xfv, pA[:, :], ALU.mult),
                      reads=fpg(xfc, 512) + ps_res(pA),
                      writes=fpg(T0_C + (cc % 2) * 512, 512), tag="lnn1")
                xnv = bf(XN_C + cc * 4096 + tt * 512, 512)
                s.dve(lambda e, xnv=xnv, t0v=t0v: e.tensor_tensor(xnv, t0v, pB[:, :], ALU.add),
                      reads=fpg(T0_C + (cc % 2) * 512, 512) + ps_res(pB),
                      writes=bpg(XN_C + cc * 4096 + tt * 512, 512), tag="lnn2")

        xn = [bf(XN_C + cc * 4096, 4096) for cc in range(NCH)]
        xn_pg = [bpg(XN_C + cc * 4096, 4096) for cc in range(NCH)]
        v2a_full = bf(V2A_C, 4160)
        s.dve(lambda e: e.memset(
            v2a_full.rearrange("p (tc w) -> p tc w", w=130)[:, :, 64:65], 1.0),
            writes=bpg(V2A_C, 4160), tag="v2ones1")
        s.dve(lambda e: e.memset(
            v2a_full.rearrange("p (tc w) -> p tc w", w=130)[:, :, 129:130], 1.0),
            writes=bpg(V2A_C, 4160), tag="v2ones2")

        evrot = [0]

        def evac_bank():
            b = evrot[0] % 4
            evrot[0] += 1
            return scoreP[:, b * 512:(b + 1) * 512], [("PS", b)]

        for hp in range(8):
            par = hp % 2
            wq_s = bf(WQS_C + par * 1024, 1024)
            wk_s = bf(WKS_C + par * 1024, 1024)
            wv_s = bf(WVS_C + par * 1024, 1024)
            wq_pg = bpg(WQS_C + par * 1024, 1024)
            wk_pg = bpg(WKS_C + par * 1024, 1024)
            wv_pg = bpg(WVS_C + par * 1024, 1024)
            for (w_s, w_pg, wsrc, nm) in ((wq_s, wq_pg, wq, "wq"),
                                          (wk_s, wk_pg, wk, "wk"),
                                          (wv_s, wv_pg, wv, "wv")):
                src = wsrc.rearrange("(cc p) d -> p cc d", p=128)[:, :, hp * 128:(hp + 1) * 128]
                s.dma("pool", lambda e, w_s=w_s, src=src: e.dma_start(
                    w_s.rearrange("p (cc d) -> p cc d", d=128), src),
                    writes=w_pg, tag=nm, chan="wqv")
            qtb = bf(QTB_C, 4096)
            ktb = bf(KTB_C, 4096)
            for (dst_c, w_s, w_pg, sc, nm) in ((QTB_C, wq_s, wq_pg, qscale, "qt"),
                                               (KTB_C, wk_s, wk_pg, None, "kt")):
                for tt in range(8):
                    ts = slice(tt * 512, (tt + 1) * 512)
                    bank, bres = evac_bank()
                    for cc in range(NCH):
                        s.pe(lambda e, bank=bank, w_s=w_s, cc=cc, ts=ts: e.matmul(
                                bank, w_s[:, cc * 128:(cc + 1) * 128], xn[cc][:, ts],
                                start=(cc == 0), stop=(cc == 7)),
                             reads=w_pg + xn_pg[cc], writes=bres, tag=nm)
                    dst = bf(dst_c + tt * 512, 512)
                    if sc is None:
                        s.act(lambda e, dst=dst, bank=bank: e.copy(dst, bank),
                              reads=bres, writes=bpg(dst_c + tt * 512, 512), tag=nm + "ev")
                    else:
                        s.act(lambda e, dst=dst, bank=bank, sc=sc: e.activation(
                                dst, bank, AF.Copy, scale=sc),
                              reads=bres + CPQ, writes=bpg(dst_c + tt * 512, 512),
                              tag=nm + "ev")
            for tc in range(32):
                bank = (pC, ps_res(pC)) if tc % 2 == 0 else (pD, ps_res(pD))
                bk, bres = bank
                for cc in range(NCH):
                    s.pe(lambda e, bk=bk, cc=cc, tc=tc, wv_s=wv_s: e.matmul(
                            bk[:, 0:128], xn[cc][:, tc * 128:(tc + 1) * 128],
                            wv_s[:, cc * 128:(cc + 1) * 128],
                            start=(cc == 0), stop=(cc == 7)),
                         reads=wv_pg + xn_pg[cc], writes=bres, tag="v")
                va = bf(V2A_C + tc * 130, 64)
                vb = bf(V2A_C + tc * 130 + 65, 64)
                s.dve(lambda e, va=va, bk=bk: e.tensor_copy(va, bk[:, 0:64]),
                      reads=bres, writes=bpg(V2A_C + tc * 130, 64), tag="vev")
                s.dve(lambda e, vb=vb, bk=bk: e.tensor_copy(vb, bk[:, 64:128]),
                      reads=bres, writes=bpg(V2A_C + tc * 130 + 65, 64), tag="vev")
            for h in range(2):
                hb = 64 * h
                obank = pA if h == 0 else pB
                ob_res = ps_res(obank)
                for qt_i in range(8):
                    qts = slice(qt_i * 512, (qt_i + 1) * 512)
                    ng = 2 * (qt_i + 1)
                    prev = None
                    for g in range(ng):
                        gpar = g % 2
                        for j in range(2):
                            kc = 2 * g + j
                            sb = scoreP[:, gpar * 1024 + j * 512:
                                        gpar * 1024 + (j + 1) * 512]
                            sres = [("PS", gpar * 2 + j)]
                            s.pe(lambda e, sb=sb, kc=kc, hb=hb, qts=qts,
                                 ktb=ktb, qtb=qtb: e.matmul(
                                    sb, ktb[hb:hb + 64, kc * 128:(kc + 1) * 128],
                                    qtb[hb:hb + 64, qts], start=True, stop=True),
                                 reads=bpg(KTB_C + kc * 128 * 0, 4096) +
                                       bpg(QTB_C + qt_i * 512, 512),
                                 writes=sres, tag="sc")
                        pbv = bf(PB_C + gpar * 1024, 1024)
                        s.act(lambda e, pbv=pbv, gpar=gpar: e.activation(
                                pbv, scoreP[:, gpar * 1024:(gpar + 1) * 1024],
                                AF.Exp, bias=ebias),
                              reads=[("PS", gpar * 2), ("PS", gpar * 2 + 1)] + CPQ,
                              writes=bpg(PB_C + gpar * 1024, 1024), tag="exp")
                        if g >= ng - 2:
                            dj = (g - (ng - 2))
                            trv = bf(TRI_C + dj * 1024, 1024)
                            s.dve(lambda e, pbv=pbv, trv=trv: e.tensor_tensor(
                                    pbv, pbv, trv, ALU.mult),
                                  reads=bpg(PB_C + gpar * 1024, 1024) +
                                        bpg(TRI_C + dj * 1024, 1024),
                                  writes=bpg(PB_C + gpar * 1024, 1024), tag="trim")
                        if prev is not None:
                            pg, ppb = prev
                            for j in range(2):
                                kc = 2 * pg + j
                                s.pe(lambda e, kc=kc, h=h, ppb=ppb, j=j, pg=pg,
                                     obank=obank: e.matmul(
                                        obank[0:65, :],
                                        v2a_full[:, kc * 130 + 65 * h:
                                                 kc * 130 + 65 * h + 65],
                                        ppb[:, j * 512:(j + 1) * 512],
                                        start=(kc == 0), stop=False),
                                     reads=bpg(V2A_C + kc * 130, 130) +
                                           bpg(PB_C + (pg % 2) * 1024, 1024),
                                     writes=ob_res, tag="pv")
                        prev = (g, pbv)
                    pg, ppb = prev
                    for j in range(2):
                        kc = 2 * pg + j
                        last = (j == 1)
                        s.pe(lambda e, kc=kc, h=h, ppb=ppb, j=j, last=last, pg=pg,
                             obank=obank: e.matmul(
                                obank[0:65, :],
                                v2a_full[:, kc * 130 + 65 * h:kc * 130 + 65 * h + 65],
                                ppb[:, j * 512:(j + 1) * 512],
                                start=(kc == 0), stop=last),
                             reads=bpg(V2A_C + kc * 130, 130) +
                                   bpg(PB_C + (pg % 2) * 1024, 1024),
                             writes=ob_res, tag="pv")
                    r0 = f32(ROW_C, 512, 0, 1)
                    s.dve(lambda e, obank=obank, r0=r0: e.reciprocal(r0, obank[64:65, :]),
                          reads=ob_res, writes=fpg(ROW_C, 512), tag="recip")
                    s.pe(lambda e, r0=r0: e.matmul(pC[0:64, :], onesf[0:1, 0:64], r0,
                                                   start=True, stop=True),
                         reads=fpg(ROW_C, 512) + fpg(ONF_C, 128),
                         writes=ps_res(pC), tag="rbc")
                    dst = bf(O2P_C + qt_i * 512, 512, hb, hb + 64)
                    s.act(lambda e, dst=dst, obank=obank: e.copy(dst, obank[0:64, :]),
                          reads=ob_res, writes=bpg(O2P_C + qt_i * 512, 512), tag="oev")
                    s.dve(lambda e, dst=dst: e.tensor_tensor(dst, dst, pC[0:64, :],
                                                             ALU.mult),
                          reads=bpg(O2P_C + qt_i * 512, 512) + ps_res(pC),
                          writes=bpg(O2P_C + qt_i * 512, 512), tag="onorm")
            s.dma("sync", lambda e, hp=hp: e.dma_start(
                o2s[hp * 128:(hp + 1) * 128, :], bf(O2P_C, 4096)),
                reads=bpg(O2P_C, 4096), writes=[("o2s", hp)], tag="o2st", chan="o2w")

        for hp in range(8):
            s.dma("pool", lambda e, hp=hp: e.dma_start(
                bf(WOS_C + hp * 1024, 1024), wo[hp * 128:(hp + 1) * 128, :]),
                writes=bpg(WOS_C + hp * 1024, 1024), tag="wo", chan="wo")

        brot = [0]

        def rot_bank(group):
            if group == "z":
                b = brot[0] % 2
                brot[0] += 1
                return scoreP[:, b * 512:(b + 1) * 512], [("PS", b)]
            if group == "y":
                b = brot[0] % 2
                brot[0] += 1
                return scoreP[:, (2 + b) * 512:(3 + b) * 512], [("PS", 2 + b)]
            raise ValueError

        urot = [0]

        def u_bank():
            b = urot[0] % 2
            urot[0] += 1
            t = pC if b == 0 else pD
            return t[:, :], ps_res(t)

        xacc = [f32(XA_C + c * 1024, 1024) for c in range(NCH)]
        xacc_pg = [fpg(XA_C + c * 1024, 1024) for c in range(NCH)]
        xn2 = [bf(XN2_C + c * 1024, 1024) for c in range(NCH)]
        xn2_pg = [bpg(XN2_C + c * 1024, 1024) for c in range(NCH)]

        for q in range(NQ):
            qc = q * QT
            for tp in range(2):
                c0 = qc + tp * 512
                o2h = bf(O2H_C, 4096)
                s.dma("sync", lambda e, c0=c0, o2h=o2h: e.dma_start(
                    o2h.rearrange("p (hp t) -> p hp t", t=512),
                    o2s.rearrange("(hp p) t -> p hp t", p=128)[:, :, c0:c0 + 512]),
                    reads=[("o2s", hp) for hp in range(8)],
                    writes=bpg(O2H_C, 4096), tag="o2h", chan="o2h")
                for do in range(NCH):
                    bank, bres = evac_bank()
                    for hp in range(8):
                        s.pe(lambda e, bank=bank, hp=hp, do=do, o2h=o2h: e.matmul(
                                bank, bf(WOS_C + hp * 1024 + do * 128, 128),
                                o2h[:, hp * 512:(hp + 1) * 512],
                                start=(hp == 0), stop=(hp == 7)),
                             reads=bpg(WOS_C + hp * 1024, 1024) + bpg(O2H_C, 4096),
                             writes=bres, tag="wop")
                    xov = f32(XO_C + (do % 2) * 512, 512)
                    s.dma("sync", lambda e, xov=xov, do=do, c0=c0: e.dma_start(
                        xov, xT[do * 128:(do + 1) * 128, c0:c0 + 512]),
                        writes=fpg(XO_C + (do % 2) * 512, 512), tag="xo", chan="xs")
                    dst = f32(XA_C + do * 1024 + tp * 512, 512)
                    s.dve(lambda e, dst=dst, bank=bank, do=do, xov=xov:
                          e.scalar_tensor_tensor(dst, bank, f32(OUTB_C + do, 1),
                                                 xov, ALU.add, ALU.add),
                          reads=bres + fpg(XO_C + (do % 2) * 512, 512) + fpg(0, 512),
                          writes=fpg(XA_C + do * 1024 + tp * 512, 512), tag="b1acc")
            if mode == "attn":
                for do in range(NCH):
                    s.dma("sync", lambda e, do=do, qc=qc: e.dma_start(
                        outT[do * 128:(do + 1) * 128, qc:qc + QT], xacc[do]),
                        reads=xacc_pg[do], writes=[("out", q, do)], tag="st", chan="st")
                continue
            for tp in range(2):
                for cc in range(NCH):
                    src = f32(XA_C + cc * 1024 + tp * 512, 512)
                    xbv = bf(XB_C + (cc % 2) * 512, 512)
                    s.pool(lambda e, src=src, xbv=xbv: e.tensor_copy(xbv, src),
                           reads=fpg(XA_C + cc * 1024 + tp * 512, 512),
                           writes=bpg(XB_C + (cc % 2) * 512, 512), tag="l2cast")
                    xqv = bf(XQ_C + (cc % 2) * 512, 512)
                    s.act(lambda e, xbv=xbv, xqv=xqv: e.activation(xqv, xbv, AF.Square),
                          reads=bpg(XB_C + (cc % 2) * 512, 512),
                          writes=bpg(XQ_C + (cc % 2) * 512, 512), tag="l2sq")
                    s.pe(lambda e, xbv=xbv, cc=cc: e.matmul(
                            pC[0:1, :], onescol_b, xbv, start=(cc == 0), stop=(cc == 7)),
                         reads=bpg(XB_C + (cc % 2) * 512, 512) + bpg(CON_C, 512),
                         writes=ps_res(pC), tag="l2s1")
                    s.pe(lambda e, xqv=xqv, cc=cc: e.matmul(
                            pD[0:1, :], onescol_b, xqv, start=(cc == 0), stop=(cc == 7)),
                         reads=bpg(XQ_C + (cc % 2) * 512, 512) + bpg(CON_C, 512),
                         writes=ps_res(pD), tag="l2s2")
                r = [f32(ROW_C + i * 512, 512, 0, 1) for i in range(6)]
                RP = [fpg(ROW_C + i * 512, 512) for i in range(6)]
                s.dve(lambda e, r=r: e.tensor_scalar(r[0], pC[0:1, :], inv1024, None, ALU.mult),
                      reads=ps_res(pC) + CPQ, writes=RP[0], tag="l2m")
                s.dve(lambda e, r=r: e.tensor_scalar(r[1], pD[0:1, :], inv1024, None, ALU.mult),
                      reads=ps_res(pD) + CPQ, writes=RP[1], tag="l2q")
                s.dve(lambda e, r=r: e.tensor_tensor(r[2], r[0], r[0], ALU.mult),
                      reads=RP[0], writes=RP[2], tag="l2m2")
                s.dve(lambda e, r=r: e.tensor_tensor(r[2], r[1], r[2], ALU.subtract),
                      reads=RP[1] + RP[2], writes=RP[2], tag="l2var")
                s.act(lambda e, r=r: e.activation(r[3], r[2], AF.Sqrt, bias=lneps),
                      reads=RP[2] + CPQ, writes=RP[3], tag="l2sd")
                s.dve(lambda e, r=r: e.reciprocal(r[4], r[3]),
                      reads=RP[3], writes=RP[4], tag="l2rstd")
                s.dve(lambda e, r=r: e.scalar_tensor_tensor(r[5], r[0], neg1, r[4],
                                                            ALU.mult, ALU.mult),
                      reads=RP[0] + RP[4] + CPQ, writes=RP[5], tag="l2b")
                s.pe(lambda e, r=r: e.matmul(pA[:, :], onesf, r[4], start=True, stop=True),
                     reads=RP[4] + fpg(ONF_C, 128), writes=ps_res(pA), tag="l2bca")
                s.pe(lambda e, r=r: e.matmul(pB[:, :], onesf, r[5], start=True, stop=True),
                     reads=RP[5] + fpg(ONF_C, 128), writes=ps_res(pB), tag="l2bcb")
                for cc in range(NCH):
                    src = f32(XA_C + cc * 1024 + tp * 512, 512)
                    t0v = f32(T0_C + (cc % 2) * 512, 512)
                    s.dve(lambda e, src=src, t0v=t0v: e.tensor_tensor(t0v, src, pA[:, :], ALU.mult),
                          reads=fpg(XA_C + cc * 1024 + tp * 512, 512) + ps_res(pA),
                          writes=fpg(T0_C + (cc % 2) * 512, 512), tag="l2n1")
                    dst = bf(XN2_C + cc * 1024 + tp * 512, 512)
                    s.dve(lambda e, dst=dst, t0v=t0v: e.tensor_tensor(dst, t0v, pB[:, :], ALU.add),
                          reads=fpg(T0_C + (cc % 2) * 512, 512) + ps_res(pB),
                          writes=bpg(XN2_C + cc * 1024 + tp * 512, 512), tag="l2n2")
            lgt = f32(LGT_C, 1024, 0, 8)
            for tp in range(2):
                for cc in range(NCH):
                    s.pe(lambda e, cc=cc, tp=tp: e.matmul(
                            pC[0:8, :], bf(WRT_C + cc * 8, 8),
                            xn2[cc][:, tp * 512:(tp + 1) * 512],
                            start=(cc == 0), stop=(cc == 7)),
                         reads=bpg(WRT_C, 512) + xn2_pg[cc], writes=ps_res(pC),
                         tag="rt")
                dst = f32(LGT_C + tp * 512, 512, 0, 8)
                s.dve(lambda e, dst=dst: e.tensor_scalar(
                        dst, pC[0:8, :], f32(BR8_C, 1, 0, 8), None, ALU.add),
                      reads=ps_res(pC) + fpg(0, 512),
                      writes=fpg(LGT_C + tp * 512, 512), tag="rtev")
            tA = f32(TMA_C, 1024, 0, 8)
            tB = f32(TMB_C, 1024, 0, 8)
            LGP = fpg(LGT_C, 1024)
            TAP = fpg(TMA_C, 1024)
            TBP = fpg(TMB_C, 1024)
            s.dma("sync", lambda e: e.dma_start(tA[0:4, :], lgt[4:8, :]),
                  reads=LGP, writes=TAP, tag="mt1", chan="rt", serial=True)
            s.dve(lambda e: e.tensor_tensor(tB[0:4, :], lgt[0:4, :], tA[0:4, :], ALU.max),
                  reads=LGP + TAP, writes=TBP, tag="mt2")
            s.dma("sync", lambda e: e.dma_start(tA[0:2, :], tB[2:4, :]),
                  reads=TBP, writes=TAP, tag="mt3", chan="rt", serial=True)
            s.dve(lambda e: e.tensor_tensor(tA[0:2, :], tB[0:2, :], tA[0:2, :], ALU.max),
                  reads=TBP + TAP, writes=TAP, tag="mt4")
            s.dma("sync", lambda e: e.dma_start(tB[0:1, :], tA[1:2, :]),
                  reads=TAP, writes=TBP, tag="mt5", chan="rt", serial=True)
            s.dve(lambda e: e.tensor_tensor(tB[0:1, :], tA[0:1, :], tB[0:1, :], ALU.max),
                  reads=TAP + TBP, writes=TBP, tag="mt6")
            s.dma("sync", lambda e: e.dma_start(tA[0:1, :], tB[0:1, :]),
                  reads=TBP, writes=TAP, tag="mb1", chan="rt", serial=True)
            s.dma("sync", lambda e: e.dma_start(tA[1:2, :], tA[0:1, :]),
                  reads=TAP, writes=TAP, tag="mb2", chan="rt", serial=True)
            s.dma("sync", lambda e: e.dma_start(tA[2:4, :], tA[0:2, :]),
                  reads=TAP, writes=TAP, tag="mb3", chan="rt", serial=True)
            s.dma("sync", lambda e: e.dma_start(tA[4:8, :], tA[0:4, :]),
                  reads=TAP, writes=TAP, tag="mb4", chan="rt", serial=True)
            mk8 = bf(MK8_C, 1024, 0, 8)
            MK8P = bpg(MK8_C, 1024)
            s.dve(lambda e: e.tensor_tensor(mk8, lgt, tA, ALU.is_ge),
                  reads=LGP + TAP, writes=MK8P, tag="mk8")
            if mode == "dbg":
                for cc in range(NCH):
                    s.dma("sync", lambda e, cc=cc, qc=qc: e.dma_start(
                        dbg_xac[cc * 128:(cc + 1) * 128, qc:qc + QT], xacc[cc]),
                        reads=xacc_pg[cc], writes=[("dxa", q, cc)], tag="dxa", chan="st")
                    s.dma("sync", lambda e, cc=cc, qc=qc: e.dma_start(
                        dbg_xn2[cc * 128:(cc + 1) * 128, qc:qc + QT], xn2[cc]),
                        reads=xn2_pg[cc], writes=[("dxn", q, cc)], tag="dxn", chan="st")
                s.dma("sync", lambda e, qc=qc: e.dma_start(
                    dbg_lgt[:, qc:qc + QT], lgt), reads=LGP,
                    writes=[("dlg", q)], tag="dlg", chan="st")
                s.dma("sync", lambda e, qc=qc: e.dma_start(
                    dbg_mk8[:, qc:qc + QT], mk8), reads=MK8P,
                    writes=[("dmk", q)], tag="dmk", chan="st")
            wpar = [0]
            for ei in range(NEXP):
                mre = bf(MRE_C, 1024, 0, 1)
                s.dma("sync", lambda e, ei=ei, mre=mre: e.dma_start(mre, mk8[ei:ei + 1, :]),
                      reads=MK8P, writes=bpg(MRE_C, 1024), tag="mre", chan="rt", serial=True)
                for tp in range(2):
                    ub, ures = u_bank()
                    s.pe(lambda e, ub=ub, mre=mre, tp=tp: e.matmul(
                            ub, onesrow_b, mre[0:1, tp * 512:(tp + 1) * 512],
                            start=True, stop=True),
                         reads=bpg(MRE_C, 1024) + bpg(CON_C, 512), writes=ures,
                         tag="mbc")
                    s.act(lambda e, ub=ub, tp=tp: e.copy(bf(MKB_C + tp * 512, 512), ub),
                          reads=ures, writes=bpg(MKB_C + tp * 512, 512), tag="mbev")
                maskb = bf(MKB_C, 1024)
                MKBP = bpg(MKB_C, 1024)
                for si in range(4):
                    par = wpar[0] % 2
                    wpar[0] += 1
                    w1v = bf(W1S_C + par * 8192, 8192)
                    w2v = bf(W2S_C + par * 8192, 8192)
                    W1P = bpg(W1S_C + par * 8192, 8192)
                    W2P = bpg(W2S_C + par * 8192, 8192)
                    src1 = w1t[ei].rearrange("(cc p) f -> p cc f", p=128)[:, :, si * 1024:(si + 1) * 1024]
                    s.dma("pool", lambda e, w1v=w1v, src1=src1: e.dma_start(
                        w1v.rearrange("p (cc f) -> p cc f", f=1024), src1),
                        writes=W1P, tag="w1ld", chan="w")
                    src2 = w2t[ei][si * 1024:(si + 1) * 1024, :].rearrange(
                        "(fc p) d -> p fc d", p=128)
                    s.dma("pool", lambda e, w2v=w2v, src2=src2: e.dma_start(
                        w2v.rearrange("p (fc d) -> p fc d", d=1024), src2),
                        writes=W2P, tag="w2ld", chan="w")
                    b1v = f32(B1S_C + par * 512, 8)
                    B1P = fpg(B1S_C + par * 512, 512)
                    s.dma("sync", lambda e, b1v=b1v, ei=ei, si=si: e.dma_start(
                        b1v, b1c[ei][:, si * 8:(si + 1) * 8]),
                        writes=B1P, tag="b1ld", chan="b1", serial=True)
                    for tp in range(2):
                        tps = slice(tp * 512, (tp + 1) * 512)
                        for fc in range(8):
                            zb, zres = rot_bank("z")
                            for cc in range(NCH):
                                s.pe(lambda e, zb=zb, w1v=w1v, cc=cc, fc=fc, tps=tps:
                                     e.matmul(zb, w1v[:, cc * 1024 + fc * 128:
                                                      cc * 1024 + (fc + 1) * 128],
                                              xn2[cc][:, tps],
                                              start=(cc == 0), stop=(cc == 7)),
                                     reads=W1P + xn2_pg[cc], writes=zres, tag="z")
                            h0v = bf(H0_C + (fc % 2) * 512, 512)
                            s.act(lambda e, h0v=h0v, zb=zb, b1v=b1v, fc=fc:
                                  e.activation(h0v, zb, AF.Relu, bias=b1v[:, fc:fc + 1]),
                                  reads=zres + B1P,
                                  writes=bpg(H0_C + (fc % 2) * 512, 512), tag="relu")
                            hv = bf(H_C + fc * 512, 512)
                            s.dve(lambda e, hv=hv, h0v=h0v, tp=tp: e.tensor_tensor(
                                    hv, h0v, maskb[:, tp * 512:(tp + 1) * 512], ALU.mult),
                                  reads=bpg(H0_C + (fc % 2) * 512, 512) + MKBP,
                                  writes=bpg(H_C + fc * 512, 512), tag="hm")
                        for do in range(NCH):
                            yb, yres = rot_bank("y")
                            for fc in range(8):
                                s.pe(lambda e, yb=yb, w2v=w2v, fc=fc, do=do:
                                     e.matmul(yb, w2v[:, fc * 1024 + do * 128:
                                                      fc * 1024 + (do + 1) * 128],
                                              bf(H_C + fc * 512, 512),
                                              start=(fc == 0), stop=(fc == 7)),
                                     reads=W2P + bpg(H_C + fc * 512, 512),
                                     writes=yres, tag="y")
                            dst = f32(XA_C + do * 1024 + tp * 512, 512)
                            s.dve(lambda e, dst=dst, yb=yb: e.tensor_tensor(
                                    dst, yb, dst, ALU.add),
                                  reads=yres + fpg(XA_C + do * 1024 + tp * 512, 512),
                                  writes=fpg(XA_C + do * 1024 + tp * 512, 512),
                                  tag="yacc")
            for tp in range(2):
                for do in range(NCH):
                    ub, ures = u_bank()
                    s.pe(lambda e, ub=ub, do=do, tp=tp: e.matmul(
                            ub, bf(B2SB_C + do * 128, 128, 0, 8),
                            mk8[:, tp * 512:(tp + 1) * 512], start=True, stop=True),
                         reads=bpg(B2SB_C, 1024) + MK8P, writes=ures, tag="b2")
                    dst = f32(XA_C + do * 1024 + tp * 512, 512)
                    s.dve(lambda e, dst=dst, ub=ub: e.tensor_tensor(dst, ub, dst, ALU.add),
                          reads=ures + fpg(XA_C + do * 1024 + tp * 512, 512),
                          writes=fpg(XA_C + do * 1024 + tp * 512, 512), tag="b2acc")
            for do in range(NCH):
                s.dma("sync", lambda e, do=do, qc=qc: e.dma_start(
                    outT[do * 128:(do + 1) * 128, qc:qc + QT], xacc[do]),
                    reads=xacc_pg[do], writes=[("out", q, do)], tag="st", chan="st")

        fin_reads = [("out", q, do) for q in range(NQ) for do in range(NCH)]
        if mode == "dbg":
            fin_reads += [("dxa", q, cc) for q in range(NQ) for cc in range(NCH)]
            fin_reads += [("dxn", q, cc) for q in range(NQ) for cc in range(NCH)]
            fin_reads += [("dlg", q) for q in range(NQ)]
            fin_reads += [("dmk", q) for q in range(NQ)]
        s.op("sync", lambda e: e.nop(), reads=fin_reads, tag="fin")

        emitters, stats = s.emit(nc, es)
        blk = es.enter_context(nc.Block())
        blk.tensor(emitters["pe"])
        blk.scalar(emitters["act"])
        blk.vector(emitters["dve"])
        blk.gpsimd(emitters["pool"])
        blk.sync(emitters["sync"])
    return nc, stats


def make_host_inputs(L0_attn_batch, L0_attn_w_query, L0_attn_w_key,
                     L0_attn_w_value, L0_attn_out_w, L0_attn_out_b,
                     L0_ffn_Wr, L0_ffn_br, L0_ffn_W1, L0_ffn_b1,
                     L0_ffn_W2, L0_ffn_b2):
    bf16 = ml_dtypes.bfloat16
    x = np.asarray(L0_attn_batch, dtype=np.float32)
    shared = {
        "wq": np.ascontiguousarray(np.asarray(L0_attn_w_query, np.float32)),
        "wk": np.ascontiguousarray(np.asarray(L0_attn_w_key, np.float32)),
        "wv": np.ascontiguousarray(np.asarray(L0_attn_w_value, np.float32)),
        "wo": np.ascontiguousarray(np.asarray(L0_attn_out_w, np.float32)),
        "outb": np.ascontiguousarray(
            np.asarray(L0_attn_out_b, np.float32).reshape(8, 128).T),
        "wrt": np.ascontiguousarray(np.asarray(L0_ffn_Wr, np.float32).T),
        "br8": np.ascontiguousarray(np.asarray(L0_ffn_br, np.float32).reshape(8, 1)),
        "w1t": np.ascontiguousarray(
            np.asarray(L0_ffn_W1, np.float32).transpose(0, 2, 1)),
        "b1c": np.ascontiguousarray(
            np.asarray(L0_ffn_b1, np.float32).reshape(8, 32, 128).transpose(0, 2, 1)),
        "w2t": np.ascontiguousarray(
            np.asarray(L0_ffn_W2, np.float32).transpose(0, 2, 1)),
        "b2e": np.ascontiguousarray(np.asarray(L0_ffn_b2, np.float32)),
    }
    k_idx = np.arange(128)[:, None]
    q_idx = np.arange(512)[None, :]
    tri = np.zeros((128, 2048), dtype=bf16)
    for j in range(4):
        tri[:, j * 512:(j + 1) * 512] = ((128 * j + k_idx) <= q_idx).astype(bf16)
    shared["tri"] = tri
    shared["onesb"] = np.ones((128, 2), dtype=bf16)
    shared["onesr"] = np.ones((1, 128), dtype=bf16)
    in_maps = []
    for i in range(8):
        m = dict(shared)
        m["xT"] = np.ascontiguousarray(x[i].T)
        in_maps.append(m)
    return in_maps


_CACHE = {}


def kernel(**inputs):
    in_maps = make_host_inputs(**inputs)
    if "nc" not in _CACHE:
        _CACHE["nc"] = build_program("full")[0]
    nc = _CACHE["nc"]
    res = run_bass_kernel_spmd(nc, in_maps, core_ids=list(range(8)))
    out = np.stack([np.asarray(r["outT"], dtype=np.float32).T
                    for r in res.results])
    return np.ascontiguousarray(out)


# version 06
# speedup vs baseline: 1.1589x; 1.1589x over previous
import numpy as np
import ml_dtypes
import concourse.bass as bass
import concourse.mybir as mybir
from concourse.bass_utils import run_bass_kernel_spmd
from contextlib import ExitStack

F32 = mybir.dt.float32
BF16 = mybir.dt.bfloat16
AF = mybir.ActivationFunctionType
ALU = mybir.AluOpType

STREAMS = ("pe", "act", "dve", "pool", "sync")


class _Op:
    __slots__ = ("idx", "stream", "ctr", "fn", "reads", "writes", "deps",
                 "waits", "inc_val", "tag")

    def __init__(self, idx, stream, ctr, fn, reads, writes, tag):
        self.idx = idx
        self.stream = stream
        self.ctr = ctr
        self.fn = fn
        self.reads = reads
        self.writes = writes
        self.deps = []
        self.waits = []
        self.inc_val = None
        self.tag = tag


class Sched:
    def __init__(self):
        self.ops = []
        self.last_write = {}
        self.readers = {}
        self.last_on_ctr = {}

    def op(self, stream, fn, reads=(), writes=(), dma=False, tag="",
           chan="d", serial=False):
        ctr = (stream + ":" + chan) if dma else stream
        idx = len(self.ops)
        o = _Op(idx, stream, ctr, fn, tuple(reads), tuple(writes), tag)
        deps = set()
        for r in o.reads:
            j = self.last_write.get(r)
            if j is not None:
                deps.add(j)
        for w in o.writes:
            j = self.last_write.get(w)
            if j is not None:
                deps.add(j)
            for rd in self.readers.get(w, ()):
                deps.add(rd)
        if dma and serial:
            p = self.last_on_ctr.get(ctr)
            if p is not None:
                deps.add(p)
        deps.discard(idx)
        o.deps = sorted(deps)
        self.ops.append(o)
        for r in o.reads:
            self.readers.setdefault(r, []).append(idx)
        for w in o.writes:
            self.last_write[w] = idx
            self.readers[w] = []
        self.last_on_ctr[ctr] = idx
        return idx

    def pe(self, fn, reads=(), writes=(), tag=""):
        return self.op("pe", fn, reads, writes, tag=tag)

    def act(self, fn, reads=(), writes=(), tag=""):
        return self.op("act", fn, reads, writes, tag=tag)

    def dve(self, fn, reads=(), writes=(), tag=""):
        return self.op("dve", fn, reads, writes, tag=tag)

    def pool(self, fn, reads=(), writes=(), tag=""):
        return self.op("pool", fn, reads, writes, tag=tag)

    def dma(self, queue, fn, reads=(), writes=(), tag="", chan="d",
            serial=False):
        return self.op(queue, fn, reads, writes, dma=True, tag=tag,
                       chan=chan, serial=serial)

    def _resolve(self):
        n = len(self.ops)
        ctrs = sorted({o.ctr for o in self.ops})
        cidx = {c: i for i, c in enumerate(ctrs)}
        nc_ = len(ctrs)
        seq = [0] * n
        cnt = {c: 0 for c in ctrs}
        for o in self.ops:
            cnt[o.ctr] += 1
            seq[o.idx] = cnt[o.ctr]
        ZERO = tuple([0] * nc_)

        def vmax(a, b):
            return tuple(x if x >= y else y for x, y in zip(a, b))

        start_know = [ZERO] * n
        complete = [ZERO] * n
        prev_on_stream = {}
        for o in self.ops:
            i = o.idx
            know = ZERO
            p = prev_on_stream.get(o.stream)
            if p is not None:
                po = self.ops[p]
                know = start_know[p] if (":" in po.ctr) else complete[p]
            for d in o.deps:
                if know[cidx[self.ops[d].ctr]] >= seq[d]:
                    continue
                do = self.ops[d]
                if do.stream == o.stream and not (":" in do.ctr) \
                        and not (":" in o.ctr):
                    know = vmax(know, complete[d])
                    continue
                o.waits.append(d)
                know = vmax(know, complete[d])
            start_know[i] = know
            me = list(know)
            j = cidx[o.ctr]
            if seq[o.idx] > me[j]:
                me[j] = seq[o.idx]
            complete[i] = tuple(me)
            prev_on_stream[o.stream] = i
        marked = set()
        for o in self.ops:
            if (":" in o.ctr):
                marked.add(o.idx)
            for d in o.waits:
                marked.add(d)
        val = {}
        mcnt = {c: 0 for c in ctrs}
        for o in self.ops:
            if o.idx in marked:
                mcnt[o.ctr] += 1
                step = 16 if (":" in o.ctr) else 1
                o.inc_val = mcnt[o.ctr] * step
                val[o.idx] = o.inc_val
        for o in self.ops:
            best = {}
            for d in o.waits:
                c, v = self.ops[d].ctr, val[d]
                if best.get(c, -1) < v:
                    best[c] = v
            o.waits = sorted(best.items())
        return ctrs

    def emit(self, nc, es):
        ctrs = self._resolve()
        sems = {c: es.enter_context(nc.semaphore("sem_" + c)) for c in ctrs}
        by_stream = {s: [] for s in STREAMS}
        for o in self.ops:
            by_stream[o.stream].append(o)
        stats = {
            "n_ops": len(self.ops),
            "n_waits": sum(len(o.waits) for o in self.ops),
            "n_incs": sum(1 for o in self.ops if o.inc_val is not None),
        }

        def make_emitter(stream):
            def emitter(eng):
                for o in by_stream[stream]:
                    for c, v in o.waits:
                        eng.wait_ge(sems[c], v)
                    inst = o.fn(eng)
                    if o.inc_val is not None:
                        if inst is None:
                            raise RuntimeError(
                                f"op {o.idx} ({o.tag}) needs inc, fn gave None")
                        inst.then_inc(sems[o.ctr],
                                      16 if (":" in o.ctr) else 1)
            return emitter

        return {st: make_emitter(st) for st in STREAMS}, stats


S = 4096
D = 1024
NCH = 8
NQ = 4
QT = S // NQ
DFF = 4096
NEXP = 8

TRI_C = 0
B2SB_C = 2048
WRT_C = 3072
CON_C = 3584
BASE_A = 4096
XN_C = BASE_A
QTB_C = XN_C + 8 * 4096
KTB_C = QTB_C + 4096
O2P_C = KTB_C + 4096
V2A_C = O2P_C + 4096
PB_C = V2A_C + 4224
PBB_C = 61568
WQS_C = PB_C + 2048
WKS_C = WQS_C + 2048
WVS_C = WKS_C + 2048
TOP_A = WVS_C + 2048
XN2_C = BASE_A
W1S_C = XN2_C + 8 * 1024
W2S_C = W1S_C + 16384
H_C = W2S_C + 16384
H0_C = H_C + 4096
O2H_C = H0_C + 1024
WOS_C = O2H_C + 4096
MKB_C = WOS_C + 8192
MK8_C = MKB_C + 1024
MRE_C = MK8_C + 1024
TOP_B = MRE_C + 1024
XB_C = TOP_B
XQ_C = XB_C + 1024
NB = XQ_C + 1024

CP_C = 0
OUTB_C = 16
BR8_C = 24
ONF_C = 128
ROW_C = 512
XF_C = ROW_C + 6 * 512
T0_C = XF_C + 4 * 512
XO_C = T0_C + 1024
LGT_C = XO_C + 1024
TMA_C = LGT_C + 1024
TMB_C = XO_C
XA_C = TMA_C + 1024
B1S_C = XA_C + 8 * 1024
NF = B1S_C + 1024


def _pages(arena, c0, c1):
    return [(arena, p) for p in range(c0 // 512, (c1 + 511) // 512)]


def build_program(mode="full"):
    nc = bass.Bass("TRN2", target_bir_lowering=False, debug=False)
    xT = nc.dram_tensor("xT", [D, S], F32, kind="ExternalInput").ap()
    wq = nc.dram_tensor("wq", [D, D], F32, kind="ExternalInput").ap()
    wk = nc.dram_tensor("wk", [D, D], F32, kind="ExternalInput").ap()
    wv = nc.dram_tensor("wv", [D, D], F32, kind="ExternalInput").ap()
    wo = nc.dram_tensor("wo", [D, D], F32, kind="ExternalInput").ap()
    outb = nc.dram_tensor("outb", [128, 8], F32, kind="ExternalInput").ap()
    wrt = nc.dram_tensor("wrt", [D, NEXP], F32, kind="ExternalInput").ap()
    br8 = nc.dram_tensor("br8", [NEXP, 1], F32, kind="ExternalInput").ap()
    w1t = nc.dram_tensor("w1t", [NEXP, D, DFF], F32, kind="ExternalInput").ap()
    b1c = nc.dram_tensor("b1c", [NEXP, 128, 32], F32, kind="ExternalInput").ap()
    w2t = nc.dram_tensor("w2t", [NEXP, DFF, D], F32, kind="ExternalInput").ap()
    b2e = nc.dram_tensor("b2e", [NEXP, D], F32, kind="ExternalInput").ap()
    tri = nc.dram_tensor("tri", [128, 2048], BF16, kind="ExternalInput").ap()
    onesb = nc.dram_tensor("onesb", [128, 2], BF16, kind="ExternalInput").ap()
    o2s = nc.dram_tensor("o2s", [D, S], BF16, kind="Internal").ap()
    outT = nc.dram_tensor("outT", [D, S], F32, kind="ExternalOutput").ap()
    if mode == "dbg":
        dbg_xac = nc.dram_tensor("dbg_xac", [D, S], F32, kind="ExternalOutput").ap()
        dbg_xn2 = nc.dram_tensor("dbg_xn2", [D, S], BF16, kind="ExternalOutput").ap()
        dbg_lgt = nc.dram_tensor("dbg_lgt", [8, S], F32, kind="ExternalOutput").ap()
        dbg_mk8 = nc.dram_tensor("dbg_mk8", [8, S], BF16, kind="ExternalOutput").ap()

    s = Sched()
    with ExitStack() as es:
        AB = es.enter_context(nc.sbuf_tensor("AB", [128, NB], BF16))
        AF32 = es.enter_context(nc.sbuf_tensor("AF32", [128, NF], F32))
        scoreP = es.enter_context(nc.psum_tensor("scoreP", [128, 2048], F32))
        pA = es.enter_context(nc.psum_tensor("pA", [128, 512], F32))
        pB = es.enter_context(nc.psum_tensor("pB", [128, 512], F32))
        pC = es.enter_context(nc.psum_tensor("pC", [128, 512], F32))
        pD = es.enter_context(nc.psum_tensor("pD", [128, 512], F32))
        def ps_res(t, off=0):
            if t is scoreP:
                return [("PS", off // 512)]
            return [("PS", {id(pA): 4, id(pB): 5, id(pC): 6, id(pD): 7}[id(t)])]

        def bf(c0, w, r0=0, rn=128):
            return AB[r0:rn, c0:c0 + w]

        def f32(c0, w, r0=0, rn=128):
            return AF32[r0:rn, c0:c0 + w]

        def bpg(c0, w):
            return _pages("B", c0, c0 + w)

        def fpg(c0, w):
            return _pages("F", c0, c0 + w)

        s.dma("sync", lambda e: e.dma_start(bf(TRI_C, 2048), tri),
              writes=bpg(TRI_C, 2048), tag="tri", chan="ini", serial=True)
        s.dma("sync", lambda e: e.dma_start(bf(CON_C, 2, 0, 128), onesb),
              writes=bpg(CON_C, 512), tag="ones", chan="ini", serial=True)
        s.dma("sync", lambda e: e.dma_start(f32(OUTB_C, 8), outb),
              writes=fpg(0, 512), tag="outb", chan="ini", serial=True)
        s.dma("sync", lambda e: e.dma_start(f32(BR8_C, 1, 0, 8), br8),
              writes=fpg(0, 512), tag="br8", chan="ini", serial=True)
        s.dma("pool", lambda e: e.dma_start(bf(B2SB_C, 1024, 0, 8), b2e),
              writes=bpg(B2SB_C, 1024), tag="b2", chan="ini", serial=True)
        s.dma("pool", lambda e: e.dma_start(
            bf(WRT_C, 64).rearrange("p (cc e) -> p cc e", e=8),
            wrt.rearrange("(cc p) e -> p cc e", p=128)),
              writes=bpg(WRT_C, 512), tag="wrt", chan="ini", serial=True)
        s.dve(lambda e: e.memset(f32(CP_C, 1), 0.125), writes=fpg(0, 512))
        s.dve(lambda e: e.memset(f32(CP_C + 1, 1), -12.0), writes=fpg(0, 512))
        s.dve(lambda e: e.memset(f32(CP_C + 2, 1), 1e-5), writes=fpg(0, 512))
        s.dve(lambda e: e.memset(f32(CP_C + 3, 1), 1.0 / 1024.0), writes=fpg(0, 512))
        s.dve(lambda e: e.memset(f32(CP_C + 4, 1), -1.0), writes=fpg(0, 512))
        s.dve(lambda e: e.memset(f32(ONF_C, 128, 0, 1), 1.0), writes=fpg(ONF_C, 128))
        CPQ = fpg(0, 512)
        qscale = f32(CP_C, 1)
        ebias = f32(CP_C + 1, 1)
        lneps = f32(CP_C + 2, 1, 0, 1)
        inv1024 = f32(CP_C + 3, 1, 0, 1)
        neg1 = f32(CP_C + 4, 1, 0, 1)
        onesf = f32(ONF_C, 128, 0, 1)
        onescol_b = bf(CON_C, 1, 0, 128)
        onesr = nc.dram_tensor("onesr", [1, 128], BF16, kind="ExternalInput").ap()
        s.dma("sync", lambda e: e.dma_start(bf(CON_C + 2, 128, 0, 1), onesr),
              writes=bpg(CON_C, 512), tag="onesr", chan="ini", serial=True)
        onesrow_b = bf(CON_C + 2, 128, 0, 1)

        for tt in range(8):
            ts = slice(tt * 512, (tt + 1) * 512)
            tpar = tt % 2
            S1 = scoreP[0:1, tpar * 1024:tpar * 1024 + 512]
            S2 = scoreP[0:1, tpar * 1024 + 512:tpar * 1024 + 1024]
            S1R = [("PS", 2 * tpar)]
            S2R = [("PS", 2 * tpar + 1)]
            bA = pA if tpar == 0 else pC
            bB = pB if tpar == 0 else pD
            for cc in range(NCH):
                sl = cc % 2
                xfc = XF_C + sl * 512
                xfv = f32(xfc, 512)
                s.dma("sync", lambda e, cc=cc, ts=ts, xfv=xfv: e.dma_start(
                    xfv, xT[cc * 128:(cc + 1) * 128, ts]),
                    writes=fpg(xfc, 512), tag="lnx", chan="xs")
                xbv = bf(XB_C + sl * 512, 512)
                s.pool(lambda e, xfv=xfv, xbv=xbv: e.tensor_copy(xbv, xfv),
                       reads=fpg(xfc, 512),
                       writes=bpg(XB_C + sl * 512, 512), tag="lncast")
                xqv = bf(XQ_C + sl * 512, 512)
                s.act(lambda e, xbv=xbv, xqv=xqv: e.activation(xqv, xbv, AF.Square),
                      reads=bpg(XB_C + sl * 512, 512),
                      writes=bpg(XQ_C + sl * 512, 512), tag="lnsq")
                s.pe(lambda e, xbv=xbv, cc=cc, S1=S1: e.matmul(
                        S1, onescol_b, xbv, start=(cc == 0), stop=(cc == 7)),
                     reads=bpg(XB_C + sl * 512, 512) + bpg(CON_C, 512),
                     writes=S1R, tag="lns1")
                s.pe(lambda e, xqv=xqv, cc=cc, S2=S2: e.matmul(
                        S2, onescol_b, xqv, start=(cc == 0), stop=(cc == 7)),
                     reads=bpg(XQ_C + sl * 512, 512) + bpg(CON_C, 512),
                     writes=S2R, tag="lns2")
            r = [f32(ROW_C + i * 512, 512, 0, 1) for i in range(6)]
            RP = [fpg(ROW_C + i * 512, 512) for i in range(6)]
            s.dve(lambda e, r=r, S1=S1: e.tensor_scalar(r[0], S1, inv1024, None, ALU.mult),
                  reads=S1R + CPQ, writes=RP[0], tag="lnm")
            s.dve(lambda e, r=r, S2=S2: e.tensor_scalar(r[1], S2, inv1024, None, ALU.mult),
                  reads=S2R + CPQ, writes=RP[1], tag="lnq")
            s.dve(lambda e, r=r: e.tensor_tensor(r[2], r[0], r[0], ALU.mult),
                  reads=RP[0], writes=RP[2], tag="lnm2")
            s.dve(lambda e, r=r: e.tensor_tensor(r[2], r[1], r[2], ALU.subtract),
                  reads=RP[1] + RP[2], writes=RP[2], tag="lnvar")
            s.act(lambda e, r=r: e.activation(r[3], r[2], AF.Sqrt, bias=lneps),
                  reads=RP[2] + CPQ, writes=RP[3], tag="lnsd")
            s.dve(lambda e, r=r: e.reciprocal(r[4], r[3]),
                  reads=RP[3], writes=RP[4], tag="lnrstd")
            s.dve(lambda e, r=r: e.scalar_tensor_tensor(r[5], r[0], neg1, r[4],
                                                        ALU.mult, ALU.mult),
                  reads=RP[0] + RP[4] + CPQ, writes=RP[5], tag="lnb")
            s.pe(lambda e, r=r, bA=bA: e.matmul(bA[:, :], onesf, r[4], start=True, stop=True),
                 reads=RP[4] + fpg(ONF_C, 128), writes=ps_res(bA), tag="lnbca")
            s.pe(lambda e, r=r, bB=bB: e.matmul(bB[:, :], onesf, r[5], start=True, stop=True),
                 reads=RP[5] + fpg(ONF_C, 128), writes=ps_res(bB), tag="lnbcb")
            for cc in range(NCH):
                xfc = XF_C + (2 + cc % 2) * 512
                xfv = f32(xfc, 512)
                s.dma("sync", lambda e, cc=cc, ts=ts, xfv=xfv: e.dma_start(
                    xfv, xT[cc * 128:(cc + 1) * 128, ts]),
                    writes=fpg(xfc, 512), tag="lnx2", chan="xs")
                t0v = f32(T0_C + (cc % 2) * 512, 512)
                s.dve(lambda e, xfv=xfv, t0v=t0v, bA=bA: e.tensor_tensor(t0v, xfv, bA[:, :], ALU.mult),
                      reads=fpg(xfc, 512) + ps_res(bA),
                      writes=fpg(T0_C + (cc % 2) * 512, 512), tag="lnn1")
                xnv = bf(XN_C + cc * 4096 + tt * 512, 512)
                s.dve(lambda e, xnv=xnv, t0v=t0v, bB=bB: e.tensor_tensor(xnv, t0v, bB[:, :], ALU.add),
                      reads=fpg(T0_C + (cc % 2) * 512, 512) + ps_res(bB),
                      writes=bpg(XN_C + cc * 4096 + tt * 512, 512), tag="lnn2")

        xn = [bf(XN_C + cc * 4096, 4096) for cc in range(NCH)]
        xn_pg = [bpg(XN_C + cc * 4096, 4096) for cc in range(NCH)]
        v2a_full = bf(V2A_C, 4160)
        s.dve(lambda e: e.memset(
            v2a_full.rearrange("p (tc w) -> p tc w", w=130)[:, :, 64:65], 1.0),
            writes=bpg(V2A_C, 4160), tag="v2ones1")
        s.dve(lambda e: e.memset(
            v2a_full.rearrange("p (tc w) -> p tc w", w=130)[:, :, 129:130], 1.0),
            writes=bpg(V2A_C, 4160), tag="v2ones2")

        evrot = [0]

        def evac_bank():
            b = evrot[0] % 4
            evrot[0] += 1
            return scoreP[:, b * 512:(b + 1) * 512], [("PS", b)]

        for hp in range(8):
            par = hp % 2
            wq_s = bf(WQS_C + par * 1024, 1024)
            wk_s = bf(WKS_C + par * 1024, 1024)
            wv_s = bf(WVS_C + par * 1024, 1024)
            wq_pg = bpg(WQS_C + par * 1024, 1024)
            wk_pg = bpg(WKS_C + par * 1024, 1024)
            wv_pg = bpg(WVS_C + par * 1024, 1024)
            for (w_s, w_pg, wsrc, nm) in ((wq_s, wq_pg, wq, "wq"),
                                          (wk_s, wk_pg, wk, "wk"),
                                          (wv_s, wv_pg, wv, "wv")):
                src = wsrc.rearrange("(cc p) d -> p cc d", p=128)[:, :, hp * 128:(hp + 1) * 128]
                s.dma("pool", lambda e, w_s=w_s, src=src: e.dma_start(
                    w_s.rearrange("p (cc d) -> p cc d", d=128), src),
                    writes=w_pg, tag=nm, chan="wqv")
            qtb = bf(QTB_C, 4096)
            ktb = bf(KTB_C, 4096)
            for (dst_c, w_s, w_pg, sc, nm) in ((QTB_C, wq_s, wq_pg, qscale, "qt"),
                                               (KTB_C, wk_s, wk_pg, None, "kt")):
                for tt in range(8):
                    ts = slice(tt * 512, (tt + 1) * 512)
                    bank, bres = evac_bank()
                    for cc in range(NCH):
                        s.pe(lambda e, bank=bank, w_s=w_s, cc=cc, ts=ts: e.matmul(
                                bank, w_s[:, cc * 128:(cc + 1) * 128], xn[cc][:, ts],
                                start=(cc == 0), stop=(cc == 7)),
                             reads=w_pg + xn_pg[cc], writes=bres, tag=nm)
                    dst = bf(dst_c + tt * 512, 512)
                    if sc is None:
                        s.act(lambda e, dst=dst, bank=bank: e.copy(dst, bank),
                              reads=bres, writes=bpg(dst_c + tt * 512, 512), tag=nm + "ev")
                    else:
                        s.act(lambda e, dst=dst, bank=bank, sc=sc: e.activation(
                                dst, bank, AF.Copy, scale=sc),
                              reads=bres + CPQ, writes=bpg(dst_c + tt * 512, 512),
                              tag=nm + "ev")
            for tc in range(32):
                bank = (pC, ps_res(pC)) if tc % 2 == 0 else (pD, ps_res(pD))
                bk, bres = bank
                for cc in range(NCH):
                    s.pe(lambda e, bk=bk, cc=cc, tc=tc, wv_s=wv_s: e.matmul(
                            bk[:, 0:128], xn[cc][:, tc * 128:(tc + 1) * 128],
                            wv_s[:, cc * 128:(cc + 1) * 128],
                            start=(cc == 0), stop=(cc == 7)),
                         reads=wv_pg + xn_pg[cc], writes=bres, tag="v")
                va = bf(V2A_C + tc * 130, 64)
                vb = bf(V2A_C + tc * 130 + 65, 64)
                s.dve(lambda e, va=va, bk=bk: e.tensor_copy(va, bk[:, 0:64]),
                      reads=bres, writes=bpg(V2A_C + tc * 130, 64), tag="vev")
                s.dve(lambda e, vb=vb, bk=bk: e.tensor_copy(vb, bk[:, 64:128]),
                      reads=bres, writes=bpg(V2A_C + tc * 130 + 65, 64), tag="vev")
            PBC = [[PB_C, PB_C + 1024], [PBB_C, PBB_C + 1024]]

            def emit_tail(qt_i):
                for h in range(2):
                    hb = 64 * h
                    obank = pA if h == 0 else pB
                    ob_res = ps_res(obank)
                    r0 = f32(ROW_C + h * 512, 512, 0, 1)
                    rb = pC if h == 0 else pD
                    s.dve(lambda e, obank=obank, r0=r0: e.reciprocal(
                            r0, obank[64:65, :]),
                          reads=ob_res, writes=fpg(ROW_C + h * 512, 512),
                          tag="recip")
                    s.pe(lambda e, r0=r0, rb=rb: e.matmul(
                            rb[0:64, :], onesf[0:1, 0:64], r0,
                            start=True, stop=True),
                         reads=fpg(ROW_C + h * 512, 512) + fpg(ONF_C, 128),
                         writes=ps_res(rb), tag="rbc")
                    dst = bf(O2P_C + qt_i * 512, 512, hb, hb + 64)
                    s.act(lambda e, dst=dst, obank=obank: e.copy(
                            dst, obank[0:64, :]),
                          reads=ob_res, writes=bpg(O2P_C + qt_i * 512, 512),
                          tag="oev")
                    s.dve(lambda e, dst=dst, rb=rb: e.tensor_tensor(
                            dst, dst, rb[0:64, :], ALU.mult),
                          reads=bpg(O2P_C + qt_i * 512, 512) + ps_res(rb),
                          writes=bpg(O2P_C + qt_i * 512, 512), tag="onorm")

            pending_tail = None
            for qt_i in range(8):
                qts = slice(qt_i * 512, (qt_i + 1) * 512)
                ng = 2 * (qt_i + 1)
                prev = None
                for g in range(ng):
                    for h in range(2):
                        hb = 64 * h
                        for j in range(2):
                            kc = 2 * g + j
                            bi = 2 * h + j
                            sb = scoreP[:, bi * 512:(bi + 1) * 512]
                            s.pe(lambda e, sb=sb, kc=kc, hb=hb, qts=qts,
                                 ktb=ktb, qtb=qtb: e.matmul(
                                    sb, ktb[hb:hb + 64, kc * 128:(kc + 1) * 128],
                                    qtb[hb:hb + 64, qts], start=True, stop=True),
                                 reads=bpg(KTB_C, 4096) +
                                       bpg(QTB_C + qt_i * 512, 512),
                                 writes=[("PS", bi)], tag="sc")
                    for h in range(2):
                        pbc = PBC[h][g % 2]
                        pbv = bf(pbc, 1024)
                        s.act(lambda e, pbv=pbv, h=h: e.activation(
                                pbv, scoreP[:, 2 * h * 512:(2 * h + 2) * 512],
                                AF.Exp, bias=ebias),
                              reads=[("PS", 2 * h), ("PS", 2 * h + 1)] + CPQ,
                              writes=bpg(pbc, 1024), tag="exp")
                        if g >= ng - 2:
                            dj = (g - (ng - 2))
                            trv = bf(TRI_C + dj * 1024, 1024)
                            s.dve(lambda e, pbv=pbv, trv=trv: e.tensor_tensor(
                                    pbv, pbv, trv, ALU.mult),
                                  reads=bpg(pbc, 1024) +
                                        bpg(TRI_C + dj * 1024, 1024),
                                  writes=bpg(pbc, 1024), tag="trim")
                    if prev is not None:
                        if prev == 0 and pending_tail is not None:
                            emit_tail(pending_tail)
                            pending_tail = None
                        for h in range(2):
                            obank = pA if h == 0 else pB
                            ob_res = ps_res(obank)
                            pbc = PBC[h][prev % 2]
                            ppb = bf(pbc, 1024)
                            for j in range(2):
                                kc = 2 * prev + j
                                s.pe(lambda e, kc=kc, h=h, ppb=ppb, j=j,
                                     obank=obank: e.matmul(
                                        obank[0:65, :],
                                        v2a_full[:, kc * 130 + 65 * h:
                                                 kc * 130 + 65 * h + 65],
                                        ppb[:, j * 512:(j + 1) * 512],
                                        start=(kc == 0), stop=False),
                                     reads=bpg(V2A_C + kc * 130, 130) +
                                           bpg(pbc, 1024),
                                     writes=ob_res, tag="pv")
                    prev = g
                if pending_tail is not None:
                    emit_tail(pending_tail)
                    pending_tail = None
                for h in range(2):
                    obank = pA if h == 0 else pB
                    ob_res = ps_res(obank)
                    pbc = PBC[h][prev % 2]
                    ppb = bf(pbc, 1024)
                    for j in range(2):
                        kc = 2 * prev + j
                        s.pe(lambda e, kc=kc, h=h, ppb=ppb, j=j,
                             last=(j == 1), obank=obank: e.matmul(
                                obank[0:65, :],
                                v2a_full[:, kc * 130 + 65 * h:
                                         kc * 130 + 65 * h + 65],
                                ppb[:, j * 512:(j + 1) * 512],
                                start=(kc == 0), stop=last),
                             reads=bpg(V2A_C + kc * 130, 130) + bpg(pbc, 1024),
                             writes=ob_res, tag="pv")
                pending_tail = qt_i
            emit_tail(pending_tail)
            pending_tail = None
            s.dma("sync", lambda e, hp=hp: e.dma_start(
                o2s[hp * 128:(hp + 1) * 128, :], bf(O2P_C, 4096)),
                reads=bpg(O2P_C, 4096), writes=[("o2s", hp)], tag="o2st", chan="o2w")

        for hp in range(8):
            s.dma("pool", lambda e, hp=hp: e.dma_start(
                bf(WOS_C + hp * 1024, 1024), wo[hp * 128:(hp + 1) * 128, :]),
                writes=bpg(WOS_C + hp * 1024, 1024), tag="wo", chan="wo")

        brot = [0]

        def rot_bank(group):
            if group == "z":
                b = brot[0] % 2
                brot[0] += 1
                return scoreP[:, b * 512:(b + 1) * 512], [("PS", b)]
            if group == "y":
                b = brot[0] % 2
                brot[0] += 1
                return scoreP[:, (2 + b) * 512:(3 + b) * 512], [("PS", 2 + b)]
            raise ValueError

        urot = [0]

        def u_bank():
            b = urot[0] % 2
            urot[0] += 1
            t = pC if b == 0 else pD
            return t[:, :], ps_res(t)

        xacc = [f32(XA_C + c * 1024, 1024) for c in range(NCH)]
        xacc_pg = [fpg(XA_C + c * 1024, 1024) for c in range(NCH)]
        xn2 = [bf(XN2_C + c * 1024, 1024) for c in range(NCH)]
        xn2_pg = [bpg(XN2_C + c * 1024, 1024) for c in range(NCH)]

        for q in range(NQ):
            qc = q * QT
            for tp in range(2):
                c0 = qc + tp * 512
                o2h = bf(O2H_C, 4096)
                s.dma("sync", lambda e, c0=c0, o2h=o2h: e.dma_start(
                    o2h.rearrange("p (hp t) -> p hp t", t=512),
                    o2s.rearrange("(hp p) t -> p hp t", p=128)[:, :, c0:c0 + 512]),
                    reads=[("o2s", hp) for hp in range(8)],
                    writes=bpg(O2H_C, 4096), tag="o2h", chan="o2h")
                for do in range(NCH):
                    bank, bres = evac_bank()
                    for hp in range(8):
                        s.pe(lambda e, bank=bank, hp=hp, do=do, o2h=o2h: e.matmul(
                                bank, bf(WOS_C + hp * 1024 + do * 128, 128),
                                o2h[:, hp * 512:(hp + 1) * 512],
                                start=(hp == 0), stop=(hp == 7)),
                             reads=bpg(WOS_C + hp * 1024, 1024) + bpg(O2H_C, 4096),
                             writes=bres, tag="wop")
                    xov = f32(XO_C + (do % 2) * 512, 512)
                    s.dma("sync", lambda e, xov=xov, do=do, c0=c0: e.dma_start(
                        xov, xT[do * 128:(do + 1) * 128, c0:c0 + 512]),
                        writes=fpg(XO_C + (do % 2) * 512, 512), tag="xo", chan="xs")
                    dst = f32(XA_C + do * 1024 + tp * 512, 512)
                    s.dve(lambda e, dst=dst, bank=bank, do=do, xov=xov:
                          e.scalar_tensor_tensor(dst, bank, f32(OUTB_C + do, 1),
                                                 xov, ALU.add, ALU.add),
                          reads=bres + fpg(XO_C + (do % 2) * 512, 512) + fpg(0, 512),
                          writes=fpg(XA_C + do * 1024 + tp * 512, 512), tag="b1acc")
            if mode == "attn":
                for do in range(NCH):
                    s.dma("sync", lambda e, do=do, qc=qc: e.dma_start(
                        outT[do * 128:(do + 1) * 128, qc:qc + QT], xacc[do]),
                        reads=xacc_pg[do], writes=[("out", q, do)], tag="st", chan="st")
                continue
            for tp in range(2):
                S1 = scoreP[0:1, tp * 1024:tp * 1024 + 512]
                S2 = scoreP[0:1, tp * 1024 + 512:tp * 1024 + 1024]
                S1R = [("PS", 2 * tp)]
                S2R = [("PS", 2 * tp + 1)]
                bA = pA if tp == 0 else pC
                bB = pB if tp == 0 else pD
                for cc in range(NCH):
                    src = f32(XA_C + cc * 1024 + tp * 512, 512)
                    xbv = bf(XB_C + (cc % 2) * 512, 512)
                    s.pool(lambda e, src=src, xbv=xbv: e.tensor_copy(xbv, src),
                           reads=fpg(XA_C + cc * 1024 + tp * 512, 512),
                           writes=bpg(XB_C + (cc % 2) * 512, 512), tag="l2cast")
                    xqv = bf(XQ_C + (cc % 2) * 512, 512)
                    s.act(lambda e, xbv=xbv, xqv=xqv: e.activation(xqv, xbv, AF.Square),
                          reads=bpg(XB_C + (cc % 2) * 512, 512),
                          writes=bpg(XQ_C + (cc % 2) * 512, 512), tag="l2sq")
                    s.pe(lambda e, xbv=xbv, cc=cc, S1=S1: e.matmul(
                            S1, onescol_b, xbv, start=(cc == 0), stop=(cc == 7)),
                         reads=bpg(XB_C + (cc % 2) * 512, 512) + bpg(CON_C, 512),
                         writes=S1R, tag="l2s1")
                    s.pe(lambda e, xqv=xqv, cc=cc, S2=S2: e.matmul(
                            S2, onescol_b, xqv, start=(cc == 0), stop=(cc == 7)),
                         reads=bpg(XQ_C + (cc % 2) * 512, 512) + bpg(CON_C, 512),
                         writes=S2R, tag="l2s2")
                r = [f32(ROW_C + i * 512, 512, 0, 1) for i in range(6)]
                RP = [fpg(ROW_C + i * 512, 512) for i in range(6)]
                s.dve(lambda e, r=r, S1=S1: e.tensor_scalar(r[0], S1, inv1024, None, ALU.mult),
                      reads=S1R + CPQ, writes=RP[0], tag="l2m")
                s.dve(lambda e, r=r, S2=S2: e.tensor_scalar(r[1], S2, inv1024, None, ALU.mult),
                      reads=S2R + CPQ, writes=RP[1], tag="l2q")
                s.dve(lambda e, r=r: e.tensor_tensor(r[2], r[0], r[0], ALU.mult),
                      reads=RP[0], writes=RP[2], tag="l2m2")
                s.dve(lambda e, r=r: e.tensor_tensor(r[2], r[1], r[2], ALU.subtract),
                      reads=RP[1] + RP[2], writes=RP[2], tag="l2var")
                s.act(lambda e, r=r: e.activation(r[3], r[2], AF.Sqrt, bias=lneps),
                      reads=RP[2] + CPQ, writes=RP[3], tag="l2sd")
                s.dve(lambda e, r=r: e.reciprocal(r[4], r[3]),
                      reads=RP[3], writes=RP[4], tag="l2rstd")
                s.dve(lambda e, r=r: e.scalar_tensor_tensor(r[5], r[0], neg1, r[4],
                                                            ALU.mult, ALU.mult),
                      reads=RP[0] + RP[4] + CPQ, writes=RP[5], tag="l2b")
                s.pe(lambda e, r=r, bA=bA: e.matmul(bA[:, :], onesf, r[4], start=True, stop=True),
                     reads=RP[4] + fpg(ONF_C, 128), writes=ps_res(bA), tag="l2bca")
                s.pe(lambda e, r=r, bB=bB: e.matmul(bB[:, :], onesf, r[5], start=True, stop=True),
                     reads=RP[5] + fpg(ONF_C, 128), writes=ps_res(bB), tag="l2bcb")
                for cc in range(NCH):
                    src = f32(XA_C + cc * 1024 + tp * 512, 512)
                    t0v = f32(T0_C + (cc % 2) * 512, 512)
                    s.dve(lambda e, src=src, t0v=t0v, bA=bA: e.tensor_tensor(t0v, src, bA[:, :], ALU.mult),
                          reads=fpg(XA_C + cc * 1024 + tp * 512, 512) + ps_res(bA),
                          writes=fpg(T0_C + (cc % 2) * 512, 512), tag="l2n1")
                    dst = bf(XN2_C + cc * 1024 + tp * 512, 512)
                    s.dve(lambda e, dst=dst, t0v=t0v, bB=bB: e.tensor_tensor(dst, t0v, bB[:, :], ALU.add),
                          reads=fpg(T0_C + (cc % 2) * 512, 512) + ps_res(bB),
                          writes=bpg(XN2_C + cc * 1024 + tp * 512, 512), tag="l2n2")
            lgt = f32(LGT_C, 1024, 0, 8)
            for tp in range(2):
                for cc in range(NCH):
                    s.pe(lambda e, cc=cc, tp=tp: e.matmul(
                            pC[0:8, :], bf(WRT_C + cc * 8, 8),
                            xn2[cc][:, tp * 512:(tp + 1) * 512],
                            start=(cc == 0), stop=(cc == 7)),
                         reads=bpg(WRT_C, 512) + xn2_pg[cc], writes=ps_res(pC),
                         tag="rt")
                dst = f32(LGT_C + tp * 512, 512, 0, 8)
                s.dve(lambda e, dst=dst: e.tensor_scalar(
                        dst, pC[0:8, :], f32(BR8_C, 1, 0, 8), None, ALU.add),
                      reads=ps_res(pC) + fpg(0, 512),
                      writes=fpg(LGT_C + tp * 512, 512), tag="rtev")
            tA = f32(TMA_C, 1024, 0, 8)
            tB = f32(TMB_C, 1024, 0, 8)
            LGP = fpg(LGT_C, 1024)
            TAP = fpg(TMA_C, 1024)
            TBP = fpg(TMB_C, 1024)
            s.dma("sync", lambda e: e.dma_start(tA[0:4, :], lgt[4:8, :]),
                  reads=LGP, writes=TAP, tag="mt1", chan="rt", serial=True)
            s.dve(lambda e: e.tensor_tensor(tB[0:4, :], lgt[0:4, :], tA[0:4, :], ALU.max),
                  reads=LGP + TAP, writes=TBP, tag="mt2")
            s.dma("sync", lambda e: e.dma_start(tA[0:2, :], tB[2:4, :]),
                  reads=TBP, writes=TAP, tag="mt3", chan="rt", serial=True)
            s.dve(lambda e: e.tensor_tensor(tA[0:2, :], tB[0:2, :], tA[0:2, :], ALU.max),
                  reads=TBP + TAP, writes=TAP, tag="mt4")
            s.dma("sync", lambda e: e.dma_start(tB[0:1, :], tA[1:2, :]),
                  reads=TAP, writes=TBP, tag="mt5", chan="rt", serial=True)
            s.dve(lambda e: e.tensor_tensor(tB[0:1, :], tA[0:1, :], tB[0:1, :], ALU.max),
                  reads=TAP + TBP, writes=TBP, tag="mt6")
            s.dma("sync", lambda e: e.dma_start(tA[0:1, :], tB[0:1, :]),
                  reads=TBP, writes=TAP, tag="mb1", chan="rt", serial=True)
            s.dma("sync", lambda e: e.dma_start(tA[1:2, :], tA[0:1, :]),
                  reads=TAP, writes=TAP, tag="mb2", chan="rt", serial=True)
            s.dma("sync", lambda e: e.dma_start(tA[2:4, :], tA[0:2, :]),
                  reads=TAP, writes=TAP, tag="mb3", chan="rt", serial=True)
            s.dma("sync", lambda e: e.dma_start(tA[4:8, :], tA[0:4, :]),
                  reads=TAP, writes=TAP, tag="mb4", chan="rt", serial=True)
            mk8 = bf(MK8_C, 1024, 0, 8)
            MK8P = bpg(MK8_C, 1024)
            s.dve(lambda e: e.tensor_tensor(mk8, lgt, tA, ALU.is_ge),
                  reads=LGP + TAP, writes=MK8P, tag="mk8")
            if mode == "dbg":
                for cc in range(NCH):
                    s.dma("sync", lambda e, cc=cc, qc=qc: e.dma_start(
                        dbg_xac[cc * 128:(cc + 1) * 128, qc:qc + QT], xacc[cc]),
                        reads=xacc_pg[cc], writes=[("dxa", q, cc)], tag="dxa", chan="st")
                    s.dma("sync", lambda e, cc=cc, qc=qc: e.dma_start(
                        dbg_xn2[cc * 128:(cc + 1) * 128, qc:qc + QT], xn2[cc]),
                        reads=xn2_pg[cc], writes=[("dxn", q, cc)], tag="dxn", chan="st")
                s.dma("sync", lambda e, qc=qc: e.dma_start(
                    dbg_lgt[:, qc:qc + QT], lgt), reads=LGP,
                    writes=[("dlg", q)], tag="dlg", chan="st")
                s.dma("sync", lambda e, qc=qc: e.dma_start(
                    dbg_mk8[:, qc:qc + QT], mk8), reads=MK8P,
                    writes=[("dmk", q)], tag="dmk", chan="st")
            wpar = [0]
            for ei in range(NEXP):
                mre = bf(MRE_C, 1024, 0, 1)
                s.dma("sync", lambda e, ei=ei, mre=mre: e.dma_start(mre, mk8[ei:ei + 1, :]),
                      reads=MK8P, writes=bpg(MRE_C, 1024), tag="mre", chan="rt", serial=True)
                for tp in range(2):
                    ub, ures = u_bank()
                    s.pe(lambda e, ub=ub, mre=mre, tp=tp: e.matmul(
                            ub, onesrow_b, mre[0:1, tp * 512:(tp + 1) * 512],
                            start=True, stop=True),
                         reads=bpg(MRE_C, 1024) + bpg(CON_C, 512), writes=ures,
                         tag="mbc")
                    s.act(lambda e, ub=ub, tp=tp: e.copy(bf(MKB_C + tp * 512, 512), ub),
                          reads=ures, writes=bpg(MKB_C + tp * 512, 512), tag="mbev")
                maskb = bf(MKB_C, 1024)
                MKBP = bpg(MKB_C, 1024)
                for si in range(4):
                    par = wpar[0] % 2
                    wpar[0] += 1
                    w1v = bf(W1S_C + par * 8192, 8192)
                    w2v = bf(W2S_C + par * 8192, 8192)
                    W1P = bpg(W1S_C + par * 8192, 8192)
                    W2P = bpg(W2S_C + par * 8192, 8192)
                    src1 = w1t[ei].rearrange("(cc p) f -> p cc f", p=128)[:, :, si * 1024:(si + 1) * 1024]
                    s.dma("pool", lambda e, w1v=w1v, src1=src1: e.dma_start(
                        w1v.rearrange("p (cc f) -> p cc f", f=1024), src1),
                        writes=W1P, tag="w1ld", chan="w")
                    src2 = w2t[ei][si * 1024:(si + 1) * 1024, :].rearrange(
                        "(fc p) d -> p fc d", p=128)
                    s.dma("pool", lambda e, w2v=w2v, src2=src2: e.dma_start(
                        w2v.rearrange("p (fc d) -> p fc d", d=1024), src2),
                        writes=W2P, tag="w2ld", chan="w")
                    b1v = f32(B1S_C + par * 512, 8)
                    B1P = fpg(B1S_C + par * 512, 512)
                    s.dma("sync", lambda e, b1v=b1v, ei=ei, si=si: e.dma_start(
                        b1v, b1c[ei][:, si * 8:(si + 1) * 8]),
                        writes=B1P, tag="b1ld", chan="b1", serial=True)
                    for tp in range(2):
                        tps = slice(tp * 512, (tp + 1) * 512)
                        for fc in range(8):
                            zb, zres = rot_bank("z")
                            for cc in range(NCH):
                                s.pe(lambda e, zb=zb, w1v=w1v, cc=cc, fc=fc, tps=tps:
                                     e.matmul(zb, w1v[:, cc * 1024 + fc * 128:
                                                      cc * 1024 + (fc + 1) * 128],
                                              xn2[cc][:, tps],
                                              start=(cc == 0), stop=(cc == 7)),
                                     reads=W1P + xn2_pg[cc], writes=zres, tag="z")
                            h0v = bf(H0_C + (fc % 2) * 512, 512)
                            s.act(lambda e, h0v=h0v, zb=zb, b1v=b1v, fc=fc:
                                  e.activation(h0v, zb, AF.Relu, bias=b1v[:, fc:fc + 1]),
                                  reads=zres + B1P,
                                  writes=bpg(H0_C + (fc % 2) * 512, 512), tag="relu")
                            hv = bf(H_C + fc * 512, 512)
                            s.dve(lambda e, hv=hv, h0v=h0v, tp=tp: e.tensor_tensor(
                                    hv, h0v, maskb[:, tp * 512:(tp + 1) * 512], ALU.mult),
                                  reads=bpg(H0_C + (fc % 2) * 512, 512) + MKBP,
                                  writes=bpg(H_C + fc * 512, 512), tag="hm")
                        for do in range(NCH):
                            yb, yres = rot_bank("y")
                            for fc in range(8):
                                s.pe(lambda e, yb=yb, w2v=w2v, fc=fc, do=do:
                                     e.matmul(yb, w2v[:, fc * 1024 + do * 128:
                                                      fc * 1024 + (do + 1) * 128],
                                              bf(H_C + fc * 512, 512),
                                              start=(fc == 0), stop=(fc == 7)),
                                     reads=W2P + bpg(H_C + fc * 512, 512),
                                     writes=yres, tag="y")
                            dst = f32(XA_C + do * 1024 + tp * 512, 512)
                            s.dve(lambda e, dst=dst, yb=yb: e.tensor_tensor(
                                    dst, yb, dst, ALU.add),
                                  reads=yres + fpg(XA_C + do * 1024 + tp * 512, 512),
                                  writes=fpg(XA_C + do * 1024 + tp * 512, 512),
                                  tag="yacc")
            for tp in range(2):
                for do in range(NCH):
                    ub, ures = u_bank()
                    s.pe(lambda e, ub=ub, do=do, tp=tp: e.matmul(
                            ub, bf(B2SB_C + do * 128, 128, 0, 8),
                            mk8[:, tp * 512:(tp + 1) * 512], start=True, stop=True),
                         reads=bpg(B2SB_C, 1024) + MK8P, writes=ures, tag="b2")
                    dst = f32(XA_C + do * 1024 + tp * 512, 512)
                    s.dve(lambda e, dst=dst, ub=ub: e.tensor_tensor(dst, ub, dst, ALU.add),
                          reads=ures + fpg(XA_C + do * 1024 + tp * 512, 512),
                          writes=fpg(XA_C + do * 1024 + tp * 512, 512), tag="b2acc")
            for do in range(NCH):
                s.dma("sync", lambda e, do=do, qc=qc: e.dma_start(
                    outT[do * 128:(do + 1) * 128, qc:qc + QT], xacc[do]),
                    reads=xacc_pg[do], writes=[("out", q, do)], tag="st", chan="st")

        fin_reads = [("out", q, do) for q in range(NQ) for do in range(NCH)]
        if mode == "dbg":
            fin_reads += [("dxa", q, cc) for q in range(NQ) for cc in range(NCH)]
            fin_reads += [("dxn", q, cc) for q in range(NQ) for cc in range(NCH)]
            fin_reads += [("dlg", q) for q in range(NQ)]
            fin_reads += [("dmk", q) for q in range(NQ)]
        s.op("sync", lambda e: e.nop(), reads=fin_reads, tag="fin")

        emitters, stats = s.emit(nc, es)
        blk = es.enter_context(nc.Block())
        blk.tensor(emitters["pe"])
        blk.scalar(emitters["act"])
        blk.vector(emitters["dve"])
        blk.gpsimd(emitters["pool"])
        blk.sync(emitters["sync"])
    return nc, stats


def make_host_inputs(L0_attn_batch, L0_attn_w_query, L0_attn_w_key,
                     L0_attn_w_value, L0_attn_out_w, L0_attn_out_b,
                     L0_ffn_Wr, L0_ffn_br, L0_ffn_W1, L0_ffn_b1,
                     L0_ffn_W2, L0_ffn_b2):
    bf16 = ml_dtypes.bfloat16
    x = np.asarray(L0_attn_batch, dtype=np.float32)
    shared = {
        "wq": np.ascontiguousarray(np.asarray(L0_attn_w_query, np.float32)),
        "wk": np.ascontiguousarray(np.asarray(L0_attn_w_key, np.float32)),
        "wv": np.ascontiguousarray(np.asarray(L0_attn_w_value, np.float32)),
        "wo": np.ascontiguousarray(np.asarray(L0_attn_out_w, np.float32)),
        "outb": np.ascontiguousarray(
            np.asarray(L0_attn_out_b, np.float32).reshape(8, 128).T),
        "wrt": np.ascontiguousarray(np.asarray(L0_ffn_Wr, np.float32).T),
        "br8": np.ascontiguousarray(np.asarray(L0_ffn_br, np.float32).reshape(8, 1)),
        "w1t": np.ascontiguousarray(
            np.asarray(L0_ffn_W1, np.float32).transpose(0, 2, 1)),
        "b1c": np.ascontiguousarray(
            np.asarray(L0_ffn_b1, np.float32).reshape(8, 32, 128).transpose(0, 2, 1)),
        "w2t": np.ascontiguousarray(
            np.asarray(L0_ffn_W2, np.float32).transpose(0, 2, 1)),
        "b2e": np.ascontiguousarray(np.asarray(L0_ffn_b2, np.float32)),
    }
    k_idx = np.arange(128)[:, None]
    q_idx = np.arange(512)[None, :]
    tri = np.zeros((128, 2048), dtype=bf16)
    for j in range(4):
        tri[:, j * 512:(j + 1) * 512] = ((128 * j + k_idx) <= q_idx).astype(bf16)
    shared["tri"] = tri
    shared["onesb"] = np.ones((128, 2), dtype=bf16)
    shared["onesr"] = np.ones((1, 128), dtype=bf16)
    in_maps = []
    for i in range(8):
        m = dict(shared)
        m["xT"] = np.ascontiguousarray(x[i].T)
        in_maps.append(m)
    return in_maps


_CACHE = {}


def kernel(**inputs):
    in_maps = make_host_inputs(**inputs)
    if "nc" not in _CACHE:
        _CACHE["nc"] = build_program("full")[0]
    nc = _CACHE["nc"]
    res = run_bass_kernel_spmd(nc, in_maps, core_ids=list(range(8)))
    out = np.stack([np.asarray(r["outT"], dtype=np.float32).T
                    for r in res.results])
    return np.ascontiguousarray(out)


# version 07
# speedup vs baseline: 1.2441x; 1.0735x over previous
import numpy as np
import ml_dtypes
import concourse.bass as bass
import concourse.mybir as mybir
from concourse.bass_utils import run_bass_kernel_spmd
from contextlib import ExitStack

F32 = mybir.dt.float32
BF16 = mybir.dt.bfloat16
AF = mybir.ActivationFunctionType
ALU = mybir.AluOpType

STREAMS = ("pe", "act", "dve", "pool", "sync")


class _Op:
    __slots__ = ("idx", "stream", "ctr", "fn", "reads", "writes", "deps",
                 "waits", "inc_val", "tag")

    def __init__(self, idx, stream, ctr, fn, reads, writes, tag):
        self.idx = idx
        self.stream = stream
        self.ctr = ctr
        self.fn = fn
        self.reads = reads
        self.writes = writes
        self.deps = []
        self.waits = []
        self.inc_val = None
        self.tag = tag


class Sched:
    def __init__(self):
        self.ops = []
        self.last_write = {}
        self.readers = {}
        self.last_on_ctr = {}

    def op(self, stream, fn, reads=(), writes=(), dma=False, tag="",
           chan="d", serial=False):
        ctr = (stream + ":" + chan) if dma else stream
        idx = len(self.ops)
        o = _Op(idx, stream, ctr, fn, tuple(reads), tuple(writes), tag)
        deps = set()
        for r in o.reads:
            j = self.last_write.get(r)
            if j is not None:
                deps.add(j)
        for w in o.writes:
            j = self.last_write.get(w)
            if j is not None:
                deps.add(j)
            for rd in self.readers.get(w, ()):
                deps.add(rd)
        if dma and serial:
            p = self.last_on_ctr.get(ctr)
            if p is not None:
                deps.add(p)
        deps.discard(idx)
        o.deps = sorted(deps)
        self.ops.append(o)
        for r in o.reads:
            self.readers.setdefault(r, []).append(idx)
        for w in o.writes:
            self.last_write[w] = idx
            self.readers[w] = []
        self.last_on_ctr[ctr] = idx
        return idx

    def pe(self, fn, reads=(), writes=(), tag=""):
        return self.op("pe", fn, reads, writes, tag=tag)

    def act(self, fn, reads=(), writes=(), tag=""):
        return self.op("act", fn, reads, writes, tag=tag)

    def dve(self, fn, reads=(), writes=(), tag=""):
        return self.op("dve", fn, reads, writes, tag=tag)

    def pool(self, fn, reads=(), writes=(), tag=""):
        return self.op("pool", fn, reads, writes, tag=tag)

    def dma(self, queue, fn, reads=(), writes=(), tag="", chan="d",
            serial=False):
        return self.op(queue, fn, reads, writes, dma=True, tag=tag,
                       chan=chan, serial=serial)

    def _resolve(self):
        n = len(self.ops)
        ctrs = sorted({o.ctr for o in self.ops})
        cidx = {c: i for i, c in enumerate(ctrs)}
        nc_ = len(ctrs)
        seq = [0] * n
        cnt = {c: 0 for c in ctrs}
        for o in self.ops:
            cnt[o.ctr] += 1
            seq[o.idx] = cnt[o.ctr]
        ZERO = tuple([0] * nc_)

        def vmax(a, b):
            return tuple(x if x >= y else y for x, y in zip(a, b))

        start_know = [ZERO] * n
        complete = [ZERO] * n
        prev_on_stream = {}
        for o in self.ops:
            i = o.idx
            know = ZERO
            p = prev_on_stream.get(o.stream)
            if p is not None:
                po = self.ops[p]
                know = start_know[p] if (":" in po.ctr) else complete[p]
            for d in o.deps:
                if know[cidx[self.ops[d].ctr]] >= seq[d]:
                    continue
                do = self.ops[d]
                if do.stream == o.stream and not (":" in do.ctr) \
                        and not (":" in o.ctr):
                    know = vmax(know, complete[d])
                    continue
                o.waits.append(d)
                know = vmax(know, complete[d])
            start_know[i] = know
            me = list(know)
            j = cidx[o.ctr]
            if seq[o.idx] > me[j]:
                me[j] = seq[o.idx]
            complete[i] = tuple(me)
            prev_on_stream[o.stream] = i
        marked = set()
        for o in self.ops:
            if (":" in o.ctr):
                marked.add(o.idx)
            for d in o.waits:
                marked.add(d)
        val = {}
        mcnt = {c: 0 for c in ctrs}
        for o in self.ops:
            if o.idx in marked:
                mcnt[o.ctr] += 1
                step = 16 if (":" in o.ctr) else 1
                o.inc_val = mcnt[o.ctr] * step
                val[o.idx] = o.inc_val
        for o in self.ops:
            best = {}
            for d in o.waits:
                c, v = self.ops[d].ctr, val[d]
                if best.get(c, -1) < v:
                    best[c] = v
            o.waits = sorted(best.items())
        return ctrs

    def emit(self, nc, es):
        ctrs = self._resolve()
        sems = {c: es.enter_context(nc.semaphore("sem_" + c)) for c in ctrs}
        by_stream = {s: [] for s in STREAMS}
        for o in self.ops:
            by_stream[o.stream].append(o)
        stats = {
            "n_ops": len(self.ops),
            "n_waits": sum(len(o.waits) for o in self.ops),
            "n_incs": sum(1 for o in self.ops if o.inc_val is not None),
        }

        def make_emitter(stream):
            def emitter(eng):
                for o in by_stream[stream]:
                    for c, v in o.waits:
                        eng.wait_ge(sems[c], v)
                    inst = o.fn(eng)
                    if o.inc_val is not None:
                        if inst is None:
                            raise RuntimeError(
                                f"op {o.idx} ({o.tag}) needs inc, fn gave None")
                        inst.then_inc(sems[o.ctr],
                                      16 if (":" in o.ctr) else 1)
            return emitter

        return {st: make_emitter(st) for st in STREAMS}, stats


S = 4096
D = 1024
NCH = 8
NQ = 4
QT = S // NQ
DFF = 4096
NEXP = 8

TRI_C = 0
B2SB_C = 2048
WRT_C = 3072
CON_C = 3584
BASE_A = 4096
XN_C = BASE_A
QTB_C = XN_C + 8 * 4096
KTB_C = QTB_C + 4096
O2P_C = KTB_C + 4096
V2A_C = O2P_C + 4096
PB_C = V2A_C + 4224
PBB_C = 61568
WQS_C = PB_C + 2048
WKS_C = WQS_C + 2048
WVS_C = WKS_C + 2048
TOP_A = WVS_C + 2048
XN2_C = BASE_A
W1S_C = XN2_C + 8 * 1024
W2S_C = W1S_C + 16384
H_C = W2S_C + 16384
H0_C = H_C + 4096
O2H_C = H0_C + 1024
WOS_C = O2H_C + 4096
MKB_C = WOS_C + 8192
MK8_C = MKB_C + 1024
MRE_C = MK8_C + 1024
TOP_B = MRE_C + 1024
XB_C = TOP_B
XQ_C = XB_C + 1024
NB = XQ_C + 1024

CP_C = 0
OUTB_C = 16
BR8_C = 24
ONF_C = 128
ROW_C = 512
XF_C = ROW_C + 4 * 512
T0_C = XF_C + 8 * 512
XO_C = T0_C + 1024
LGT_C = XO_C + 1024
TMA_C = T0_C
TMB_C = XO_C
XA_C = LGT_C + 1024
B1S_C = XA_C + 8 * 1024
NF = B1S_C + 1024


def _pages(arena, c0, c1):
    return [(arena, p) for p in range(c0 // 512, (c1 + 511) // 512)]


def build_program(mode="full"):
    nc = bass.Bass("TRN2", target_bir_lowering=False, debug=False)
    xT = nc.dram_tensor("xT", [D, S], F32, kind="ExternalInput").ap()
    wq = nc.dram_tensor("wq", [D, D], F32, kind="ExternalInput").ap()
    wk = nc.dram_tensor("wk", [D, D], F32, kind="ExternalInput").ap()
    wv = nc.dram_tensor("wv", [D, D], F32, kind="ExternalInput").ap()
    wo = nc.dram_tensor("wo", [D, D], F32, kind="ExternalInput").ap()
    outb = nc.dram_tensor("outb", [128, 8], F32, kind="ExternalInput").ap()
    wrt = nc.dram_tensor("wrt", [D, NEXP], F32, kind="ExternalInput").ap()
    br8 = nc.dram_tensor("br8", [NEXP, 1], F32, kind="ExternalInput").ap()
    w1t = nc.dram_tensor("w1t", [NEXP, D, DFF], F32, kind="ExternalInput").ap()
    b1c = nc.dram_tensor("b1c", [NEXP, 128, 32], F32, kind="ExternalInput").ap()
    w2t = nc.dram_tensor("w2t", [NEXP, DFF, D], F32, kind="ExternalInput").ap()
    b2e = nc.dram_tensor("b2e", [NEXP, D], F32, kind="ExternalInput").ap()
    tri = nc.dram_tensor("tri", [128, 2048], BF16, kind="ExternalInput").ap()
    onesb = nc.dram_tensor("onesb", [128, 2], BF16, kind="ExternalInput").ap()
    o2s = nc.dram_tensor("o2s", [D, S], BF16, kind="Internal").ap()
    outT = nc.dram_tensor("outT", [D, S], F32, kind="ExternalOutput").ap()
    if mode == "dbg":
        dbg_xac = nc.dram_tensor("dbg_xac", [D, S], F32, kind="ExternalOutput").ap()
        dbg_xn2 = nc.dram_tensor("dbg_xn2", [D, S], BF16, kind="ExternalOutput").ap()
        dbg_lgt = nc.dram_tensor("dbg_lgt", [8, S], F32, kind="ExternalOutput").ap()
        dbg_mk8 = nc.dram_tensor("dbg_mk8", [8, S], BF16, kind="ExternalOutput").ap()

    s = Sched()
    with ExitStack() as es:
        AB = es.enter_context(nc.sbuf_tensor("AB", [128, NB], BF16))
        AF32 = es.enter_context(nc.sbuf_tensor("AF32", [128, NF], F32))
        scoreP = es.enter_context(nc.psum_tensor("scoreP", [128, 2048], F32))
        pA = es.enter_context(nc.psum_tensor("pA", [128, 512], F32))
        pB = es.enter_context(nc.psum_tensor("pB", [128, 512], F32))
        pC = es.enter_context(nc.psum_tensor("pC", [128, 512], F32))
        pD = es.enter_context(nc.psum_tensor("pD", [128, 512], F32))
        def ps_res(t, off=0):
            if t is scoreP:
                return [("PS", off // 512)]
            return [("PS", {id(pA): 4, id(pB): 5, id(pC): 6, id(pD): 7}[id(t)])]

        def bf(c0, w, r0=0, rn=128):
            return AB[r0:rn, c0:c0 + w]

        def f32(c0, w, r0=0, rn=128):
            return AF32[r0:rn, c0:c0 + w]

        def bpg(c0, w):
            return _pages("B", c0, c0 + w)

        def fpg(c0, w):
            return _pages("F", c0, c0 + w)

        s.dma("sync", lambda e: e.dma_start(bf(TRI_C, 2048), tri),
              writes=bpg(TRI_C, 2048), tag="tri", chan="ini", serial=True)
        s.dma("sync", lambda e: e.dma_start(bf(CON_C, 2, 0, 128), onesb),
              writes=bpg(CON_C, 512), tag="ones", chan="ini", serial=True)
        s.dma("sync", lambda e: e.dma_start(f32(OUTB_C, 8), outb),
              writes=fpg(0, 512), tag="outb", chan="ini", serial=True)
        s.dma("sync", lambda e: e.dma_start(f32(BR8_C, 1, 0, 8), br8),
              writes=fpg(0, 512), tag="br8", chan="ini", serial=True)
        s.dma("pool", lambda e: e.dma_start(bf(B2SB_C, 1024, 0, 8), b2e),
              writes=bpg(B2SB_C, 1024), tag="b2", chan="ini", serial=True)
        s.dma("pool", lambda e: e.dma_start(
            bf(WRT_C, 64).rearrange("p (cc e) -> p cc e", e=8),
            wrt.rearrange("(cc p) e -> p cc e", p=128)),
              writes=bpg(WRT_C, 512), tag="wrt", chan="ini", serial=True)
        s.dve(lambda e: e.memset(f32(CP_C, 1), 0.125), writes=fpg(0, 512))
        s.dve(lambda e: e.memset(f32(CP_C + 1, 1), -12.0), writes=fpg(0, 512))
        s.dve(lambda e: e.memset(f32(CP_C + 2, 1), 1e-5), writes=fpg(0, 512))
        s.dve(lambda e: e.memset(f32(CP_C + 3, 1), 1.0 / 1024.0), writes=fpg(0, 512))
        s.dve(lambda e: e.memset(f32(CP_C + 4, 1), -1.0), writes=fpg(0, 512))
        s.dve(lambda e: e.memset(f32(ONF_C, 128, 0, 1), 1.0), writes=fpg(ONF_C, 128))
        CPQ = fpg(0, 512)
        qscale = f32(CP_C, 1)
        ebias = f32(CP_C + 1, 1)
        lneps = f32(CP_C + 2, 1, 0, 1)
        inv1024 = f32(CP_C + 3, 1, 0, 1)
        neg1 = f32(CP_C + 4, 1, 0, 1)
        onesf = f32(ONF_C, 128, 0, 1)
        onescol_b = bf(CON_C, 1, 0, 128)
        onesr = nc.dram_tensor("onesr", [1, 128], BF16, kind="ExternalInput").ap()
        s.dma("sync", lambda e: e.dma_start(bf(CON_C + 2, 128, 0, 1), onesr),
              writes=bpg(CON_C, 512), tag="onesr", chan="ini", serial=True)
        onesrow_b = bf(CON_C + 2, 128, 0, 1)

        for tt in range(8):
            ts = slice(tt * 512, (tt + 1) * 512)
            tpar = tt % 2
            S1 = scoreP[0:1, tpar * 1024:tpar * 1024 + 512]
            S2 = scoreP[0:1, tpar * 1024 + 512:tpar * 1024 + 1024]
            S1R = [("PS", 2 * tpar)]
            S2R = [("PS", 2 * tpar + 1)]
            bA = pA if tpar == 0 else pC
            bB = pB if tpar == 0 else pD
            for cc in range(NCH):
                sl = cc % 2
                xfc = XF_C + cc * 512
                xfv = f32(xfc, 512)
                s.dma("sync", lambda e, cc=cc, ts=ts, xfv=xfv: e.dma_start(
                    xfv, xT[cc * 128:(cc + 1) * 128, ts]),
                    writes=fpg(xfc, 512), tag="lnx", chan="xs")
                xbv = bf(XB_C + sl * 512, 512)
                s.pool(lambda e, xfv=xfv, xbv=xbv: e.tensor_copy(xbv, xfv),
                       reads=fpg(xfc, 512),
                       writes=bpg(XB_C + sl * 512, 512), tag="lncast")
                xqv = bf(XQ_C + sl * 512, 512)
                s.act(lambda e, xbv=xbv, xqv=xqv: e.activation(xqv, xbv, AF.Square),
                      reads=bpg(XB_C + sl * 512, 512),
                      writes=bpg(XQ_C + sl * 512, 512), tag="lnsq")
                s.pe(lambda e, xbv=xbv, cc=cc, S1=S1: e.matmul(
                        S1, onescol_b, xbv, start=(cc == 0), stop=(cc == 7)),
                     reads=bpg(XB_C + sl * 512, 512) + bpg(CON_C, 512),
                     writes=S1R, tag="lns1")
                s.pe(lambda e, xqv=xqv, cc=cc, S2=S2: e.matmul(
                        S2, onescol_b, xqv, start=(cc == 0), stop=(cc == 7)),
                     reads=bpg(XQ_C + sl * 512, 512) + bpg(CON_C, 512),
                     writes=S2R, tag="lns2")
            r = [f32(ROW_C + i * 512, 512, 0, 1) for i in range(4)]
            RP = [fpg(ROW_C + i * 512, 512) for i in range(4)]
            s.dve(lambda e, r=r, S1=S1: e.tensor_scalar(r[0], S1, inv1024, None, ALU.mult),
                  reads=S1R + CPQ, writes=RP[0], tag="lnm")
            s.dve(lambda e, r=r, S2=S2: e.tensor_scalar(r[1], S2, inv1024, None, ALU.mult),
                  reads=S2R + CPQ, writes=RP[1], tag="lnq")
            s.dve(lambda e, r=r: e.tensor_tensor(r[2], r[0], r[0], ALU.mult),
                  reads=RP[0], writes=RP[2], tag="lnm2")
            s.dve(lambda e, r=r: e.tensor_tensor(r[1], r[1], r[2], ALU.subtract),
                  reads=RP[1] + RP[2], writes=RP[1], tag="lnvar")
            s.act(lambda e, r=r: e.activation(r[2], r[1], AF.Sqrt, bias=lneps),
                  reads=RP[1] + CPQ, writes=RP[2], tag="lnsd")
            s.dve(lambda e, r=r: e.reciprocal(r[3], r[2]),
                  reads=RP[2], writes=RP[3], tag="lnrstd")
            s.dve(lambda e, r=r: e.scalar_tensor_tensor(r[2], r[0], neg1, r[3],
                                                        ALU.mult, ALU.mult),
                  reads=RP[0] + RP[3] + CPQ, writes=RP[2], tag="lnb")
            s.pe(lambda e, r=r, bA=bA: e.matmul(bA[:, :], onesf, r[3], start=True, stop=True),
                 reads=RP[3] + fpg(ONF_C, 128), writes=ps_res(bA), tag="lnbca")
            s.pe(lambda e, r=r, bB=bB: e.matmul(bB[:, :], onesf, r[2], start=True, stop=True),
                 reads=RP[2] + fpg(ONF_C, 128), writes=ps_res(bB), tag="lnbcb")
            for cc in range(NCH):
                xfc = XF_C + cc * 512
                xfv = f32(xfc, 512)
                t0v = f32(T0_C + (cc % 2) * 512, 512)
                s.dve(lambda e, xfv=xfv, t0v=t0v, bA=bA: e.tensor_tensor(t0v, xfv, bA[:, :], ALU.mult),
                      reads=fpg(xfc, 512) + ps_res(bA),
                      writes=fpg(T0_C + (cc % 2) * 512, 512), tag="lnn1")
                xnv = bf(XN_C + cc * 4096 + tt * 512, 512)
                s.dve(lambda e, xnv=xnv, t0v=t0v, bB=bB: e.tensor_tensor(xnv, t0v, bB[:, :], ALU.add),
                      reads=fpg(T0_C + (cc % 2) * 512, 512) + ps_res(bB),
                      writes=bpg(XN_C + cc * 4096 + tt * 512, 512), tag="lnn2")

        xn = [bf(XN_C + cc * 4096, 4096) for cc in range(NCH)]
        xn_pg = [bpg(XN_C + cc * 4096, 4096) for cc in range(NCH)]
        v2a_full = bf(V2A_C, 4160)
        s.dve(lambda e: e.memset(
            v2a_full.rearrange("p (tc w) -> p tc w", w=130)[:, :, 64:65], 1.0),
            writes=bpg(V2A_C, 4160), tag="v2ones1")
        s.dve(lambda e: e.memset(
            v2a_full.rearrange("p (tc w) -> p tc w", w=130)[:, :, 129:130], 1.0),
            writes=bpg(V2A_C, 4160), tag="v2ones2")

        evrot = [0]

        def evac_bank():
            b = evrot[0] % 4
            evrot[0] += 1
            return scoreP[:, b * 512:(b + 1) * 512], [("PS", b)]

        for hp in range(8):
            par = hp % 2
            wq_s = bf(WQS_C + par * 1024, 1024)
            wk_s = bf(WKS_C + par * 1024, 1024)
            wv_s = bf(WVS_C + par * 1024, 1024)
            wq_pg = bpg(WQS_C + par * 1024, 1024)
            wk_pg = bpg(WKS_C + par * 1024, 1024)
            wv_pg = bpg(WVS_C + par * 1024, 1024)
            for (w_s, w_pg, wsrc, nm) in ((wq_s, wq_pg, wq, "wq"),
                                          (wk_s, wk_pg, wk, "wk"),
                                          (wv_s, wv_pg, wv, "wv")):
                src = wsrc.rearrange("(cc p) d -> p cc d", p=128)[:, :, hp * 128:(hp + 1) * 128]
                s.dma("pool", lambda e, w_s=w_s, src=src: e.dma_start(
                    w_s.rearrange("p (cc d) -> p cc d", d=128), src),
                    writes=w_pg, tag=nm, chan="wqv")
            qtb = bf(QTB_C, 4096)
            ktb = bf(KTB_C, 4096)
            for (dst_c, w_s, w_pg, sc, nm) in ((QTB_C, wq_s, wq_pg, qscale, "qt"),
                                               (KTB_C, wk_s, wk_pg, None, "kt")):
                for tt in range(8):
                    ts = slice(tt * 512, (tt + 1) * 512)
                    bank, bres = evac_bank()
                    for cc in range(NCH):
                        s.pe(lambda e, bank=bank, w_s=w_s, cc=cc, ts=ts: e.matmul(
                                bank, w_s[:, cc * 128:(cc + 1) * 128], xn[cc][:, ts],
                                start=(cc == 0), stop=(cc == 7)),
                             reads=w_pg + xn_pg[cc], writes=bres, tag=nm)
                    dst = bf(dst_c + tt * 512, 512)
                    if sc is None:
                        s.act(lambda e, dst=dst, bank=bank: e.copy(dst, bank),
                              reads=bres, writes=bpg(dst_c + tt * 512, 512), tag=nm + "ev")
                    else:
                        s.act(lambda e, dst=dst, bank=bank, sc=sc: e.activation(
                                dst, bank, AF.Copy, scale=sc),
                              reads=bres + CPQ, writes=bpg(dst_c + tt * 512, 512),
                              tag=nm + "ev")
            for tc in range(32):
                bank = (pC, ps_res(pC)) if tc % 2 == 0 else (pD, ps_res(pD))
                bk, bres = bank
                for cc in range(NCH):
                    s.pe(lambda e, bk=bk, cc=cc, tc=tc, wv_s=wv_s: e.matmul(
                            bk[:, 0:128], xn[cc][:, tc * 128:(tc + 1) * 128],
                            wv_s[:, cc * 128:(cc + 1) * 128],
                            start=(cc == 0), stop=(cc == 7)),
                         reads=wv_pg + xn_pg[cc], writes=bres, tag="v")
                va = bf(V2A_C + tc * 130, 64)
                vb = bf(V2A_C + tc * 130 + 65, 64)
                s.dve(lambda e, va=va, bk=bk: e.tensor_copy(va, bk[:, 0:64]),
                      reads=bres, writes=bpg(V2A_C + tc * 130, 64), tag="vev")
                s.dve(lambda e, vb=vb, bk=bk: e.tensor_copy(vb, bk[:, 64:128]),
                      reads=bres, writes=bpg(V2A_C + tc * 130 + 65, 64), tag="vev")
            PBC = [[PB_C, PB_C + 1024], [PBB_C, PBB_C + 1024]]

            def emit_tail_recips(qt_i):
                for h in range(2):
                    obank = pA if h == 0 else pB
                    r0 = f32(ROW_C + h * 512, 512, 0, 1)
                    s.dve(lambda e, obank=obank, r0=r0: e.reciprocal(
                            r0, obank[64:65, :]),
                          reads=ps_res(obank), writes=fpg(ROW_C + h * 512, 512),
                          tag="recip")

            def emit_tail_rest(qt_i):
                for h in range(2):
                    hb = 64 * h
                    obank = pA if h == 0 else pB
                    ob_res = ps_res(obank)
                    r0 = f32(ROW_C + h * 512, 512, 0, 1)
                    rb = pC if h == 0 else pD
                    s.pe(lambda e, r0=r0, rb=rb: e.matmul(
                            rb[0:64, :], onesf[0:1, 0:64], r0,
                            start=True, stop=True),
                         reads=fpg(ROW_C + h * 512, 512) + fpg(ONF_C, 128),
                         writes=ps_res(rb), tag="rbc")
                    dst = bf(O2P_C + qt_i * 512, 512, hb, hb + 64)
                    s.act(lambda e, dst=dst, obank=obank: e.copy(
                            dst, obank[0:64, :]),
                          reads=ob_res, writes=bpg(O2P_C + qt_i * 512, 512),
                          tag="oev")
                    s.dve(lambda e, dst=dst, rb=rb: e.tensor_tensor(
                            dst, dst, rb[0:64, :], ALU.mult),
                          reads=bpg(O2P_C + qt_i * 512, 512) + ps_res(rb),
                          writes=bpg(O2P_C + qt_i * 512, 512), tag="onorm")

            def emit_pv(prev_g, slot, first, last):
                for h in range(2):
                    obank = pA if h == 0 else pB
                    pbc = PBC[h][slot]
                    ppb = bf(pbc, 1024)
                    for j in range(2):
                        kc = 2 * prev_g + j
                        s.pe(lambda e, kc=kc, h=h, ppb=ppb, j=j,
                             st=(first and j == 0), sp=(last and j == 1),
                             obank=obank: e.matmul(
                                obank[0:65, :],
                                v2a_full[:, kc * 130 + 65 * h:
                                         kc * 130 + 65 * h + 65],
                                ppb[:, j * 512:(j + 1) * 512],
                                start=st, stop=sp),
                             reads=bpg(V2A_C + kc * 130, 130) + bpg(pbc, 1024),
                             writes=ps_res(obank), tag="pv")

            pending_tail = None
            for qt_i in range(8):
                qts = slice(qt_i * 512, (qt_i + 1) * 512)
                ng = 2 * (qt_i + 1)
                order = list(range(ng - 2, ng)) + list(range(0, ng - 2))
                if pending_tail is not None:
                    emit_tail_recips(pending_tail)
                prev = None
                npv = 0
                for idx, g in enumerate(order):
                    slot = idx % 2
                    for h in range(2):
                        hb = 64 * h
                        for j in range(2):
                            kc = 2 * g + j
                            bi = 2 * h + j
                            sb = scoreP[:, bi * 512:(bi + 1) * 512]
                            s.pe(lambda e, sb=sb, kc=kc, hb=hb, qts=qts,
                                 ktb=ktb, qtb=qtb: e.matmul(
                                    sb, ktb[hb:hb + 64, kc * 128:(kc + 1) * 128],
                                    qtb[hb:hb + 64, qts], start=True, stop=True),
                                 reads=bpg(KTB_C, 4096) +
                                       bpg(QTB_C + qt_i * 512, 512),
                                 writes=[("PS", bi)], tag="sc")
                    for h in range(2):
                        pbc = PBC[h][slot]
                        pbv = bf(pbc, 1024)
                        s.act(lambda e, pbv=pbv, h=h: e.activation(
                                pbv, scoreP[:, 2 * h * 512:(2 * h + 2) * 512],
                                AF.Exp, bias=ebias),
                              reads=[("PS", 2 * h), ("PS", 2 * h + 1)] + CPQ,
                              writes=bpg(pbc, 1024), tag="exp")
                        if g >= ng - 2:
                            dj = (g - (ng - 2))
                            trv = bf(TRI_C + dj * 1024, 1024)
                            s.dve(lambda e, pbv=pbv, trv=trv: e.tensor_tensor(
                                    pbv, pbv, trv, ALU.mult),
                                  reads=bpg(pbc, 1024) +
                                        bpg(TRI_C + dj * 1024, 1024),
                                  writes=bpg(pbc, 1024), tag="trim")
                    if prev is not None:
                        if pending_tail is not None:
                            emit_tail_rest(pending_tail)
                            pending_tail = None
                        emit_pv(prev[0], prev[1], npv == 0, False)
                        npv += 1
                    prev = (g, slot)
                if pending_tail is not None:
                    emit_tail_rest(pending_tail)
                    pending_tail = None
                emit_pv(prev[0], prev[1], npv == 0, True)
                pending_tail = qt_i
            emit_tail_recips(pending_tail)
            emit_tail_rest(pending_tail)
            pending_tail = None
            s.dma("sync", lambda e, hp=hp: e.dma_start(
                o2s[hp * 128:(hp + 1) * 128, :], bf(O2P_C, 4096)),
                reads=bpg(O2P_C, 4096), writes=[("o2s", hp)], tag="o2st", chan="o2w")

        for hp in range(8):
            s.dma("pool", lambda e, hp=hp: e.dma_start(
                bf(WOS_C + hp * 1024, 1024), wo[hp * 128:(hp + 1) * 128, :]),
                writes=bpg(WOS_C + hp * 1024, 1024), tag="wo", chan="wo")

        brot = [0]

        def rot_bank(group):
            if group == "z":
                b = brot[0] % 2
                brot[0] += 1
                return scoreP[:, b * 512:(b + 1) * 512], [("PS", b)]
            if group == "y":
                b = brot[0] % 2
                brot[0] += 1
                return scoreP[:, (2 + b) * 512:(3 + b) * 512], [("PS", 2 + b)]
            raise ValueError

        urot = [0]

        def u_bank():
            b = urot[0] % 2
            urot[0] += 1
            t = pC if b == 0 else pD
            return t[:, :], ps_res(t)

        xacc = [f32(XA_C + c * 1024, 1024) for c in range(NCH)]
        xacc_pg = [fpg(XA_C + c * 1024, 1024) for c in range(NCH)]
        xn2 = [bf(XN2_C + c * 1024, 1024) for c in range(NCH)]
        xn2_pg = [bpg(XN2_C + c * 1024, 1024) for c in range(NCH)]

        for q in range(NQ):
            qc = q * QT
            for tp in range(2):
                c0 = qc + tp * 512
                o2h = bf(O2H_C, 4096)
                s.dma("sync", lambda e, c0=c0, o2h=o2h: e.dma_start(
                    o2h.rearrange("p (hp t) -> p hp t", t=512),
                    o2s.rearrange("(hp p) t -> p hp t", p=128)[:, :, c0:c0 + 512]),
                    reads=[("o2s", hp) for hp in range(8)],
                    writes=bpg(O2H_C, 4096), tag="o2h", chan="o2h")
                for do in range(NCH):
                    bank, bres = evac_bank()
                    for hp in range(8):
                        s.pe(lambda e, bank=bank, hp=hp, do=do, o2h=o2h: e.matmul(
                                bank, bf(WOS_C + hp * 1024 + do * 128, 128),
                                o2h[:, hp * 512:(hp + 1) * 512],
                                start=(hp == 0), stop=(hp == 7)),
                             reads=bpg(WOS_C + hp * 1024, 1024) + bpg(O2H_C, 4096),
                             writes=bres, tag="wop")
                    xov = f32(XO_C + (do % 2) * 512, 512)
                    s.dma("sync", lambda e, xov=xov, do=do, c0=c0: e.dma_start(
                        xov, xT[do * 128:(do + 1) * 128, c0:c0 + 512]),
                        writes=fpg(XO_C + (do % 2) * 512, 512), tag="xo", chan="xs")
                    dst = f32(XA_C + do * 1024 + tp * 512, 512)
                    s.dve(lambda e, dst=dst, bank=bank, do=do, xov=xov:
                          e.scalar_tensor_tensor(dst, bank, f32(OUTB_C + do, 1),
                                                 xov, ALU.add, ALU.add),
                          reads=bres + fpg(XO_C + (do % 2) * 512, 512) + fpg(0, 512),
                          writes=fpg(XA_C + do * 1024 + tp * 512, 512), tag="b1acc")
            if mode == "attn":
                for do in range(NCH):
                    s.dma("sync", lambda e, do=do, qc=qc: e.dma_start(
                        outT[do * 128:(do + 1) * 128, qc:qc + QT], xacc[do]),
                        reads=xacc_pg[do], writes=[("out", q, do)], tag="st", chan="st")
                continue
            for tp in range(2):
                S1 = scoreP[0:1, tp * 1024:tp * 1024 + 512]
                S2 = scoreP[0:1, tp * 1024 + 512:tp * 1024 + 1024]
                S1R = [("PS", 2 * tp)]
                S2R = [("PS", 2 * tp + 1)]
                bA = pA if tp == 0 else pC
                bB = pB if tp == 0 else pD
                for cc in range(NCH):
                    src = f32(XA_C + cc * 1024 + tp * 512, 512)
                    xbv = bf(XB_C + (cc % 2) * 512, 512)
                    s.pool(lambda e, src=src, xbv=xbv: e.tensor_copy(xbv, src),
                           reads=fpg(XA_C + cc * 1024 + tp * 512, 512),
                           writes=bpg(XB_C + (cc % 2) * 512, 512), tag="l2cast")
                    xqv = bf(XQ_C + (cc % 2) * 512, 512)
                    s.act(lambda e, xbv=xbv, xqv=xqv: e.activation(xqv, xbv, AF.Square),
                          reads=bpg(XB_C + (cc % 2) * 512, 512),
                          writes=bpg(XQ_C + (cc % 2) * 512, 512), tag="l2sq")
                    s.pe(lambda e, xbv=xbv, cc=cc, S1=S1: e.matmul(
                            S1, onescol_b, xbv, start=(cc == 0), stop=(cc == 7)),
                         reads=bpg(XB_C + (cc % 2) * 512, 512) + bpg(CON_C, 512),
                         writes=S1R, tag="l2s1")
                    s.pe(lambda e, xqv=xqv, cc=cc, S2=S2: e.matmul(
                            S2, onescol_b, xqv, start=(cc == 0), stop=(cc == 7)),
                         reads=bpg(XQ_C + (cc % 2) * 512, 512) + bpg(CON_C, 512),
                         writes=S2R, tag="l2s2")
                r = [f32(ROW_C + i * 512, 512, 0, 1) for i in range(4)]
                RP = [fpg(ROW_C + i * 512, 512) for i in range(4)]
                s.dve(lambda e, r=r, S1=S1: e.tensor_scalar(r[0], S1, inv1024, None, ALU.mult),
                      reads=S1R + CPQ, writes=RP[0], tag="l2m")
                s.dve(lambda e, r=r, S2=S2: e.tensor_scalar(r[1], S2, inv1024, None, ALU.mult),
                      reads=S2R + CPQ, writes=RP[1], tag="l2q")
                s.dve(lambda e, r=r: e.tensor_tensor(r[2], r[0], r[0], ALU.mult),
                      reads=RP[0], writes=RP[2], tag="l2m2")
                s.dve(lambda e, r=r: e.tensor_tensor(r[1], r[1], r[2], ALU.subtract),
                      reads=RP[1] + RP[2], writes=RP[1], tag="l2var")
                s.act(lambda e, r=r: e.activation(r[2], r[1], AF.Sqrt, bias=lneps),
                      reads=RP[1] + CPQ, writes=RP[2], tag="l2sd")
                s.dve(lambda e, r=r: e.reciprocal(r[3], r[2]),
                      reads=RP[2], writes=RP[3], tag="l2rstd")
                s.dve(lambda e, r=r: e.scalar_tensor_tensor(r[2], r[0], neg1, r[3],
                                                            ALU.mult, ALU.mult),
                      reads=RP[0] + RP[3] + CPQ, writes=RP[2], tag="l2b")
                s.pe(lambda e, r=r, bA=bA: e.matmul(bA[:, :], onesf, r[3], start=True, stop=True),
                     reads=RP[3] + fpg(ONF_C, 128), writes=ps_res(bA), tag="l2bca")
                s.pe(lambda e, r=r, bB=bB: e.matmul(bB[:, :], onesf, r[2], start=True, stop=True),
                     reads=RP[2] + fpg(ONF_C, 128), writes=ps_res(bB), tag="l2bcb")
                for cc in range(NCH):
                    src = f32(XA_C + cc * 1024 + tp * 512, 512)
                    t0v = f32(T0_C + (cc % 2) * 512, 512)
                    s.dve(lambda e, src=src, t0v=t0v, bA=bA: e.tensor_tensor(t0v, src, bA[:, :], ALU.mult),
                          reads=fpg(XA_C + cc * 1024 + tp * 512, 512) + ps_res(bA),
                          writes=fpg(T0_C + (cc % 2) * 512, 512), tag="l2n1")
                    dst = bf(XN2_C + cc * 1024 + tp * 512, 512)
                    s.dve(lambda e, dst=dst, t0v=t0v, bB=bB: e.tensor_tensor(dst, t0v, bB[:, :], ALU.add),
                          reads=fpg(T0_C + (cc % 2) * 512, 512) + ps_res(bB),
                          writes=bpg(XN2_C + cc * 1024 + tp * 512, 512), tag="l2n2")
            lgt = f32(LGT_C, 1024, 0, 8)
            for tp in range(2):
                for cc in range(NCH):
                    s.pe(lambda e, cc=cc, tp=tp: e.matmul(
                            pC[0:8, :], bf(WRT_C + cc * 8, 8),
                            xn2[cc][:, tp * 512:(tp + 1) * 512],
                            start=(cc == 0), stop=(cc == 7)),
                         reads=bpg(WRT_C, 512) + xn2_pg[cc], writes=ps_res(pC),
                         tag="rt")
                dst = f32(LGT_C + tp * 512, 512, 0, 8)
                s.dve(lambda e, dst=dst: e.tensor_scalar(
                        dst, pC[0:8, :], f32(BR8_C, 1, 0, 8), None, ALU.add),
                      reads=ps_res(pC) + fpg(0, 512),
                      writes=fpg(LGT_C + tp * 512, 512), tag="rtev")
            tA = f32(TMA_C, 1024, 0, 8)
            tB = f32(TMB_C, 1024, 0, 8)
            LGP = fpg(LGT_C, 1024)
            TAP = fpg(TMA_C, 1024)
            TBP = fpg(TMB_C, 1024)
            s.dma("sync", lambda e: e.dma_start(tA[0:4, :], lgt[4:8, :]),
                  reads=LGP, writes=TAP, tag="mt1", chan="rt", serial=True)
            s.dve(lambda e: e.tensor_tensor(tB[0:4, :], lgt[0:4, :], tA[0:4, :], ALU.max),
                  reads=LGP + TAP, writes=TBP, tag="mt2")
            s.dma("sync", lambda e: e.dma_start(tA[0:2, :], tB[2:4, :]),
                  reads=TBP, writes=TAP, tag="mt3", chan="rt", serial=True)
            s.dve(lambda e: e.tensor_tensor(tA[0:2, :], tB[0:2, :], tA[0:2, :], ALU.max),
                  reads=TBP + TAP, writes=TAP, tag="mt4")
            s.dma("sync", lambda e: e.dma_start(tB[0:1, :], tA[1:2, :]),
                  reads=TAP, writes=TBP, tag="mt5", chan="rt", serial=True)
            s.dve(lambda e: e.tensor_tensor(tB[0:1, :], tA[0:1, :], tB[0:1, :], ALU.max),
                  reads=TAP + TBP, writes=TBP, tag="mt6")
            s.dma("sync", lambda e: e.dma_start(tA[0:1, :], tB[0:1, :]),
                  reads=TBP, writes=TAP, tag="mb1", chan="rt", serial=True)
            s.dma("sync", lambda e: e.dma_start(tA[1:2, :], tA[0:1, :]),
                  reads=TAP, writes=TAP, tag="mb2", chan="rt", serial=True)
            s.dma("sync", lambda e: e.dma_start(tA[2:4, :], tA[0:2, :]),
                  reads=TAP, writes=TAP, tag="mb3", chan="rt", serial=True)
            s.dma("sync", lambda e: e.dma_start(tA[4:8, :], tA[0:4, :]),
                  reads=TAP, writes=TAP, tag="mb4", chan="rt", serial=True)
            mk8 = bf(MK8_C, 1024, 0, 8)
            MK8P = bpg(MK8_C, 1024)
            s.dve(lambda e: e.tensor_tensor(mk8, lgt, tA, ALU.is_ge),
                  reads=LGP + TAP, writes=MK8P, tag="mk8")
            if mode == "dbg":
                for cc in range(NCH):
                    s.dma("sync", lambda e, cc=cc, qc=qc: e.dma_start(
                        dbg_xac[cc * 128:(cc + 1) * 128, qc:qc + QT], xacc[cc]),
                        reads=xacc_pg[cc], writes=[("dxa", q, cc)], tag="dxa", chan="st")
                    s.dma("sync", lambda e, cc=cc, qc=qc: e.dma_start(
                        dbg_xn2[cc * 128:(cc + 1) * 128, qc:qc + QT], xn2[cc]),
                        reads=xn2_pg[cc], writes=[("dxn", q, cc)], tag="dxn", chan="st")
                s.dma("sync", lambda e, qc=qc: e.dma_start(
                    dbg_lgt[:, qc:qc + QT], lgt), reads=LGP,
                    writes=[("dlg", q)], tag="dlg", chan="st")
                s.dma("sync", lambda e, qc=qc: e.dma_start(
                    dbg_mk8[:, qc:qc + QT], mk8), reads=MK8P,
                    writes=[("dmk", q)], tag="dmk", chan="st")
            wpar = [0]
            for ei in range(NEXP):
                mre = bf(MRE_C, 1024, 0, 1)
                s.dma("sync", lambda e, ei=ei, mre=mre: e.dma_start(mre, mk8[ei:ei + 1, :]),
                      reads=MK8P, writes=bpg(MRE_C, 1024), tag="mre", chan="rt", serial=True)
                for tp in range(2):
                    ub, ures = u_bank()
                    s.pe(lambda e, ub=ub, mre=mre, tp=tp: e.matmul(
                            ub, onesrow_b, mre[0:1, tp * 512:(tp + 1) * 512],
                            start=True, stop=True),
                         reads=bpg(MRE_C, 1024) + bpg(CON_C, 512), writes=ures,
                         tag="mbc")
                    s.act(lambda e, ub=ub, tp=tp: e.copy(bf(MKB_C + tp * 512, 512), ub),
                          reads=ures, writes=bpg(MKB_C + tp * 512, 512), tag="mbev")
                maskb = bf(MKB_C, 1024)
                MKBP = bpg(MKB_C, 1024)
                for si in range(4):
                    par = wpar[0] % 2
                    wpar[0] += 1
                    w1v = bf(W1S_C + par * 8192, 8192)
                    w2v = bf(W2S_C + par * 8192, 8192)
                    W1P = bpg(W1S_C + par * 8192, 8192)
                    W2P = bpg(W2S_C + par * 8192, 8192)
                    src1 = w1t[ei].rearrange("(cc p) f -> p cc f", p=128)[:, :, si * 1024:(si + 1) * 1024]
                    s.dma("pool", lambda e, w1v=w1v, src1=src1: e.dma_start(
                        w1v.rearrange("p (cc f) -> p cc f", f=1024), src1),
                        writes=W1P, tag="w1ld", chan="w")
                    src2 = w2t[ei][si * 1024:(si + 1) * 1024, :].rearrange(
                        "(fc p) d -> p fc d", p=128)
                    s.dma("pool", lambda e, w2v=w2v, src2=src2: e.dma_start(
                        w2v.rearrange("p (fc d) -> p fc d", d=1024), src2),
                        writes=W2P, tag="w2ld", chan="w")
                    b1v = f32(B1S_C + par * 512, 8)
                    B1P = fpg(B1S_C + par * 512, 512)
                    s.dma("sync", lambda e, b1v=b1v, ei=ei, si=si: e.dma_start(
                        b1v, b1c[ei][:, si * 8:(si + 1) * 8]),
                        writes=B1P, tag="b1ld", chan="b1", serial=True)
                    for tp in range(2):
                        tps = slice(tp * 512, (tp + 1) * 512)
                        for fc in range(8):
                            zb, zres = rot_bank("z")
                            for cc in range(NCH):
                                s.pe(lambda e, zb=zb, w1v=w1v, cc=cc, fc=fc, tps=tps:
                                     e.matmul(zb, w1v[:, cc * 1024 + fc * 128:
                                                      cc * 1024 + (fc + 1) * 128],
                                              xn2[cc][:, tps],
                                              start=(cc == 0), stop=(cc == 7)),
                                     reads=W1P + xn2_pg[cc], writes=zres, tag="z")
                            h0v = bf(H0_C + (fc % 2) * 512, 512)
                            s.act(lambda e, h0v=h0v, zb=zb, b1v=b1v, fc=fc:
                                  e.activation(h0v, zb, AF.Relu, bias=b1v[:, fc:fc + 1]),
                                  reads=zres + B1P,
                                  writes=bpg(H0_C + (fc % 2) * 512, 512), tag="relu")
                            hv = bf(H_C + fc * 512, 512)
                            s.dve(lambda e, hv=hv, h0v=h0v, tp=tp: e.tensor_tensor(
                                    hv, h0v, maskb[:, tp * 512:(tp + 1) * 512], ALU.mult),
                                  reads=bpg(H0_C + (fc % 2) * 512, 512) + MKBP,
                                  writes=bpg(H_C + fc * 512, 512), tag="hm")
                        for do in range(NCH):
                            yb, yres = rot_bank("y")
                            for fc in range(8):
                                s.pe(lambda e, yb=yb, w2v=w2v, fc=fc, do=do:
                                     e.matmul(yb, w2v[:, fc * 1024 + do * 128:
                                                      fc * 1024 + (do + 1) * 128],
                                              bf(H_C + fc * 512, 512),
                                              start=(fc == 0), stop=(fc == 7)),
                                     reads=W2P + bpg(H_C + fc * 512, 512),
                                     writes=yres, tag="y")
                            dst = f32(XA_C + do * 1024 + tp * 512, 512)
                            s.dve(lambda e, dst=dst, yb=yb: e.tensor_tensor(
                                    dst, yb, dst, ALU.add),
                                  reads=yres + fpg(XA_C + do * 1024 + tp * 512, 512),
                                  writes=fpg(XA_C + do * 1024 + tp * 512, 512),
                                  tag="yacc")
            for tp in range(2):
                for do in range(NCH):
                    ub, ures = u_bank()
                    s.pe(lambda e, ub=ub, do=do, tp=tp: e.matmul(
                            ub, bf(B2SB_C + do * 128, 128, 0, 8),
                            mk8[:, tp * 512:(tp + 1) * 512], start=True, stop=True),
                         reads=bpg(B2SB_C, 1024) + MK8P, writes=ures, tag="b2")
                    dst = f32(XA_C + do * 1024 + tp * 512, 512)
                    s.dve(lambda e, dst=dst, ub=ub: e.tensor_tensor(dst, ub, dst, ALU.add),
                          reads=ures + fpg(XA_C + do * 1024 + tp * 512, 512),
                          writes=fpg(XA_C + do * 1024 + tp * 512, 512), tag="b2acc")
            for do in range(NCH):
                s.dma("sync", lambda e, do=do, qc=qc: e.dma_start(
                    outT[do * 128:(do + 1) * 128, qc:qc + QT], xacc[do]),
                    reads=xacc_pg[do], writes=[("out", q, do)], tag="st", chan="st")

        fin_reads = [("out", q, do) for q in range(NQ) for do in range(NCH)]
        if mode == "dbg":
            fin_reads += [("dxa", q, cc) for q in range(NQ) for cc in range(NCH)]
            fin_reads += [("dxn", q, cc) for q in range(NQ) for cc in range(NCH)]
            fin_reads += [("dlg", q) for q in range(NQ)]
            fin_reads += [("dmk", q) for q in range(NQ)]
        s.op("sync", lambda e: e.nop(), reads=fin_reads, tag="fin")

        emitters, stats = s.emit(nc, es)
        blk = es.enter_context(nc.Block())
        blk.tensor(emitters["pe"])
        blk.scalar(emitters["act"])
        blk.vector(emitters["dve"])
        blk.gpsimd(emitters["pool"])
        blk.sync(emitters["sync"])
    return nc, stats


def make_host_inputs(L0_attn_batch, L0_attn_w_query, L0_attn_w_key,
                     L0_attn_w_value, L0_attn_out_w, L0_attn_out_b,
                     L0_ffn_Wr, L0_ffn_br, L0_ffn_W1, L0_ffn_b1,
                     L0_ffn_W2, L0_ffn_b2):
    bf16 = ml_dtypes.bfloat16
    x = np.asarray(L0_attn_batch, dtype=np.float32)
    shared = {
        "wq": np.ascontiguousarray(np.asarray(L0_attn_w_query, np.float32)),
        "wk": np.ascontiguousarray(np.asarray(L0_attn_w_key, np.float32)),
        "wv": np.ascontiguousarray(np.asarray(L0_attn_w_value, np.float32)),
        "wo": np.ascontiguousarray(np.asarray(L0_attn_out_w, np.float32)),
        "outb": np.ascontiguousarray(
            np.asarray(L0_attn_out_b, np.float32).reshape(8, 128).T),
        "wrt": np.ascontiguousarray(np.asarray(L0_ffn_Wr, np.float32).T),
        "br8": np.ascontiguousarray(np.asarray(L0_ffn_br, np.float32).reshape(8, 1)),
        "w1t": np.ascontiguousarray(
            np.asarray(L0_ffn_W1, np.float32).transpose(0, 2, 1)),
        "b1c": np.ascontiguousarray(
            np.asarray(L0_ffn_b1, np.float32).reshape(8, 32, 128).transpose(0, 2, 1)),
        "w2t": np.ascontiguousarray(
            np.asarray(L0_ffn_W2, np.float32).transpose(0, 2, 1)),
        "b2e": np.ascontiguousarray(np.asarray(L0_ffn_b2, np.float32)),
    }
    k_idx = np.arange(128)[:, None]
    q_idx = np.arange(512)[None, :]
    tri = np.zeros((128, 2048), dtype=bf16)
    for j in range(4):
        tri[:, j * 512:(j + 1) * 512] = ((128 * j + k_idx) <= q_idx).astype(bf16)
    shared["tri"] = tri
    shared["onesb"] = np.ones((128, 2), dtype=bf16)
    shared["onesr"] = np.ones((1, 128), dtype=bf16)
    in_maps = []
    for i in range(8):
        m = dict(shared)
        m["xT"] = np.ascontiguousarray(x[i].T)
        in_maps.append(m)
    return in_maps


_CACHE = {}


def kernel(**inputs):
    in_maps = make_host_inputs(**inputs)
    if "nc" not in _CACHE:
        _CACHE["nc"] = build_program("full")[0]
    nc = _CACHE["nc"]
    res = run_bass_kernel_spmd(nc, in_maps, core_ids=list(range(8)))
    out = np.stack([np.asarray(r["outT"], dtype=np.float32).T
                    for r in res.results])
    return np.ascontiguousarray(out)


# version 10
# speedup vs baseline: 1.3001x; 1.0450x over previous
import numpy as np
import ml_dtypes
import concourse.bass as bass
import concourse.mybir as mybir
from concourse.bass_utils import run_bass_kernel_spmd
from contextlib import ExitStack

F32 = mybir.dt.float32
BF16 = mybir.dt.bfloat16
AF = mybir.ActivationFunctionType
ALU = mybir.AluOpType

STREAMS = ("pe", "act", "dve", "pool", "sync")


class _Op:
    __slots__ = ("idx", "stream", "ctr", "fn", "reads", "writes", "deps",
                 "waits", "inc_val", "tag")

    def __init__(self, idx, stream, ctr, fn, reads, writes, tag):
        self.idx = idx
        self.stream = stream
        self.ctr = ctr
        self.fn = fn
        self.reads = reads
        self.writes = writes
        self.deps = []
        self.waits = []
        self.inc_val = None
        self.tag = tag


class Sched:
    def __init__(self):
        self.ops = []
        self.last_write = {}
        self.readers = {}
        self.last_on_ctr = {}

    def op(self, stream, fn, reads=(), writes=(), dma=False, tag="",
           chan="d", serial=False):
        ctr = (stream + ":" + chan) if dma else stream
        idx = len(self.ops)
        o = _Op(idx, stream, ctr, fn, tuple(reads), tuple(writes), tag)
        deps = set()
        for r in o.reads:
            j = self.last_write.get(r)
            if j is not None:
                deps.add(j)
        for w in o.writes:
            j = self.last_write.get(w)
            if j is not None:
                deps.add(j)
            for rd in self.readers.get(w, ()):
                deps.add(rd)
        if dma and serial:
            p = self.last_on_ctr.get(ctr)
            if p is not None:
                deps.add(p)
        deps.discard(idx)
        o.deps = sorted(deps)
        self.ops.append(o)
        for r in o.reads:
            self.readers.setdefault(r, []).append(idx)
        for w in o.writes:
            self.last_write[w] = idx
            self.readers[w] = []
        self.last_on_ctr[ctr] = idx
        return idx

    def pe(self, fn, reads=(), writes=(), tag=""):
        return self.op("pe", fn, reads, writes, tag=tag)

    def act(self, fn, reads=(), writes=(), tag=""):
        return self.op("act", fn, reads, writes, tag=tag)

    def dve(self, fn, reads=(), writes=(), tag=""):
        return self.op("dve", fn, reads, writes, tag=tag)

    def pool(self, fn, reads=(), writes=(), tag=""):
        return self.op("pool", fn, reads, writes, tag=tag)

    def dma(self, queue, fn, reads=(), writes=(), tag="", chan="d",
            serial=False):
        return self.op(queue, fn, reads, writes, dma=True, tag=tag,
                       chan=chan, serial=serial)

    def _resolve(self):
        n = len(self.ops)
        ctrs = sorted({o.ctr for o in self.ops})
        cidx = {c: i for i, c in enumerate(ctrs)}
        nc_ = len(ctrs)
        seq = [0] * n
        cnt = {c: 0 for c in ctrs}
        for o in self.ops:
            cnt[o.ctr] += 1
            seq[o.idx] = cnt[o.ctr]
        ZERO = tuple([0] * nc_)

        def vmax(a, b):
            return tuple(x if x >= y else y for x, y in zip(a, b))

        start_know = [ZERO] * n
        complete = [ZERO] * n
        prev_on_stream = {}
        for o in self.ops:
            i = o.idx
            know = ZERO
            p = prev_on_stream.get(o.stream)
            if p is not None:
                po = self.ops[p]
                know = start_know[p] if (":" in po.ctr) else complete[p]
            for d in o.deps:
                if know[cidx[self.ops[d].ctr]] >= seq[d]:
                    continue
                do = self.ops[d]
                if do.stream == o.stream and not (":" in do.ctr) \
                        and not (":" in o.ctr):
                    know = vmax(know, complete[d])
                    continue
                o.waits.append(d)
                know = vmax(know, complete[d])
            start_know[i] = know
            me = list(know)
            j = cidx[o.ctr]
            if seq[o.idx] > me[j]:
                me[j] = seq[o.idx]
            complete[i] = tuple(me)
            prev_on_stream[o.stream] = i
        marked = set()
        for o in self.ops:
            if (":" in o.ctr):
                marked.add(o.idx)
            for d in o.waits:
                marked.add(d)
        val = {}
        mcnt = {c: 0 for c in ctrs}
        for o in self.ops:
            if o.idx in marked:
                mcnt[o.ctr] += 1
                step = 16 if (":" in o.ctr) else 1
                o.inc_val = mcnt[o.ctr] * step
                val[o.idx] = o.inc_val
        for o in self.ops:
            best = {}
            for d in o.waits:
                c, v = self.ops[d].ctr, val[d]
                if best.get(c, -1) < v:
                    best[c] = v
            o.waits = sorted(best.items())
        return ctrs

    def emit(self, nc, es):
        ctrs = self._resolve()
        sems = {c: es.enter_context(nc.semaphore("sem_" + c)) for c in ctrs}
        by_stream = {s: [] for s in STREAMS}
        for o in self.ops:
            by_stream[o.stream].append(o)
        stats = {
            "n_ops": len(self.ops),
            "n_waits": sum(len(o.waits) for o in self.ops),
            "n_incs": sum(1 for o in self.ops if o.inc_val is not None),
        }

        def make_emitter(stream):
            def emitter(eng):
                for o in by_stream[stream]:
                    for c, v in o.waits:
                        eng.wait_ge(sems[c], v)
                    inst = o.fn(eng)
                    if o.inc_val is not None:
                        if inst is None:
                            raise RuntimeError(
                                f"op {o.idx} ({o.tag}) needs inc, fn gave None")
                        inst.then_inc(sems[o.ctr],
                                      16 if (":" in o.ctr) else 1)
            return emitter

        return {st: make_emitter(st) for st in STREAMS}, stats


S = 4096
D = 1024
NCH = 8
NQ = 4
QT = S // NQ
DFF = 4096
NEXP = 8

TRI_C = 0
B2SB_C = 2048
WRT_C = 3072
CON_C = 3584
BASE_A = 4096
XN_C = BASE_A
QTB_C = XN_C + 8 * 4096
KTB_C = QTB_C + 4096
O2P_C = KTB_C + 4096
V2A_C = O2P_C + 4096
PB_C = V2A_C + 4224
PBB_C = 61568
WQS_C = PB_C + 2048
WKS_C = WQS_C + 2048
WVS_C = WKS_C + 2048
TOP_A = WVS_C + 2048
XN2_C = BASE_A
W1S_C = XN2_C + 8 * 1024
W2S_C = W1S_C + 16384
H_C = W2S_C + 16384
H0_C = H_C + 4096
O2H_C = H0_C + 1024
WOS_C = O2H_C + 4096
MKB_C = WOS_C + 8192
MK8_C = MKB_C + 1024
MRE_C = MK8_C + 1024
TOP_B = MRE_C + 1024
XB_C = TOP_B
XQ_C = XB_C + 1024
NB = XQ_C + 1024

CP_C = 0
OUTB_C = 16
BR8_C = 24
ONF_C = 128
ROW_C = 512
XF_C = ROW_C + 4 * 512
T0_C = XF_C + 8 * 512
XO_C = T0_C + 1024
LGT_C = XO_C + 1024
TMA_C = T0_C
TMB_C = XO_C
XA_C = LGT_C + 1024
B1S_C = XA_C + 8 * 1024
NF = B1S_C + 1024


def _pages(arena, c0, c1):
    return [(arena, p) for p in range(c0 // 512, (c1 + 511) // 512)]


def build_program(mode="full"):
    nc = bass.Bass("TRN2", target_bir_lowering=False, debug=False)
    xT = nc.dram_tensor("xT", [D, S], F32, kind="ExternalInput").ap()
    wq = nc.dram_tensor("wq", [D, D], F32, kind="ExternalInput").ap()
    wk = nc.dram_tensor("wk", [D, D], F32, kind="ExternalInput").ap()
    wv = nc.dram_tensor("wv", [D, D], F32, kind="ExternalInput").ap()
    wo = nc.dram_tensor("wo", [D, D], F32, kind="ExternalInput").ap()
    outb = nc.dram_tensor("outb", [128, 8], F32, kind="ExternalInput").ap()
    wrt = nc.dram_tensor("wrt", [D, NEXP], F32, kind="ExternalInput").ap()
    br8 = nc.dram_tensor("br8", [NEXP, 1], F32, kind="ExternalInput").ap()
    w1t = nc.dram_tensor("w1t", [NEXP, D, DFF], F32, kind="ExternalInput").ap()
    b1c = nc.dram_tensor("b1c", [NEXP, 128, 32], F32, kind="ExternalInput").ap()
    w2t = nc.dram_tensor("w2t", [NEXP, DFF, D], F32, kind="ExternalInput").ap()
    b2e = nc.dram_tensor("b2e", [NEXP, D], F32, kind="ExternalInput").ap()
    tri = nc.dram_tensor("tri", [128, 2048], BF16, kind="ExternalInput").ap()
    onesb = nc.dram_tensor("onesb", [128, 2], BF16, kind="ExternalInput").ap()
    o2s = nc.dram_tensor("o2s", [D, S], BF16, kind="Internal").ap()
    outT = nc.dram_tensor("outT", [D, S], F32, kind="ExternalOutput").ap()
    if mode == "dbg":
        dbg_xac = nc.dram_tensor("dbg_xac", [D, S], F32, kind="ExternalOutput").ap()
        dbg_xn2 = nc.dram_tensor("dbg_xn2", [D, S], BF16, kind="ExternalOutput").ap()
        dbg_lgt = nc.dram_tensor("dbg_lgt", [8, S], F32, kind="ExternalOutput").ap()
        dbg_mk8 = nc.dram_tensor("dbg_mk8", [8, S], BF16, kind="ExternalOutput").ap()

    s = Sched()
    with ExitStack() as es:
        AB = es.enter_context(nc.sbuf_tensor("AB", [128, NB], BF16))
        AF32 = es.enter_context(nc.sbuf_tensor("AF32", [128, NF], F32))
        scoreP = es.enter_context(nc.psum_tensor("scoreP", [128, 2048], F32))
        pA = es.enter_context(nc.psum_tensor("pA", [128, 512], F32))
        pB = es.enter_context(nc.psum_tensor("pB", [128, 512], F32))
        pC = es.enter_context(nc.psum_tensor("pC", [128, 512], F32))
        pD = es.enter_context(nc.psum_tensor("pD", [128, 512], F32))
        def ps_res(t, off=0):
            if t is scoreP:
                return [("PS", off // 512)]
            return [("PS", {id(pA): 4, id(pB): 5, id(pC): 6, id(pD): 7}[id(t)])]

        def bf(c0, w, r0=0, rn=128):
            return AB[r0:rn, c0:c0 + w]

        def f32(c0, w, r0=0, rn=128):
            return AF32[r0:rn, c0:c0 + w]

        def bpg(c0, w):
            return _pages("B", c0, c0 + w)

        def fpg(c0, w):
            return _pages("F", c0, c0 + w)

        s.dma("sync", lambda e: e.dma_start(bf(TRI_C, 2048), tri),
              writes=bpg(TRI_C, 2048), tag="tri", chan="ini", serial=True)
        s.dma("sync", lambda e: e.dma_start(bf(CON_C, 2, 0, 128), onesb),
              writes=bpg(CON_C, 512), tag="ones", chan="ini", serial=True)
        s.dma("sync", lambda e: e.dma_start(f32(OUTB_C, 8), outb),
              writes=fpg(0, 512), tag="outb", chan="ini", serial=True)
        s.dma("sync", lambda e: e.dma_start(f32(BR8_C, 1, 0, 8), br8),
              writes=fpg(0, 512), tag="br8", chan="ini", serial=True)
        s.dma("pool", lambda e: e.dma_start(bf(B2SB_C, 1024, 0, 8), b2e),
              writes=bpg(B2SB_C, 1024), tag="b2", chan="ini", serial=True)
        s.dma("pool", lambda e: e.dma_start(
            bf(WRT_C, 64).rearrange("p (cc e) -> p cc e", e=8),
            wrt.rearrange("(cc p) e -> p cc e", p=128)),
              writes=bpg(WRT_C, 512), tag="wrt", chan="ini", serial=True)
        s.dve(lambda e: e.memset(f32(CP_C, 1), 0.125), writes=fpg(0, 512))
        s.dve(lambda e: e.memset(f32(CP_C + 1, 1), -12.0), writes=fpg(0, 512))
        s.dve(lambda e: e.memset(f32(CP_C + 2, 1), 1e-5), writes=fpg(0, 512))
        s.dve(lambda e: e.memset(f32(CP_C + 3, 1), 1.0 / 1024.0), writes=fpg(0, 512))
        s.dve(lambda e: e.memset(f32(CP_C + 4, 1), -1.0), writes=fpg(0, 512))
        s.dve(lambda e: e.memset(f32(ONF_C, 128, 0, 1), 1.0), writes=fpg(ONF_C, 128))
        CPQ = fpg(0, 512)
        qscale = f32(CP_C, 1)
        ebias = f32(CP_C + 1, 1)
        lneps = f32(CP_C + 2, 1, 0, 1)
        inv1024 = f32(CP_C + 3, 1, 0, 1)
        neg1 = f32(CP_C + 4, 1, 0, 1)
        onesf = f32(ONF_C, 128, 0, 1)
        onescol_b = bf(CON_C, 1, 0, 128)
        onesr = nc.dram_tensor("onesr", [1, 128], BF16, kind="ExternalInput").ap()
        s.dma("sync", lambda e: e.dma_start(bf(CON_C + 2, 128, 0, 1), onesr),
              writes=bpg(CON_C, 512), tag="onesr", chan="ini", serial=True)
        onesrow_b = bf(CON_C + 2, 128, 0, 1)

        for tt in range(8):
            ts = slice(tt * 512, (tt + 1) * 512)
            tpar = tt % 2
            S1 = scoreP[0:1, tpar * 1024:tpar * 1024 + 512]
            S2 = scoreP[0:1, tpar * 1024 + 512:tpar * 1024 + 1024]
            S1R = [("PS", 2 * tpar)]
            S2R = [("PS", 2 * tpar + 1)]
            bA = pA if tpar == 0 else pC
            bB = pB if tpar == 0 else pD
            for cc in range(NCH):
                sl = cc % 2
                xfc = XF_C + cc * 512
                xfv = f32(xfc, 512)
                s.dma("sync", lambda e, cc=cc, ts=ts, xfv=xfv: e.dma_start(
                    xfv, xT[cc * 128:(cc + 1) * 128, ts]),
                    writes=fpg(xfc, 512), tag="lnx", chan="xs")
                xbv = bf(XB_C + sl * 512, 512)
                s.pool(lambda e, xfv=xfv, xbv=xbv: e.tensor_copy(xbv, xfv),
                       reads=fpg(xfc, 512),
                       writes=bpg(XB_C + sl * 512, 512), tag="lncast")
                xqv = bf(XQ_C + sl * 512, 512)
                s.act(lambda e, xbv=xbv, xqv=xqv: e.activation(xqv, xbv, AF.Square),
                      reads=bpg(XB_C + sl * 512, 512),
                      writes=bpg(XQ_C + sl * 512, 512), tag="lnsq")
                s.pe(lambda e, xbv=xbv, cc=cc, S1=S1: e.matmul(
                        S1, onescol_b, xbv, start=(cc == 0), stop=(cc == 7)),
                     reads=bpg(XB_C + sl * 512, 512) + bpg(CON_C, 512),
                     writes=S1R, tag="lns1")
                s.pe(lambda e, xqv=xqv, cc=cc, S2=S2: e.matmul(
                        S2, onescol_b, xqv, start=(cc == 0), stop=(cc == 7)),
                     reads=bpg(XQ_C + sl * 512, 512) + bpg(CON_C, 512),
                     writes=S2R, tag="lns2")
            r = [f32(ROW_C + i * 512, 512, 0, 1) for i in range(4)]
            RP = [fpg(ROW_C + i * 512, 512) for i in range(4)]
            s.dve(lambda e, r=r, S1=S1: e.tensor_scalar(r[0], S1, inv1024, None, ALU.mult),
                  reads=S1R + CPQ, writes=RP[0], tag="lnm")
            s.dve(lambda e, r=r, S2=S2: e.tensor_scalar(r[1], S2, inv1024, None, ALU.mult),
                  reads=S2R + CPQ, writes=RP[1], tag="lnq")
            s.dve(lambda e, r=r: e.tensor_tensor(r[2], r[0], r[0], ALU.mult),
                  reads=RP[0], writes=RP[2], tag="lnm2")
            s.dve(lambda e, r=r: e.tensor_tensor(r[1], r[1], r[2], ALU.subtract),
                  reads=RP[1] + RP[2], writes=RP[1], tag="lnvar")
            s.act(lambda e, r=r: e.activation(r[2], r[1], AF.Sqrt, bias=lneps),
                  reads=RP[1] + CPQ, writes=RP[2], tag="lnsd")
            s.dve(lambda e, r=r: e.reciprocal(r[3], r[2]),
                  reads=RP[2], writes=RP[3], tag="lnrstd")
            s.dve(lambda e, r=r: e.scalar_tensor_tensor(r[2], r[0], neg1, r[3],
                                                        ALU.mult, ALU.mult),
                  reads=RP[0] + RP[3] + CPQ, writes=RP[2], tag="lnb")
            s.pe(lambda e, r=r, bA=bA: e.matmul(bA[:, :], onesf, r[3], start=True, stop=True),
                 reads=RP[3] + fpg(ONF_C, 128), writes=ps_res(bA), tag="lnbca")
            s.pe(lambda e, r=r, bB=bB: e.matmul(bB[:, :], onesf, r[2], start=True, stop=True),
                 reads=RP[2] + fpg(ONF_C, 128), writes=ps_res(bB), tag="lnbcb")
            for cc in range(NCH):
                xfc = XF_C + cc * 512
                xfv = f32(xfc, 512)
                t0v = f32(T0_C + (cc % 2) * 512, 512)
                s.dve(lambda e, xfv=xfv, t0v=t0v, bA=bA: e.tensor_tensor(t0v, xfv, bA[:, :], ALU.mult),
                      reads=fpg(xfc, 512) + ps_res(bA),
                      writes=fpg(T0_C + (cc % 2) * 512, 512), tag="lnn1")
                xnv = bf(XN_C + cc * 4096 + tt * 512, 512)
                s.dve(lambda e, xnv=xnv, t0v=t0v, bB=bB: e.tensor_tensor(xnv, t0v, bB[:, :], ALU.add),
                      reads=fpg(T0_C + (cc % 2) * 512, 512) + ps_res(bB),
                      writes=bpg(XN_C + cc * 4096 + tt * 512, 512), tag="lnn2")

        xn = [bf(XN_C + cc * 4096, 4096) for cc in range(NCH)]
        xn_pg = [bpg(XN_C + cc * 4096, 4096) for cc in range(NCH)]
        v2a_full = bf(V2A_C, 4160)
        s.dve(lambda e: e.memset(
            v2a_full.rearrange("p (tc w) -> p tc w", w=130)[:, :, 64:65], 1.0),
            writes=bpg(V2A_C, 4160), tag="v2ones1")
        s.dve(lambda e: e.memset(
            v2a_full.rearrange("p (tc w) -> p tc w", w=130)[:, :, 129:130], 1.0),
            writes=bpg(V2A_C, 4160), tag="v2ones2")

        evrot = [0]

        def evac_bank():
            b = evrot[0] % 4
            evrot[0] += 1
            return scoreP[:, b * 512:(b + 1) * 512], [("PS", b)]

        for hp in range(8):
            par = hp % 2
            wq_s = bf(WQS_C + par * 1024, 1024)
            wk_s = bf(WKS_C + par * 1024, 1024)
            wv_s = bf(WVS_C + par * 1024, 1024)
            wq_pg = bpg(WQS_C + par * 1024, 1024)
            wk_pg = bpg(WKS_C + par * 1024, 1024)
            wv_pg = bpg(WVS_C + par * 1024, 1024)
            for (w_s, w_pg, wsrc, nm) in ((wq_s, wq_pg, wq, "wq"),
                                          (wk_s, wk_pg, wk, "wk"),
                                          (wv_s, wv_pg, wv, "wv")):
                src = wsrc.rearrange("(cc p) d -> p cc d", p=128)[:, :, hp * 128:(hp + 1) * 128]
                s.dma("pool", lambda e, w_s=w_s, src=src: e.dma_start(
                    w_s.rearrange("p (cc d) -> p cc d", d=128), src),
                    writes=w_pg, tag=nm, chan="wqv")
            qtb = bf(QTB_C, 4096)
            ktb = bf(KTB_C, 4096)
            for (dst_c, w_s, w_pg, sc, nm) in ((QTB_C, wq_s, wq_pg, qscale, "qt"),
                                               (KTB_C, wk_s, wk_pg, None, "kt")):
                for tt in range(8):
                    ts = slice(tt * 512, (tt + 1) * 512)
                    bank, bres = evac_bank()
                    for cc in range(NCH):
                        s.pe(lambda e, bank=bank, w_s=w_s, cc=cc, ts=ts: e.matmul(
                                bank, w_s[:, cc * 128:(cc + 1) * 128], xn[cc][:, ts],
                                start=(cc == 0), stop=(cc == 7)),
                             reads=w_pg + xn_pg[cc], writes=bres, tag=nm)
                    dst = bf(dst_c + tt * 512, 512)
                    if sc is None:
                        s.act(lambda e, dst=dst, bank=bank: e.copy(dst, bank),
                              reads=bres, writes=bpg(dst_c + tt * 512, 512), tag=nm + "ev")
                    else:
                        s.act(lambda e, dst=dst, bank=bank, sc=sc: e.activation(
                                dst, bank, AF.Copy, scale=sc),
                              reads=bres + CPQ, writes=bpg(dst_c + tt * 512, 512),
                              tag=nm + "ev")
            for tc in range(32):
                bank = (pC, ps_res(pC)) if tc % 2 == 0 else (pD, ps_res(pD))
                bk, bres = bank
                for cc in range(NCH):
                    s.pe(lambda e, bk=bk, cc=cc, tc=tc, wv_s=wv_s: e.matmul(
                            bk[:, 0:128], xn[cc][:, tc * 128:(tc + 1) * 128],
                            wv_s[:, cc * 128:(cc + 1) * 128],
                            start=(cc == 0), stop=(cc == 7)),
                         reads=wv_pg + xn_pg[cc], writes=bres, tag="v")
                va = bf(V2A_C + tc * 130, 64)
                vb = bf(V2A_C + tc * 130 + 65, 64)
                s.dve(lambda e, va=va, bk=bk: e.tensor_copy(va, bk[:, 0:64]),
                      reads=bres, writes=bpg(V2A_C + tc * 130, 64), tag="vev")
                s.dve(lambda e, vb=vb, bk=bk: e.tensor_copy(vb, bk[:, 64:128]),
                      reads=bres, writes=bpg(V2A_C + tc * 130 + 65, 64), tag="vev")
            PBC = [[PB_C, PBB_C], [PB_C + 1024, PBB_C + 1024]]
            SLOTC = [PB_C, PBB_C]

            def emit_tail_recips(qt_i):
                for h in range(2):
                    obank = pA if h == 0 else pB
                    r0 = bf(63616 + h * 512, 512, 0, 1)

                    def recip_fn(e, obank=obank, r0=r0):
                        with nc.allow_low_precision(
                                reason="bf16 softmax normalizer broadcast"):
                            return e.reciprocal(r0, obank[64:65, :])
                    s.dve(recip_fn,
                          reads=ps_res(obank), writes=bpg(63616 + h * 512, 512),
                          tag="recip")

            def emit_tail_rest(qt_i):
                for h in range(2):
                    hb = 64 * h
                    obank = pA if h == 0 else pB
                    ob_res = ps_res(obank)
                    r0 = bf(63616 + h * 512, 512, 0, 1)
                    rb = pC if h == 0 else pD
                    s.pe(lambda e, r0=r0, rb=rb: e.matmul(
                            rb[0:64, :], onesrow_b[0:1, 0:64], r0,
                            start=True, stop=True),
                         reads=bpg(63616 + h * 512, 512) + bpg(CON_C, 512),
                         writes=ps_res(rb), tag="rbc")
                    dst = bf(O2P_C + qt_i * 512, 512, hb, hb + 64)
                    s.act(lambda e, dst=dst, obank=obank: e.copy(
                            dst, obank[0:64, :]),
                          reads=ob_res, writes=bpg(O2P_C + qt_i * 512, 512),
                          tag="oev")
                    s.dve(lambda e, dst=dst, rb=rb: e.tensor_tensor(
                            dst, dst, rb[0:64, :], ALU.mult),
                          reads=bpg(O2P_C + qt_i * 512, 512) + ps_res(rb),
                          writes=bpg(O2P_C + qt_i * 512, 512), tag="onorm")

            def emit_pv(prev_g, slot, first, last):
                for h in range(2):
                    obank = pA if h == 0 else pB
                    pbc = PBC[h][slot]
                    ppb = bf(pbc, 1024)
                    for j in range(2):
                        kc = 2 * prev_g + j
                        s.pe(lambda e, kc=kc, h=h, ppb=ppb, j=j,
                             st=(first and j == 0), sp=(last and j == 1),
                             obank=obank: e.matmul(
                                obank[0:65, :],
                                v2a_full[:, kc * 130 + 65 * h:
                                         kc * 130 + 65 * h + 65],
                                ppb[:, j * 512:(j + 1) * 512],
                                start=st, stop=sp),
                             reads=bpg(V2A_C + kc * 130, 130) + bpg(pbc, 1024),
                             writes=ps_res(obank), tag="pv")

            pending_tail = None
            for qt_i in range(8):
                qts = slice(qt_i * 512, (qt_i + 1) * 512)
                ng = 2 * (qt_i + 1)
                order = list(range(ng - 2, ng)) + list(range(0, ng - 2))
                if pending_tail is not None:
                    emit_tail_recips(pending_tail)
                prev = None
                npv = 0
                for idx, g in enumerate(order):
                    slot = idx % 2
                    for h in range(2):
                        hb = 64 * h
                        for j in range(2):
                            kc = 2 * g + j
                            bi = 2 * h + j
                            sb = scoreP[:, bi * 512:(bi + 1) * 512]
                            s.pe(lambda e, sb=sb, kc=kc, hb=hb, qts=qts,
                                 ktb=ktb, qtb=qtb: e.matmul(
                                    sb, ktb[hb:hb + 64, kc * 128:(kc + 1) * 128],
                                    qtb[hb:hb + 64, qts], start=True, stop=True),
                                 reads=bpg(KTB_C, 4096) +
                                       bpg(QTB_C + qt_i * 512, 512),
                                 writes=[("PS", bi)], tag="sc")
                    pbig = bf(SLOTC[slot], 2048)
                    s.act(lambda e, pbig=pbig: e.activation(
                            pbig, scoreP[:, 0:2048], AF.Exp, bias=ebias),
                          reads=[("PS", 0), ("PS", 1), ("PS", 2), ("PS", 3)] + CPQ,
                          writes=bpg(SLOTC[slot], 2048), tag="exp")
                    for h in range(2):
                        pbc = PBC[h][slot]
                        pbv = bf(pbc, 1024)
                        if g >= ng - 2:
                            dj = (g - (ng - 2))
                            trv = bf(TRI_C + dj * 1024, 1024)
                            s.dve(lambda e, pbv=pbv, trv=trv: e.tensor_tensor(
                                    pbv, pbv, trv, ALU.mult),
                                  reads=bpg(pbc, 1024) +
                                        bpg(TRI_C + dj * 1024, 1024),
                                  writes=bpg(pbc, 1024), tag="trim")
                    if prev is not None:
                        if pending_tail is not None:
                            emit_tail_rest(pending_tail)
                            pending_tail = None
                        emit_pv(prev[0], prev[1], npv == 0, False)
                        npv += 1
                    prev = (g, slot)
                if pending_tail is not None:
                    emit_tail_rest(pending_tail)
                    pending_tail = None
                emit_pv(prev[0], prev[1], npv == 0, True)
                pending_tail = qt_i
            emit_tail_recips(pending_tail)
            emit_tail_rest(pending_tail)
            pending_tail = None
            s.dma("sync", lambda e, hp=hp: e.dma_start(
                o2s[hp * 128:(hp + 1) * 128, :], bf(O2P_C, 4096)),
                reads=bpg(O2P_C, 4096), writes=[("o2s", hp)], tag="o2st", chan="o2w")

        for hp in range(8):
            s.dma("pool", lambda e, hp=hp: e.dma_start(
                bf(WOS_C + hp * 1024, 1024), wo[hp * 128:(hp + 1) * 128, :]),
                writes=bpg(WOS_C + hp * 1024, 1024), tag="wo", chan="wo")

        brot = [0]

        def rot_bank(group):
            if group == "z":
                b = brot[0] % 2
                brot[0] += 1
                return scoreP[:, b * 512:(b + 1) * 512], [("PS", b)]
            if group == "y":
                b = brot[0] % 2
                brot[0] += 1
                return scoreP[:, (2 + b) * 512:(3 + b) * 512], [("PS", 2 + b)]
            raise ValueError

        urot = [0]

        def u_bank():
            b = urot[0] % 2
            urot[0] += 1
            t = pC if b == 0 else pD
            return t[:, :], ps_res(t)

        xacc = [f32(XA_C + c * 1024, 1024) for c in range(NCH)]
        xacc_pg = [fpg(XA_C + c * 1024, 1024) for c in range(NCH)]
        xn2 = [bf(XN2_C + c * 1024, 1024) for c in range(NCH)]
        xn2_pg = [bpg(XN2_C + c * 1024, 1024) for c in range(NCH)]

        for q in range(NQ):
            qc = q * QT
            for tp in range(2):
                c0 = qc + tp * 512
                o2h = bf(O2H_C, 4096)
                s.dma("sync", lambda e, c0=c0, o2h=o2h: e.dma_start(
                    o2h.rearrange("p (hp t) -> p hp t", t=512),
                    o2s.rearrange("(hp p) t -> p hp t", p=128)[:, :, c0:c0 + 512]),
                    reads=[("o2s", hp) for hp in range(8)],
                    writes=bpg(O2H_C, 4096), tag="o2h", chan="o2h")
                for do in range(NCH):
                    bank, bres = evac_bank()
                    for hp in range(8):
                        s.pe(lambda e, bank=bank, hp=hp, do=do, o2h=o2h: e.matmul(
                                bank, bf(WOS_C + hp * 1024 + do * 128, 128),
                                o2h[:, hp * 512:(hp + 1) * 512],
                                start=(hp == 0), stop=(hp == 7)),
                             reads=bpg(WOS_C + hp * 1024, 1024) + bpg(O2H_C, 4096),
                             writes=bres, tag="wop")
                    xov = f32(XO_C + (do % 2) * 512, 512)
                    s.dma("sync", lambda e, xov=xov, do=do, c0=c0: e.dma_start(
                        xov, xT[do * 128:(do + 1) * 128, c0:c0 + 512]),
                        writes=fpg(XO_C + (do % 2) * 512, 512), tag="xo", chan="xs")
                    dst = f32(XA_C + do * 1024 + tp * 512, 512)
                    s.dve(lambda e, dst=dst, bank=bank, do=do, xov=xov:
                          e.scalar_tensor_tensor(dst, bank, f32(OUTB_C + do, 1),
                                                 xov, ALU.add, ALU.add),
                          reads=bres + fpg(XO_C + (do % 2) * 512, 512) + fpg(0, 512),
                          writes=fpg(XA_C + do * 1024 + tp * 512, 512), tag="b1acc")
            if mode == "attn":
                for do in range(NCH):
                    s.dma("sync", lambda e, do=do, qc=qc: e.dma_start(
                        outT[do * 128:(do + 1) * 128, qc:qc + QT], xacc[do]),
                        reads=xacc_pg[do], writes=[("out", q, do)], tag="st", chan="st")
                continue
            for tp in range(2):
                S1 = scoreP[0:1, tp * 1024:tp * 1024 + 512]
                S2 = scoreP[0:1, tp * 1024 + 512:tp * 1024 + 1024]
                S1R = [("PS", 2 * tp)]
                S2R = [("PS", 2 * tp + 1)]
                bA = pA if tp == 0 else pC
                bB = pB if tp == 0 else pD
                for cc in range(NCH):
                    src = f32(XA_C + cc * 1024 + tp * 512, 512)
                    xbv = bf(XB_C + (cc % 2) * 512, 512)
                    s.pool(lambda e, src=src, xbv=xbv: e.tensor_copy(xbv, src),
                           reads=fpg(XA_C + cc * 1024 + tp * 512, 512),
                           writes=bpg(XB_C + (cc % 2) * 512, 512), tag="l2cast")
                    xqv = bf(XQ_C + (cc % 2) * 512, 512)
                    s.act(lambda e, xbv=xbv, xqv=xqv: e.activation(xqv, xbv, AF.Square),
                          reads=bpg(XB_C + (cc % 2) * 512, 512),
                          writes=bpg(XQ_C + (cc % 2) * 512, 512), tag="l2sq")
                    s.pe(lambda e, xbv=xbv, cc=cc, S1=S1: e.matmul(
                            S1, onescol_b, xbv, start=(cc == 0), stop=(cc == 7)),
                         reads=bpg(XB_C + (cc % 2) * 512, 512) + bpg(CON_C, 512),
                         writes=S1R, tag="l2s1")
                    s.pe(lambda e, xqv=xqv, cc=cc, S2=S2: e.matmul(
                            S2, onescol_b, xqv, start=(cc == 0), stop=(cc == 7)),
                         reads=bpg(XQ_C + (cc % 2) * 512, 512) + bpg(CON_C, 512),
                         writes=S2R, tag="l2s2")
                r = [f32(ROW_C + i * 512, 512, 0, 1) for i in range(4)]
                RP = [fpg(ROW_C + i * 512, 512) for i in range(4)]
                s.dve(lambda e, r=r, S1=S1: e.tensor_scalar(r[0], S1, inv1024, None, ALU.mult),
                      reads=S1R + CPQ, writes=RP[0], tag="l2m")
                s.dve(lambda e, r=r, S2=S2: e.tensor_scalar(r[1], S2, inv1024, None, ALU.mult),
                      reads=S2R + CPQ, writes=RP[1], tag="l2q")
                s.dve(lambda e, r=r: e.tensor_tensor(r[2], r[0], r[0], ALU.mult),
                      reads=RP[0], writes=RP[2], tag="l2m2")
                s.dve(lambda e, r=r: e.tensor_tensor(r[1], r[1], r[2], ALU.subtract),
                      reads=RP[1] + RP[2], writes=RP[1], tag="l2var")
                s.act(lambda e, r=r: e.activation(r[2], r[1], AF.Sqrt, bias=lneps),
                      reads=RP[1] + CPQ, writes=RP[2], tag="l2sd")
                s.dve(lambda e, r=r: e.reciprocal(r[3], r[2]),
                      reads=RP[2], writes=RP[3], tag="l2rstd")
                s.dve(lambda e, r=r: e.scalar_tensor_tensor(r[2], r[0], neg1, r[3],
                                                            ALU.mult, ALU.mult),
                      reads=RP[0] + RP[3] + CPQ, writes=RP[2], tag="l2b")
                s.pe(lambda e, r=r, bA=bA: e.matmul(bA[:, :], onesf, r[3], start=True, stop=True),
                     reads=RP[3] + fpg(ONF_C, 128), writes=ps_res(bA), tag="l2bca")
                s.pe(lambda e, r=r, bB=bB: e.matmul(bB[:, :], onesf, r[2], start=True, stop=True),
                     reads=RP[2] + fpg(ONF_C, 128), writes=ps_res(bB), tag="l2bcb")
                for cc in range(NCH):
                    src = f32(XA_C + cc * 1024 + tp * 512, 512)
                    t0v = f32(T0_C + (cc % 2) * 512, 512)
                    s.dve(lambda e, src=src, t0v=t0v, bA=bA: e.tensor_tensor(t0v, src, bA[:, :], ALU.mult),
                          reads=fpg(XA_C + cc * 1024 + tp * 512, 512) + ps_res(bA),
                          writes=fpg(T0_C + (cc % 2) * 512, 512), tag="l2n1")
                    dst = bf(XN2_C + cc * 1024 + tp * 512, 512)
                    s.dve(lambda e, dst=dst, t0v=t0v, bB=bB: e.tensor_tensor(dst, t0v, bB[:, :], ALU.add),
                          reads=fpg(T0_C + (cc % 2) * 512, 512) + ps_res(bB),
                          writes=bpg(XN2_C + cc * 1024 + tp * 512, 512), tag="l2n2")
            lgt = f32(LGT_C, 1024, 0, 8)
            for tp in range(2):
                for cc in range(NCH):
                    s.pe(lambda e, cc=cc, tp=tp: e.matmul(
                            pC[0:8, :], bf(WRT_C + cc * 8, 8),
                            xn2[cc][:, tp * 512:(tp + 1) * 512],
                            start=(cc == 0), stop=(cc == 7)),
                         reads=bpg(WRT_C, 512) + xn2_pg[cc], writes=ps_res(pC),
                         tag="rt")
                dst = f32(LGT_C + tp * 512, 512, 0, 8)
                s.dve(lambda e, dst=dst: e.tensor_scalar(
                        dst, pC[0:8, :], f32(BR8_C, 1, 0, 8), None, ALU.add),
                      reads=ps_res(pC) + fpg(0, 512),
                      writes=fpg(LGT_C + tp * 512, 512), tag="rtev")
            tA = f32(TMA_C, 1024, 0, 8)
            tB = f32(TMB_C, 1024, 0, 8)
            LGP = fpg(LGT_C, 1024)
            TAP = fpg(TMA_C, 1024)
            TBP = fpg(TMB_C, 1024)
            s.dma("sync", lambda e: e.dma_start(tA[0:4, :], lgt[4:8, :]),
                  reads=LGP, writes=TAP, tag="mt1", chan="rt", serial=True)
            s.dve(lambda e: e.tensor_tensor(tB[0:4, :], lgt[0:4, :], tA[0:4, :], ALU.max),
                  reads=LGP + TAP, writes=TBP, tag="mt2")
            s.dma("sync", lambda e: e.dma_start(tA[0:2, :], tB[2:4, :]),
                  reads=TBP, writes=TAP, tag="mt3", chan="rt", serial=True)
            s.dve(lambda e: e.tensor_tensor(tA[0:2, :], tB[0:2, :], tA[0:2, :], ALU.max),
                  reads=TBP + TAP, writes=TAP, tag="mt4")
            s.dma("sync", lambda e: e.dma_start(tB[0:1, :], tA[1:2, :]),
                  reads=TAP, writes=TBP, tag="mt5", chan="rt", serial=True)
            s.dve(lambda e: e.tensor_tensor(tB[0:1, :], tA[0:1, :], tB[0:1, :], ALU.max),
                  reads=TAP + TBP, writes=TBP, tag="mt6")
            s.dma("sync", lambda e: e.dma_start(tA[0:1, :], tB[0:1, :]),
                  reads=TBP, writes=TAP, tag="mb1", chan="rt", serial=True)
            s.dma("sync", lambda e: e.dma_start(tA[1:2, :], tA[0:1, :]),
                  reads=TAP, writes=TAP, tag="mb2", chan="rt", serial=True)
            s.dma("sync", lambda e: e.dma_start(tA[2:4, :], tA[0:2, :]),
                  reads=TAP, writes=TAP, tag="mb3", chan="rt", serial=True)
            s.dma("sync", lambda e: e.dma_start(tA[4:8, :], tA[0:4, :]),
                  reads=TAP, writes=TAP, tag="mb4", chan="rt", serial=True)
            mk8 = bf(MK8_C, 1024, 0, 8)
            MK8P = bpg(MK8_C, 1024)
            s.dve(lambda e: e.tensor_tensor(mk8, lgt, tA, ALU.is_ge),
                  reads=LGP + TAP, writes=MK8P, tag="mk8")
            if mode == "dbg":
                for cc in range(NCH):
                    s.dma("sync", lambda e, cc=cc, qc=qc: e.dma_start(
                        dbg_xac[cc * 128:(cc + 1) * 128, qc:qc + QT], xacc[cc]),
                        reads=xacc_pg[cc], writes=[("dxa", q, cc)], tag="dxa", chan="st")
                    s.dma("sync", lambda e, cc=cc, qc=qc: e.dma_start(
                        dbg_xn2[cc * 128:(cc + 1) * 128, qc:qc + QT], xn2[cc]),
                        reads=xn2_pg[cc], writes=[("dxn", q, cc)], tag="dxn", chan="st")
                s.dma("sync", lambda e, qc=qc: e.dma_start(
                    dbg_lgt[:, qc:qc + QT], lgt), reads=LGP,
                    writes=[("dlg", q)], tag="dlg", chan="st")
                s.dma("sync", lambda e, qc=qc: e.dma_start(
                    dbg_mk8[:, qc:qc + QT], mk8), reads=MK8P,
                    writes=[("dmk", q)], tag="dmk", chan="st")
            wpar = [0]
            for ei in range(NEXP):
                mre = bf(MRE_C, 1024, 0, 1)
                s.dma("sync", lambda e, ei=ei, mre=mre: e.dma_start(mre, mk8[ei:ei + 1, :]),
                      reads=MK8P, writes=bpg(MRE_C, 1024), tag="mre", chan="rt", serial=True)
                for tp in range(2):
                    ub, ures = u_bank()
                    s.pe(lambda e, ub=ub, mre=mre, tp=tp: e.matmul(
                            ub, onesrow_b, mre[0:1, tp * 512:(tp + 1) * 512],
                            start=True, stop=True),
                         reads=bpg(MRE_C, 1024) + bpg(CON_C, 512), writes=ures,
                         tag="mbc")
                    s.act(lambda e, ub=ub, tp=tp: e.copy(bf(MKB_C + tp * 512, 512), ub),
                          reads=ures, writes=bpg(MKB_C + tp * 512, 512), tag="mbev")
                maskb = bf(MKB_C, 1024)
                MKBP = bpg(MKB_C, 1024)
                for si in range(4):
                    par = wpar[0] % 2
                    wpar[0] += 1
                    w1v = bf(W1S_C + par * 8192, 8192)
                    w2v = bf(W2S_C + par * 8192, 8192)
                    W1P = bpg(W1S_C + par * 8192, 8192)
                    W2P = bpg(W2S_C + par * 8192, 8192)
                    src1 = w1t[ei].rearrange("(cc p) f -> p cc f", p=128)[:, :, si * 1024:(si + 1) * 1024]
                    s.dma("pool", lambda e, w1v=w1v, src1=src1: e.dma_start(
                        w1v.rearrange("p (cc f) -> p cc f", f=1024), src1),
                        writes=W1P, tag="w1ld", chan="w")
                    src2 = w2t[ei][si * 1024:(si + 1) * 1024, :].rearrange(
                        "(fc p) d -> p fc d", p=128)
                    s.dma("pool", lambda e, w2v=w2v, src2=src2: e.dma_start(
                        w2v.rearrange("p (fc d) -> p fc d", d=1024), src2),
                        writes=W2P, tag="w2ld", chan="w")
                    b1v = f32(B1S_C + par * 512, 8)
                    B1P = fpg(B1S_C + par * 512, 512)
                    s.dma("sync", lambda e, b1v=b1v, ei=ei, si=si: e.dma_start(
                        b1v, b1c[ei][:, si * 8:(si + 1) * 8]),
                        writes=B1P, tag="b1ld", chan="b1", serial=True)
                    for tp in range(2):
                        tps = slice(tp * 512, (tp + 1) * 512)
                        for fc in range(8):
                            zb, zres = rot_bank("z")
                            for cc in range(NCH):
                                s.pe(lambda e, zb=zb, w1v=w1v, cc=cc, fc=fc, tps=tps:
                                     e.matmul(zb, w1v[:, cc * 1024 + fc * 128:
                                                      cc * 1024 + (fc + 1) * 128],
                                              xn2[cc][:, tps],
                                              start=(cc == 0), stop=(cc == 7)),
                                     reads=W1P + xn2_pg[cc], writes=zres, tag="z")
                            h0v = bf(H0_C + (fc % 2) * 512, 512)
                            s.act(lambda e, h0v=h0v, zb=zb, b1v=b1v, fc=fc:
                                  e.activation(h0v, zb, AF.Relu, bias=b1v[:, fc:fc + 1]),
                                  reads=zres + B1P,
                                  writes=bpg(H0_C + (fc % 2) * 512, 512), tag="relu")
                            hv = bf(H_C + fc * 512, 512)
                            s.dve(lambda e, hv=hv, h0v=h0v, tp=tp: e.tensor_tensor(
                                    hv, h0v, maskb[:, tp * 512:(tp + 1) * 512], ALU.mult),
                                  reads=bpg(H0_C + (fc % 2) * 512, 512) + MKBP,
                                  writes=bpg(H_C + fc * 512, 512), tag="hm")
                        for do in range(NCH):
                            yb, yres = rot_bank("y")
                            for fc in range(8):
                                s.pe(lambda e, yb=yb, w2v=w2v, fc=fc, do=do:
                                     e.matmul(yb, w2v[:, fc * 1024 + do * 128:
                                                      fc * 1024 + (do + 1) * 128],
                                              bf(H_C + fc * 512, 512),
                                              start=(fc == 0), stop=(fc == 7)),
                                     reads=W2P + bpg(H_C + fc * 512, 512),
                                     writes=yres, tag="y")
                            dst = f32(XA_C + do * 1024 + tp * 512, 512)
                            s.dve(lambda e, dst=dst, yb=yb: e.tensor_tensor(
                                    dst, yb, dst, ALU.add),
                                  reads=yres + fpg(XA_C + do * 1024 + tp * 512, 512),
                                  writes=fpg(XA_C + do * 1024 + tp * 512, 512),
                                  tag="yacc")
            for tp in range(2):
                for do in range(NCH):
                    ub, ures = u_bank()
                    s.pe(lambda e, ub=ub, do=do, tp=tp: e.matmul(
                            ub, bf(B2SB_C + do * 128, 128, 0, 8),
                            mk8[:, tp * 512:(tp + 1) * 512], start=True, stop=True),
                         reads=bpg(B2SB_C, 1024) + MK8P, writes=ures, tag="b2")
                    dst = f32(XA_C + do * 1024 + tp * 512, 512)
                    s.dve(lambda e, dst=dst, ub=ub: e.tensor_tensor(dst, ub, dst, ALU.add),
                          reads=ures + fpg(XA_C + do * 1024 + tp * 512, 512),
                          writes=fpg(XA_C + do * 1024 + tp * 512, 512), tag="b2acc")
            for do in range(NCH):
                s.dma("sync", lambda e, do=do, qc=qc: e.dma_start(
                    outT[do * 128:(do + 1) * 128, qc:qc + QT], xacc[do]),
                    reads=xacc_pg[do], writes=[("out", q, do)], tag="st", chan="st")

        fin_reads = [("out", q, do) for q in range(NQ) for do in range(NCH)]
        if mode == "dbg":
            fin_reads += [("dxa", q, cc) for q in range(NQ) for cc in range(NCH)]
            fin_reads += [("dxn", q, cc) for q in range(NQ) for cc in range(NCH)]
            fin_reads += [("dlg", q) for q in range(NQ)]
            fin_reads += [("dmk", q) for q in range(NQ)]
        s.op("sync", lambda e: e.nop(), reads=fin_reads, tag="fin")

        emitters, stats = s.emit(nc, es)
        blk = es.enter_context(nc.Block())
        blk.tensor(emitters["pe"])
        blk.scalar(emitters["act"])
        blk.vector(emitters["dve"])
        blk.gpsimd(emitters["pool"])
        blk.sync(emitters["sync"])
    return nc, stats


def make_host_inputs(L0_attn_batch, L0_attn_w_query, L0_attn_w_key,
                     L0_attn_w_value, L0_attn_out_w, L0_attn_out_b,
                     L0_ffn_Wr, L0_ffn_br, L0_ffn_W1, L0_ffn_b1,
                     L0_ffn_W2, L0_ffn_b2):
    bf16 = ml_dtypes.bfloat16
    x = np.asarray(L0_attn_batch, dtype=np.float32)
    shared = {
        "wq": np.ascontiguousarray(np.asarray(L0_attn_w_query, np.float32)),
        "wk": np.ascontiguousarray(np.asarray(L0_attn_w_key, np.float32)),
        "wv": np.ascontiguousarray(np.asarray(L0_attn_w_value, np.float32)),
        "wo": np.ascontiguousarray(np.asarray(L0_attn_out_w, np.float32)),
        "outb": np.ascontiguousarray(
            np.asarray(L0_attn_out_b, np.float32).reshape(8, 128).T),
        "wrt": np.ascontiguousarray(np.asarray(L0_ffn_Wr, np.float32).T),
        "br8": np.ascontiguousarray(np.asarray(L0_ffn_br, np.float32).reshape(8, 1)),
        "w1t": np.ascontiguousarray(
            np.asarray(L0_ffn_W1, np.float32).transpose(0, 2, 1)),
        "b1c": np.ascontiguousarray(
            np.asarray(L0_ffn_b1, np.float32).reshape(8, 32, 128).transpose(0, 2, 1)),
        "w2t": np.ascontiguousarray(
            np.asarray(L0_ffn_W2, np.float32).transpose(0, 2, 1)),
        "b2e": np.ascontiguousarray(np.asarray(L0_ffn_b2, np.float32)),
    }
    k_idx = np.arange(128)[:, None]
    q_idx = np.arange(512)[None, :]
    tri = np.zeros((128, 2048), dtype=bf16)
    for j in range(4):
        tri[:, j * 512:(j + 1) * 512] = ((128 * j + k_idx) <= q_idx).astype(bf16)
    shared["tri"] = tri
    shared["onesb"] = np.ones((128, 2), dtype=bf16)
    shared["onesr"] = np.ones((1, 128), dtype=bf16)
    in_maps = []
    for i in range(8):
        m = dict(shared)
        m["xT"] = np.ascontiguousarray(x[i].T)
        in_maps.append(m)
    return in_maps


_CACHE = {}


def kernel(**inputs):
    in_maps = make_host_inputs(**inputs)
    if "nc" not in _CACHE:
        _CACHE["nc"] = build_program("full")[0]
    nc = _CACHE["nc"]
    res = run_bass_kernel_spmd(nc, in_maps, core_ids=list(range(8)))
    out = np.stack([np.asarray(r["outT"], dtype=np.float32).T
                    for r in res.results])
    return np.ascontiguousarray(out)


# version 11 (final)
# speedup vs baseline: 1.3027x; 1.0020x over previous
import numpy as np
import ml_dtypes
import concourse.bass as bass
import concourse.mybir as mybir
from concourse.bass_utils import run_bass_kernel_spmd
from contextlib import ExitStack

F32 = mybir.dt.float32
BF16 = mybir.dt.bfloat16
AF = mybir.ActivationFunctionType
ALU = mybir.AluOpType

STREAMS = ("pe", "act", "dve", "pool", "sync")


class _Op:
    __slots__ = ("idx", "stream", "ctr", "fn", "reads", "writes", "deps",
                 "waits", "inc_val", "tag")

    def __init__(self, idx, stream, ctr, fn, reads, writes, tag):
        self.idx = idx
        self.stream = stream
        self.ctr = ctr
        self.fn = fn
        self.reads = reads
        self.writes = writes
        self.deps = []
        self.waits = []
        self.inc_val = None
        self.tag = tag


class Sched:
    def __init__(self):
        self.ops = []
        self.last_write = {}
        self.readers = {}
        self.last_on_ctr = {}

    def op(self, stream, fn, reads=(), writes=(), dma=False, tag="",
           chan="d", serial=False):
        ctr = (stream + ":" + chan) if dma else stream
        idx = len(self.ops)
        o = _Op(idx, stream, ctr, fn, tuple(reads), tuple(writes), tag)
        deps = set()
        for r in o.reads:
            j = self.last_write.get(r)
            if j is not None:
                deps.add(j)
        for w in o.writes:
            j = self.last_write.get(w)
            if j is not None:
                deps.add(j)
            for rd in self.readers.get(w, ()):
                deps.add(rd)
        if dma and serial:
            p = self.last_on_ctr.get(ctr)
            if p is not None:
                deps.add(p)
        deps.discard(idx)
        o.deps = sorted(deps)
        self.ops.append(o)
        for r in o.reads:
            self.readers.setdefault(r, []).append(idx)
        for w in o.writes:
            self.last_write[w] = idx
            self.readers[w] = []
        self.last_on_ctr[ctr] = idx
        return idx

    def pe(self, fn, reads=(), writes=(), tag=""):
        return self.op("pe", fn, reads, writes, tag=tag)

    def act(self, fn, reads=(), writes=(), tag=""):
        return self.op("act", fn, reads, writes, tag=tag)

    def dve(self, fn, reads=(), writes=(), tag=""):
        return self.op("dve", fn, reads, writes, tag=tag)

    def pool(self, fn, reads=(), writes=(), tag=""):
        return self.op("pool", fn, reads, writes, tag=tag)

    def dma(self, queue, fn, reads=(), writes=(), tag="", chan="d",
            serial=False):
        return self.op(queue, fn, reads, writes, dma=True, tag=tag,
                       chan=chan, serial=serial)

    def _resolve(self):
        n = len(self.ops)
        ctrs = sorted({o.ctr for o in self.ops})
        cidx = {c: i for i, c in enumerate(ctrs)}
        nc_ = len(ctrs)
        seq = [0] * n
        cnt = {c: 0 for c in ctrs}
        for o in self.ops:
            cnt[o.ctr] += 1
            seq[o.idx] = cnt[o.ctr]
        ZERO = tuple([0] * nc_)

        def vmax(a, b):
            return tuple(x if x >= y else y for x, y in zip(a, b))

        start_know = [ZERO] * n
        complete = [ZERO] * n
        prev_on_stream = {}
        for o in self.ops:
            i = o.idx
            know = ZERO
            p = prev_on_stream.get(o.stream)
            if p is not None:
                po = self.ops[p]
                know = start_know[p] if (":" in po.ctr) else complete[p]
            for d in o.deps:
                if know[cidx[self.ops[d].ctr]] >= seq[d]:
                    continue
                do = self.ops[d]
                if do.stream == o.stream and not (":" in do.ctr) \
                        and not (":" in o.ctr):
                    know = vmax(know, complete[d])
                    continue
                o.waits.append(d)
                know = vmax(know, complete[d])
            start_know[i] = know
            me = list(know)
            j = cidx[o.ctr]
            if seq[o.idx] > me[j]:
                me[j] = seq[o.idx]
            complete[i] = tuple(me)
            prev_on_stream[o.stream] = i
        marked = set()
        for o in self.ops:
            if (":" in o.ctr):
                marked.add(o.idx)
            for d in o.waits:
                marked.add(d)
        val = {}
        mcnt = {c: 0 for c in ctrs}
        for o in self.ops:
            if o.idx in marked:
                mcnt[o.ctr] += 1
                step = 16 if (":" in o.ctr) else 1
                o.inc_val = mcnt[o.ctr] * step
                val[o.idx] = o.inc_val
        for o in self.ops:
            best = {}
            for d in o.waits:
                c, v = self.ops[d].ctr, val[d]
                if best.get(c, -1) < v:
                    best[c] = v
            o.waits = sorted(best.items())
        return ctrs

    def emit(self, nc, es):
        ctrs = self._resolve()
        sems = {c: es.enter_context(nc.semaphore("sem_" + c)) for c in ctrs}
        by_stream = {s: [] for s in STREAMS}
        for o in self.ops:
            by_stream[o.stream].append(o)
        stats = {
            "n_ops": len(self.ops),
            "n_waits": sum(len(o.waits) for o in self.ops),
            "n_incs": sum(1 for o in self.ops if o.inc_val is not None),
        }

        def make_emitter(stream):
            def emitter(eng):
                for o in by_stream[stream]:
                    for c, v in o.waits:
                        eng.wait_ge(sems[c], v)
                    inst = o.fn(eng)
                    if o.inc_val is not None:
                        if inst is None:
                            raise RuntimeError(
                                f"op {o.idx} ({o.tag}) needs inc, fn gave None")
                        inst.then_inc(sems[o.ctr],
                                      16 if (":" in o.ctr) else 1)
            return emitter

        return {st: make_emitter(st) for st in STREAMS}, stats


S = 4096
D = 1024
NCH = 8
NQ = 4
QT = S // NQ
DFF = 4096
NEXP = 8

TRI_C = 0
B2SB_C = 2048
WRT_C = 3072
CON_C = 3584
BASE_A = 4096
XN_C = BASE_A
QTB_C = XN_C + 8 * 4096
KTB_C = QTB_C + 4096
O2P_C = KTB_C + 4096
V2A_C = O2P_C + 4096
PB_C = V2A_C + 4224
PBB_C = 61568
WQS_C = PB_C + 2048
WKS_C = WQS_C + 2048
WVS_C = WKS_C + 2048
TOP_A = WVS_C + 2048
XN2_C = BASE_A
W1S_C = XN2_C + 8 * 1024
W2S_C = W1S_C + 16384
H_C = W2S_C + 16384
H0_C = H_C + 4096
O2H_C = H0_C + 1024
WOS_C = O2H_C + 4096
MKB_C = WOS_C + 8192
MK8_C = MKB_C + 1024
MRE_C = MK8_C + 1024
TOP_B = MRE_C + 1024
XB_C = TOP_B
XQ_C = XB_C + 1024
NB = XQ_C + 1024

CP_C = 0
OUTB_C = 16
BR8_C = 24
ONF_C = 128
ROW_C = 512
XF_C = ROW_C + 4 * 512
T0_C = XF_C + 8 * 512
XO_C = T0_C + 1024
LGT_C = XO_C + 1024
TMA_C = T0_C
TMB_C = XO_C
XA_C = LGT_C + 1024
B1S_C = XA_C + 8 * 1024
NF = B1S_C + 1024


def _pages(arena, c0, c1):
    return [(arena, p) for p in range(c0 // 512, (c1 + 511) // 512)]


def build_program(mode="full"):
    nc = bass.Bass("TRN2", target_bir_lowering=False, debug=False)
    xT = nc.dram_tensor("xT", [D, S], F32, kind="ExternalInput").ap()
    wq = nc.dram_tensor("wq", [D, D], F32, kind="ExternalInput").ap()
    wk = nc.dram_tensor("wk", [D, D], F32, kind="ExternalInput").ap()
    wv = nc.dram_tensor("wv", [D, D], F32, kind="ExternalInput").ap()
    wo = nc.dram_tensor("wo", [D, D], F32, kind="ExternalInput").ap()
    outb = nc.dram_tensor("outb", [128, 8], F32, kind="ExternalInput").ap()
    wrt = nc.dram_tensor("wrt", [D, NEXP], F32, kind="ExternalInput").ap()
    br8 = nc.dram_tensor("br8", [NEXP, 1], F32, kind="ExternalInput").ap()
    w1t = nc.dram_tensor("w1t", [NEXP, D, DFF], F32, kind="ExternalInput").ap()
    b1c = nc.dram_tensor("b1c", [NEXP, 128, 32], F32, kind="ExternalInput").ap()
    w2t = nc.dram_tensor("w2t", [NEXP, DFF, D], F32, kind="ExternalInput").ap()
    b2e = nc.dram_tensor("b2e", [NEXP, D], F32, kind="ExternalInput").ap()
    tri = nc.dram_tensor("tri", [128, 2048], BF16, kind="ExternalInput").ap()
    onesb = nc.dram_tensor("onesb", [128, 2], BF16, kind="ExternalInput").ap()
    o2s = nc.dram_tensor("o2s", [D, S], BF16, kind="Internal").ap()
    outT = nc.dram_tensor("outT", [D, S], F32, kind="ExternalOutput").ap()
    if mode == "dbg":
        dbg_xac = nc.dram_tensor("dbg_xac", [D, S], F32, kind="ExternalOutput").ap()
        dbg_xn2 = nc.dram_tensor("dbg_xn2", [D, S], BF16, kind="ExternalOutput").ap()
        dbg_lgt = nc.dram_tensor("dbg_lgt", [8, S], F32, kind="ExternalOutput").ap()
        dbg_mk8 = nc.dram_tensor("dbg_mk8", [8, S], BF16, kind="ExternalOutput").ap()

    s = Sched()
    with ExitStack() as es:
        AB = es.enter_context(nc.sbuf_tensor("AB", [128, NB], BF16))
        AF32 = es.enter_context(nc.sbuf_tensor("AF32", [128, NF], F32))
        scoreP = es.enter_context(nc.psum_tensor("scoreP", [128, 2048], F32))
        pA = es.enter_context(nc.psum_tensor("pA", [128, 512], F32))
        pB = es.enter_context(nc.psum_tensor("pB", [128, 512], F32))
        pC = es.enter_context(nc.psum_tensor("pC", [128, 512], F32))
        pD = es.enter_context(nc.psum_tensor("pD", [128, 512], F32))
        def ps_res(t, off=0):
            if t is scoreP:
                return [("PS", off // 512)]
            return [("PS", {id(pA): 4, id(pB): 5, id(pC): 6, id(pD): 7}[id(t)])]

        def bf(c0, w, r0=0, rn=128):
            return AB[r0:rn, c0:c0 + w]

        def f32(c0, w, r0=0, rn=128):
            return AF32[r0:rn, c0:c0 + w]

        def bpg(c0, w):
            return _pages("B", c0, c0 + w)

        def fpg(c0, w):
            return _pages("F", c0, c0 + w)

        s.dma("sync", lambda e: e.dma_start(bf(TRI_C, 2048), tri),
              writes=bpg(TRI_C, 2048), tag="tri", chan="ini", serial=True)
        s.dma("sync", lambda e: e.dma_start(bf(CON_C, 2, 0, 128), onesb),
              writes=bpg(CON_C, 512), tag="ones", chan="ini", serial=True)
        s.dma("sync", lambda e: e.dma_start(f32(OUTB_C, 8), outb),
              writes=fpg(0, 512), tag="outb", chan="ini", serial=True)
        s.dma("sync", lambda e: e.dma_start(f32(BR8_C, 1, 0, 8), br8),
              writes=fpg(0, 512), tag="br8", chan="ini", serial=True)
        s.dma("pool", lambda e: e.dma_start(bf(B2SB_C, 1024, 0, 8), b2e),
              writes=bpg(B2SB_C, 1024), tag="b2", chan="ini", serial=True)
        s.dma("pool", lambda e: e.dma_start(
            bf(WRT_C, 64).rearrange("p (cc e) -> p cc e", e=8),
            wrt.rearrange("(cc p) e -> p cc e", p=128)),
              writes=bpg(WRT_C, 512), tag="wrt", chan="ini", serial=True)
        s.dve(lambda e: e.memset(f32(CP_C, 1), 0.125), writes=fpg(0, 512))
        s.dve(lambda e: e.memset(f32(CP_C + 1, 1), -12.0), writes=fpg(0, 512))
        s.dve(lambda e: e.memset(f32(CP_C + 2, 1), 1e-5), writes=fpg(0, 512))
        s.dve(lambda e: e.memset(f32(CP_C + 3, 1), 1.0 / 1024.0), writes=fpg(0, 512))
        s.dve(lambda e: e.memset(f32(CP_C + 4, 1), -1.0), writes=fpg(0, 512))
        s.dve(lambda e: e.memset(f32(ONF_C, 128, 0, 1), 1.0), writes=fpg(ONF_C, 128))
        CPQ = fpg(0, 512)
        qscale = f32(CP_C, 1)
        ebias = f32(CP_C + 1, 1)
        lneps = f32(CP_C + 2, 1, 0, 1)
        inv1024 = f32(CP_C + 3, 1, 0, 1)
        neg1 = f32(CP_C + 4, 1, 0, 1)
        onesf = f32(ONF_C, 128, 0, 1)
        onescol_b = bf(CON_C, 1, 0, 128)
        onesr = nc.dram_tensor("onesr", [1, 128], BF16, kind="ExternalInput").ap()
        s.dma("sync", lambda e: e.dma_start(bf(CON_C + 2, 128, 0, 1), onesr),
              writes=bpg(CON_C, 512), tag="onesr", chan="ini", serial=True)
        onesrow_b = bf(CON_C + 2, 128, 0, 1)

        for tt in range(8):
            ts = slice(tt * 512, (tt + 1) * 512)
            tpar = tt % 2
            S1 = scoreP[0:1, tpar * 1024:tpar * 1024 + 512]
            S2 = scoreP[0:1, tpar * 1024 + 512:tpar * 1024 + 1024]
            S1R = [("PS", 2 * tpar)]
            S2R = [("PS", 2 * tpar + 1)]
            bA = pA if tpar == 0 else pC
            bB = pB if tpar == 0 else pD
            for cc in range(NCH):
                sl = cc % 2
                xfc = XF_C + cc * 512
                xfv = f32(xfc, 512)
                s.dma("sync", lambda e, cc=cc, ts=ts, xfv=xfv: e.dma_start(
                    xfv, xT[cc * 128:(cc + 1) * 128, ts]),
                    writes=fpg(xfc, 512), tag="lnx", chan="xs")
                xbv = bf(XB_C + sl * 512, 512)
                s.pool(lambda e, xfv=xfv, xbv=xbv: e.tensor_copy(xbv, xfv),
                       reads=fpg(xfc, 512),
                       writes=bpg(XB_C + sl * 512, 512), tag="lncast")
                xqv = bf(XQ_C + sl * 512, 512)
                s.act(lambda e, xbv=xbv, xqv=xqv: e.activation(xqv, xbv, AF.Square),
                      reads=bpg(XB_C + sl * 512, 512),
                      writes=bpg(XQ_C + sl * 512, 512), tag="lnsq")
                s.pe(lambda e, xbv=xbv, cc=cc, S1=S1: e.matmul(
                        S1, onescol_b, xbv, start=(cc == 0), stop=(cc == 7)),
                     reads=bpg(XB_C + sl * 512, 512) + bpg(CON_C, 512),
                     writes=S1R, tag="lns1")
                s.pe(lambda e, xqv=xqv, cc=cc, S2=S2: e.matmul(
                        S2, onescol_b, xqv, start=(cc == 0), stop=(cc == 7)),
                     reads=bpg(XQ_C + sl * 512, 512) + bpg(CON_C, 512),
                     writes=S2R, tag="lns2")
            r = [f32(ROW_C + i * 512, 512, 0, 1) for i in range(4)]
            RP = [fpg(ROW_C + i * 512, 512) for i in range(4)]
            s.dve(lambda e, r=r, S1=S1: e.tensor_scalar(r[0], S1, inv1024, None, ALU.mult),
                  reads=S1R + CPQ, writes=RP[0], tag="lnm")
            s.dve(lambda e, r=r, S2=S2: e.tensor_scalar(r[1], S2, inv1024, None, ALU.mult),
                  reads=S2R + CPQ, writes=RP[1], tag="lnq")
            s.dve(lambda e, r=r: e.tensor_tensor(r[2], r[0], r[0], ALU.mult),
                  reads=RP[0], writes=RP[2], tag="lnm2")
            s.dve(lambda e, r=r: e.tensor_tensor(r[1], r[1], r[2], ALU.subtract),
                  reads=RP[1] + RP[2], writes=RP[1], tag="lnvar")
            s.act(lambda e, r=r: e.activation(r[2], r[1], AF.Sqrt, bias=lneps),
                  reads=RP[1] + CPQ, writes=RP[2], tag="lnsd")
            s.dve(lambda e, r=r: e.reciprocal(r[3], r[2]),
                  reads=RP[2], writes=RP[3], tag="lnrstd")
            s.dve(lambda e, r=r: e.scalar_tensor_tensor(r[2], r[0], neg1, r[3],
                                                        ALU.mult, ALU.mult),
                  reads=RP[0] + RP[3] + CPQ, writes=RP[2], tag="lnb")
            s.pe(lambda e, r=r, bA=bA: e.matmul(bA[:, :], onesf, r[3], start=True, stop=True),
                 reads=RP[3] + fpg(ONF_C, 128), writes=ps_res(bA), tag="lnbca")
            s.pe(lambda e, r=r, bB=bB: e.matmul(bB[:, :], onesf, r[2], start=True, stop=True),
                 reads=RP[2] + fpg(ONF_C, 128), writes=ps_res(bB), tag="lnbcb")
            for cc in range(NCH):
                xfc = XF_C + cc * 512
                xfv = f32(xfc, 512)
                t0v = f32(T0_C + (cc % 2) * 512, 512)
                s.dve(lambda e, xfv=xfv, t0v=t0v, bA=bA: e.tensor_tensor(t0v, xfv, bA[:, :], ALU.mult),
                      reads=fpg(xfc, 512) + ps_res(bA),
                      writes=fpg(T0_C + (cc % 2) * 512, 512), tag="lnn1")
                xnv = bf(XN_C + cc * 4096 + tt * 512, 512)
                s.dve(lambda e, xnv=xnv, t0v=t0v, bB=bB: e.tensor_tensor(xnv, t0v, bB[:, :], ALU.add),
                      reads=fpg(T0_C + (cc % 2) * 512, 512) + ps_res(bB),
                      writes=bpg(XN_C + cc * 4096 + tt * 512, 512), tag="lnn2")

        xn = [bf(XN_C + cc * 4096, 4096) for cc in range(NCH)]
        xn_pg = [bpg(XN_C + cc * 4096, 4096) for cc in range(NCH)]
        v2a_full = bf(V2A_C, 4160)
        s.dve(lambda e: e.memset(
            v2a_full.rearrange("p (tc w) -> p tc w", w=130)[:, :, 64:65], 1.0),
            writes=bpg(V2A_C, 4160), tag="v2ones1")
        s.dve(lambda e: e.memset(
            v2a_full.rearrange("p (tc w) -> p tc w", w=130)[:, :, 129:130], 1.0),
            writes=bpg(V2A_C, 4160), tag="v2ones2")

        evrot = [0]

        def evac_bank():
            b = evrot[0] % 4
            evrot[0] += 1
            return scoreP[:, b * 512:(b + 1) * 512], [("PS", b)]

        for hp in range(8):
            par = hp % 2
            wq_s = bf(WQS_C + par * 1024, 1024)
            wk_s = bf(WKS_C + par * 1024, 1024)
            wv_s = bf(WVS_C + par * 1024, 1024)
            wq_pg = bpg(WQS_C + par * 1024, 1024)
            wk_pg = bpg(WKS_C + par * 1024, 1024)
            wv_pg = bpg(WVS_C + par * 1024, 1024)
            for (w_s, w_pg, wsrc, nm) in ((wq_s, wq_pg, wq, "wq"),
                                          (wk_s, wk_pg, wk, "wk"),
                                          (wv_s, wv_pg, wv, "wv")):
                src = wsrc.rearrange("(cc p) d -> p cc d", p=128)[:, :, hp * 128:(hp + 1) * 128]
                s.dma("pool", lambda e, w_s=w_s, src=src: e.dma_start(
                    w_s.rearrange("p (cc d) -> p cc d", d=128), src),
                    writes=w_pg, tag=nm, chan="wqv")
            qtb = bf(QTB_C, 4096)
            ktb = bf(KTB_C, 4096)
            for (dst_c, w_s, w_pg, sc, nm) in ((QTB_C, wq_s, wq_pg, qscale, "qt"),
                                               (KTB_C, wk_s, wk_pg, None, "kt")):
                for tt in range(8):
                    ts = slice(tt * 512, (tt + 1) * 512)
                    bank, bres = evac_bank()
                    for cc in range(NCH):
                        s.pe(lambda e, bank=bank, w_s=w_s, cc=cc, ts=ts: e.matmul(
                                bank, w_s[:, cc * 128:(cc + 1) * 128], xn[cc][:, ts],
                                start=(cc == 0), stop=(cc == 7)),
                             reads=w_pg + xn_pg[cc], writes=bres, tag=nm)
                    dst = bf(dst_c + tt * 512, 512)
                    if sc is None:
                        s.act(lambda e, dst=dst, bank=bank: e.copy(dst, bank),
                              reads=bres, writes=bpg(dst_c + tt * 512, 512), tag=nm + "ev")
                    else:
                        s.act(lambda e, dst=dst, bank=bank, sc=sc: e.activation(
                                dst, bank, AF.Copy, scale=sc),
                              reads=bres + CPQ, writes=bpg(dst_c + tt * 512, 512),
                              tag=nm + "ev")
            for tc in range(32):
                bank = (pC, ps_res(pC)) if tc % 2 == 0 else (pD, ps_res(pD))
                bk, bres = bank
                for cc in range(NCH):
                    s.pe(lambda e, bk=bk, cc=cc, tc=tc, wv_s=wv_s: e.matmul(
                            bk[:, 0:128], xn[cc][:, tc * 128:(tc + 1) * 128],
                            wv_s[:, cc * 128:(cc + 1) * 128],
                            start=(cc == 0), stop=(cc == 7)),
                         reads=wv_pg + xn_pg[cc], writes=bres, tag="v")
                va = bf(V2A_C + tc * 130, 64)
                vb = bf(V2A_C + tc * 130 + 65, 64)
                s.dve(lambda e, va=va, bk=bk: e.tensor_copy(va, bk[:, 0:64]),
                      reads=bres, writes=bpg(V2A_C + tc * 130, 64), tag="vev")
                s.dve(lambda e, vb=vb, bk=bk: e.tensor_copy(vb, bk[:, 64:128]),
                      reads=bres, writes=bpg(V2A_C + tc * 130 + 65, 64), tag="vev")
            PBC = [[PB_C, PBB_C], [PB_C + 1024, PBB_C + 1024]]
            SLOTC = [PB_C, PBB_C]

            def emit_tail_recips(qt_i):
                for h in range(2):
                    obank = pA if h == 0 else pB
                    r0 = bf(63616 + h * 512, 512, 0, 1)

                    def recip_fn(e, obank=obank, r0=r0):
                        with nc.allow_low_precision(
                                reason="bf16 softmax normalizer broadcast"):
                            return e.reciprocal(r0, obank[64:65, :])
                    s.dve(recip_fn,
                          reads=ps_res(obank), writes=bpg(63616 + h * 512, 512),
                          tag="recip")

            def emit_tail_rest(qt_i):
                for h in range(2):
                    hb = 64 * h
                    obank = pA if h == 0 else pB
                    ob_res = ps_res(obank)
                    r0 = bf(63616 + h * 512, 512, 0, 1)
                    rb = pC if h == 0 else pD
                    s.pe(lambda e, r0=r0, rb=rb: e.matmul(
                            rb[0:64, :], onesrow_b[0:1, 0:64], r0,
                            start=True, stop=True),
                         reads=bpg(63616 + h * 512, 512) + bpg(CON_C, 512),
                         writes=ps_res(rb), tag="rbc")
                    dst = bf(O2P_C + qt_i * 512, 512, hb, hb + 64)
                    s.act(lambda e, dst=dst, obank=obank: e.copy(
                            dst, obank[0:64, :]),
                          reads=ob_res, writes=bpg(O2P_C + qt_i * 512, 512),
                          tag="oev")
                    s.dve(lambda e, dst=dst, rb=rb: e.tensor_tensor(
                            dst, dst, rb[0:64, :], ALU.mult),
                          reads=bpg(O2P_C + qt_i * 512, 512) + ps_res(rb),
                          writes=bpg(O2P_C + qt_i * 512, 512), tag="onorm")

            def emit_pv(prev_g, slot, first, last):
                for h in range(2):
                    obank = pA if h == 0 else pB
                    pbc = PBC[h][slot]
                    ppb = bf(pbc, 1024)
                    for j in range(2):
                        kc = 2 * prev_g + j
                        s.pe(lambda e, kc=kc, h=h, ppb=ppb, j=j,
                             st=(first and j == 0), sp=(last and j == 1),
                             obank=obank: e.matmul(
                                obank[0:65, :],
                                v2a_full[:, kc * 130 + 65 * h:
                                         kc * 130 + 65 * h + 65],
                                ppb[:, j * 512:(j + 1) * 512],
                                start=st, stop=sp),
                             reads=bpg(V2A_C + kc * 130, 130) + bpg(pbc, 1024),
                             writes=ps_res(obank), tag="pv")

            pending_tail = None
            for qt_i in range(8):
                qts = slice(qt_i * 512, (qt_i + 1) * 512)
                ng = 2 * (qt_i + 1)
                order = list(range(ng - 2, ng)) + list(range(0, ng - 2))
                if pending_tail is not None:
                    emit_tail_recips(pending_tail)
                prev = None
                npv = 0
                for idx, g in enumerate(order):
                    slot = idx % 2
                    for h in range(2):
                        hb = 64 * h
                        for j in range(2):
                            kc = 2 * g + j
                            bi = 2 * h + j
                            sb = scoreP[:, bi * 512:(bi + 1) * 512]
                            s.pe(lambda e, sb=sb, kc=kc, hb=hb, qts=qts,
                                 ktb=ktb, qtb=qtb: e.matmul(
                                    sb, ktb[hb:hb + 64, kc * 128:(kc + 1) * 128],
                                    qtb[hb:hb + 64, qts], start=True, stop=True),
                                 reads=bpg(KTB_C, 4096) +
                                       bpg(QTB_C + qt_i * 512, 512),
                                 writes=[("PS", bi)], tag="sc")
                    for h in range(2):
                        pbc = PBC[h][slot]
                        pbv = bf(pbc, 1024)
                        s.act(lambda e, pbv=pbv, h=h: e.activation(
                                pbv, scoreP[:, 2 * h * 512:(2 * h + 2) * 512],
                                AF.Exp, bias=ebias),
                              reads=[("PS", 2 * h), ("PS", 2 * h + 1)] + CPQ,
                              writes=bpg(pbc, 1024), tag="exp")
                        if g >= ng - 2:
                            dj = (g - (ng - 2))
                            trv = bf(TRI_C + dj * 1024, 1024)
                            s.dve(lambda e, pbv=pbv, trv=trv: e.tensor_tensor(
                                    pbv, pbv, trv, ALU.mult),
                                  reads=bpg(pbc, 1024) +
                                        bpg(TRI_C + dj * 1024, 1024),
                                  writes=bpg(pbc, 1024), tag="trim")
                    if prev is not None:
                        if pending_tail is not None:
                            emit_tail_rest(pending_tail)
                            pending_tail = None
                        emit_pv(prev[0], prev[1], npv == 0, False)
                        npv += 1
                    prev = (g, slot)
                if pending_tail is not None:
                    emit_tail_rest(pending_tail)
                    pending_tail = None
                emit_pv(prev[0], prev[1], npv == 0, True)
                pending_tail = qt_i
            emit_tail_recips(pending_tail)
            emit_tail_rest(pending_tail)
            pending_tail = None
            s.dma("sync", lambda e, hp=hp: e.dma_start(
                o2s[hp * 128:(hp + 1) * 128, :], bf(O2P_C, 4096)),
                reads=bpg(O2P_C, 4096), writes=[("o2s", hp)], tag="o2st", chan="o2w")

        for hp in range(8):
            s.dma("pool", lambda e, hp=hp: e.dma_start(
                bf(WOS_C + hp * 1024, 1024), wo[hp * 128:(hp + 1) * 128, :]),
                writes=bpg(WOS_C + hp * 1024, 1024), tag="wo", chan="wo")

        brot = [0]

        def rot_bank(group):
            if group == "z":
                b = brot[0] % 2
                brot[0] += 1
                return scoreP[:, b * 512:(b + 1) * 512], [("PS", b)]
            if group == "y":
                b = brot[0] % 2
                brot[0] += 1
                return scoreP[:, (2 + b) * 512:(3 + b) * 512], [("PS", 2 + b)]
            raise ValueError

        urot = [0]

        def u_bank():
            b = urot[0] % 2
            urot[0] += 1
            t = pC if b == 0 else pD
            return t[:, :], ps_res(t)

        xacc = [f32(XA_C + c * 1024, 1024) for c in range(NCH)]
        xacc_pg = [fpg(XA_C + c * 1024, 1024) for c in range(NCH)]
        xn2 = [bf(XN2_C + c * 1024, 1024) for c in range(NCH)]
        xn2_pg = [bpg(XN2_C + c * 1024, 1024) for c in range(NCH)]

        for q in range(NQ):
            qc = q * QT
            for tp in range(2):
                c0 = qc + tp * 512
                o2h = bf(O2H_C, 4096)
                s.dma("sync", lambda e, c0=c0, o2h=o2h: e.dma_start(
                    o2h.rearrange("p (hp t) -> p hp t", t=512),
                    o2s.rearrange("(hp p) t -> p hp t", p=128)[:, :, c0:c0 + 512]),
                    reads=[("o2s", hp) for hp in range(8)],
                    writes=bpg(O2H_C, 4096), tag="o2h", chan="o2h")
                for do in range(NCH):
                    bank, bres = evac_bank()
                    for hp in range(8):
                        s.pe(lambda e, bank=bank, hp=hp, do=do, o2h=o2h: e.matmul(
                                bank, bf(WOS_C + hp * 1024 + do * 128, 128),
                                o2h[:, hp * 512:(hp + 1) * 512],
                                start=(hp == 0), stop=(hp == 7)),
                             reads=bpg(WOS_C + hp * 1024, 1024) + bpg(O2H_C, 4096),
                             writes=bres, tag="wop")
                    xov = f32(XO_C + (do % 2) * 512, 512)
                    s.dma("sync", lambda e, xov=xov, do=do, c0=c0: e.dma_start(
                        xov, xT[do * 128:(do + 1) * 128, c0:c0 + 512]),
                        writes=fpg(XO_C + (do % 2) * 512, 512), tag="xo", chan="xs")
                    dst = f32(XA_C + do * 1024 + tp * 512, 512)
                    s.dve(lambda e, dst=dst, bank=bank, do=do, xov=xov:
                          e.scalar_tensor_tensor(dst, bank, f32(OUTB_C + do, 1),
                                                 xov, ALU.add, ALU.add),
                          reads=bres + fpg(XO_C + (do % 2) * 512, 512) + fpg(0, 512),
                          writes=fpg(XA_C + do * 1024 + tp * 512, 512), tag="b1acc")
            if mode == "attn":
                for do in range(NCH):
                    s.dma("sync", lambda e, do=do, qc=qc: e.dma_start(
                        outT[do * 128:(do + 1) * 128, qc:qc + QT], xacc[do]),
                        reads=xacc_pg[do], writes=[("out", q, do)], tag="st", chan="st")
                continue
            for tp in range(2):
                S1 = scoreP[0:1, tp * 1024:tp * 1024 + 512]
                S2 = scoreP[0:1, tp * 1024 + 512:tp * 1024 + 1024]
                S1R = [("PS", 2 * tp)]
                S2R = [("PS", 2 * tp + 1)]
                bA = pA if tp == 0 else pC
                bB = pB if tp == 0 else pD
                for cc in range(NCH):
                    src = f32(XA_C + cc * 1024 + tp * 512, 512)
                    xbv = bf(XB_C + (cc % 2) * 512, 512)
                    s.pool(lambda e, src=src, xbv=xbv: e.tensor_copy(xbv, src),
                           reads=fpg(XA_C + cc * 1024 + tp * 512, 512),
                           writes=bpg(XB_C + (cc % 2) * 512, 512), tag="l2cast")
                    xqv = bf(XQ_C + (cc % 2) * 512, 512)
                    s.act(lambda e, xbv=xbv, xqv=xqv: e.activation(xqv, xbv, AF.Square),
                          reads=bpg(XB_C + (cc % 2) * 512, 512),
                          writes=bpg(XQ_C + (cc % 2) * 512, 512), tag="l2sq")
                    s.pe(lambda e, xbv=xbv, cc=cc, S1=S1: e.matmul(
                            S1, onescol_b, xbv, start=(cc == 0), stop=(cc == 7)),
                         reads=bpg(XB_C + (cc % 2) * 512, 512) + bpg(CON_C, 512),
                         writes=S1R, tag="l2s1")
                    s.pe(lambda e, xqv=xqv, cc=cc, S2=S2: e.matmul(
                            S2, onescol_b, xqv, start=(cc == 0), stop=(cc == 7)),
                         reads=bpg(XQ_C + (cc % 2) * 512, 512) + bpg(CON_C, 512),
                         writes=S2R, tag="l2s2")
                r = [f32(ROW_C + i * 512, 512, 0, 1) for i in range(4)]
                RP = [fpg(ROW_C + i * 512, 512) for i in range(4)]
                s.dve(lambda e, r=r, S1=S1: e.tensor_scalar(r[0], S1, inv1024, None, ALU.mult),
                      reads=S1R + CPQ, writes=RP[0], tag="l2m")
                s.dve(lambda e, r=r, S2=S2: e.tensor_scalar(r[1], S2, inv1024, None, ALU.mult),
                      reads=S2R + CPQ, writes=RP[1], tag="l2q")
                s.dve(lambda e, r=r: e.tensor_tensor(r[2], r[0], r[0], ALU.mult),
                      reads=RP[0], writes=RP[2], tag="l2m2")
                s.dve(lambda e, r=r: e.tensor_tensor(r[1], r[1], r[2], ALU.subtract),
                      reads=RP[1] + RP[2], writes=RP[1], tag="l2var")
                s.act(lambda e, r=r: e.activation(r[2], r[1], AF.Sqrt, bias=lneps),
                      reads=RP[1] + CPQ, writes=RP[2], tag="l2sd")
                s.dve(lambda e, r=r: e.reciprocal(r[3], r[2]),
                      reads=RP[2], writes=RP[3], tag="l2rstd")
                s.dve(lambda e, r=r: e.scalar_tensor_tensor(r[2], r[0], neg1, r[3],
                                                            ALU.mult, ALU.mult),
                      reads=RP[0] + RP[3] + CPQ, writes=RP[2], tag="l2b")
                s.pe(lambda e, r=r, bA=bA: e.matmul(bA[:, :], onesf, r[3], start=True, stop=True),
                     reads=RP[3] + fpg(ONF_C, 128), writes=ps_res(bA), tag="l2bca")
                s.pe(lambda e, r=r, bB=bB: e.matmul(bB[:, :], onesf, r[2], start=True, stop=True),
                     reads=RP[2] + fpg(ONF_C, 128), writes=ps_res(bB), tag="l2bcb")
                for cc in range(NCH):
                    src = f32(XA_C + cc * 1024 + tp * 512, 512)
                    t0v = f32(T0_C + (cc % 2) * 512, 512)
                    s.dve(lambda e, src=src, t0v=t0v, bA=bA: e.tensor_tensor(t0v, src, bA[:, :], ALU.mult),
                          reads=fpg(XA_C + cc * 1024 + tp * 512, 512) + ps_res(bA),
                          writes=fpg(T0_C + (cc % 2) * 512, 512), tag="l2n1")
                    dst = bf(XN2_C + cc * 1024 + tp * 512, 512)
                    s.dve(lambda e, dst=dst, t0v=t0v, bB=bB: e.tensor_tensor(dst, t0v, bB[:, :], ALU.add),
                          reads=fpg(T0_C + (cc % 2) * 512, 512) + ps_res(bB),
                          writes=bpg(XN2_C + cc * 1024 + tp * 512, 512), tag="l2n2")
            lgt = f32(LGT_C, 1024, 0, 8)
            for tp in range(2):
                for cc in range(NCH):
                    s.pe(lambda e, cc=cc, tp=tp: e.matmul(
                            pC[0:8, :], bf(WRT_C + cc * 8, 8),
                            xn2[cc][:, tp * 512:(tp + 1) * 512],
                            start=(cc == 0), stop=(cc == 7)),
                         reads=bpg(WRT_C, 512) + xn2_pg[cc], writes=ps_res(pC),
                         tag="rt")
                dst = f32(LGT_C + tp * 512, 512, 0, 8)
                s.dve(lambda e, dst=dst: e.tensor_scalar(
                        dst, pC[0:8, :], f32(BR8_C, 1, 0, 8), None, ALU.add),
                      reads=ps_res(pC) + fpg(0, 512),
                      writes=fpg(LGT_C + tp * 512, 512), tag="rtev")
            tA = f32(TMA_C, 1024, 0, 8)
            tB = f32(TMB_C, 1024, 0, 8)
            LGP = fpg(LGT_C, 1024)
            TAP = fpg(TMA_C, 1024)
            TBP = fpg(TMB_C, 1024)
            s.dma("sync", lambda e: e.dma_start(tA[0:4, :], lgt[4:8, :]),
                  reads=LGP, writes=TAP, tag="mt1", chan="rt", serial=True)
            s.dve(lambda e: e.tensor_tensor(tB[0:4, :], lgt[0:4, :], tA[0:4, :], ALU.max),
                  reads=LGP + TAP, writes=TBP, tag="mt2")
            s.dma("sync", lambda e: e.dma_start(tA[0:2, :], tB[2:4, :]),
                  reads=TBP, writes=TAP, tag="mt3", chan="rt", serial=True)
            s.dve(lambda e: e.tensor_tensor(tA[0:2, :], tB[0:2, :], tA[0:2, :], ALU.max),
                  reads=TBP + TAP, writes=TAP, tag="mt4")
            s.dma("sync", lambda e: e.dma_start(tB[0:1, :], tA[1:2, :]),
                  reads=TAP, writes=TBP, tag="mt5", chan="rt", serial=True)
            s.dve(lambda e: e.tensor_tensor(tB[0:1, :], tA[0:1, :], tB[0:1, :], ALU.max),
                  reads=TAP + TBP, writes=TBP, tag="mt6")
            s.dma("sync", lambda e: e.dma_start(tA[0:1, :], tB[0:1, :]),
                  reads=TBP, writes=TAP, tag="mb1", chan="rt", serial=True)
            s.dma("sync", lambda e: e.dma_start(tA[1:2, :], tA[0:1, :]),
                  reads=TAP, writes=TAP, tag="mb2", chan="rt", serial=True)
            s.dma("sync", lambda e: e.dma_start(tA[2:4, :], tA[0:2, :]),
                  reads=TAP, writes=TAP, tag="mb3", chan="rt", serial=True)
            s.dma("sync", lambda e: e.dma_start(tA[4:8, :], tA[0:4, :]),
                  reads=TAP, writes=TAP, tag="mb4", chan="rt", serial=True)
            mk8 = bf(MK8_C, 1024, 0, 8)
            MK8P = bpg(MK8_C, 1024)
            s.dve(lambda e: e.tensor_tensor(mk8, lgt, tA, ALU.is_ge),
                  reads=LGP + TAP, writes=MK8P, tag="mk8")
            if mode == "dbg":
                for cc in range(NCH):
                    s.dma("sync", lambda e, cc=cc, qc=qc: e.dma_start(
                        dbg_xac[cc * 128:(cc + 1) * 128, qc:qc + QT], xacc[cc]),
                        reads=xacc_pg[cc], writes=[("dxa", q, cc)], tag="dxa", chan="st")
                    s.dma("sync", lambda e, cc=cc, qc=qc: e.dma_start(
                        dbg_xn2[cc * 128:(cc + 1) * 128, qc:qc + QT], xn2[cc]),
                        reads=xn2_pg[cc], writes=[("dxn", q, cc)], tag="dxn", chan="st")
                s.dma("sync", lambda e, qc=qc: e.dma_start(
                    dbg_lgt[:, qc:qc + QT], lgt), reads=LGP,
                    writes=[("dlg", q)], tag="dlg", chan="st")
                s.dma("sync", lambda e, qc=qc: e.dma_start(
                    dbg_mk8[:, qc:qc + QT], mk8), reads=MK8P,
                    writes=[("dmk", q)], tag="dmk", chan="st")
            wpar = [0]
            for ei in range(NEXP):
                mre = bf(MRE_C, 1024, 0, 1)
                s.dma("sync", lambda e, ei=ei, mre=mre: e.dma_start(mre, mk8[ei:ei + 1, :]),
                      reads=MK8P, writes=bpg(MRE_C, 1024), tag="mre", chan="rt", serial=True)
                for tp in range(2):
                    ub, ures = u_bank()
                    s.pe(lambda e, ub=ub, mre=mre, tp=tp: e.matmul(
                            ub, onesrow_b, mre[0:1, tp * 512:(tp + 1) * 512],
                            start=True, stop=True),
                         reads=bpg(MRE_C, 1024) + bpg(CON_C, 512), writes=ures,
                         tag="mbc")
                    s.act(lambda e, ub=ub, tp=tp: e.copy(bf(MKB_C + tp * 512, 512), ub),
                          reads=ures, writes=bpg(MKB_C + tp * 512, 512), tag="mbev")
                maskb = bf(MKB_C, 1024)
                MKBP = bpg(MKB_C, 1024)
                for si in range(4):
                    par = wpar[0] % 2
                    wpar[0] += 1
                    w1v = bf(W1S_C + par * 8192, 8192)
                    w2v = bf(W2S_C + par * 8192, 8192)
                    W1P = bpg(W1S_C + par * 8192, 8192)
                    W2P = bpg(W2S_C + par * 8192, 8192)
                    src1 = w1t[ei].rearrange("(cc p) f -> p cc f", p=128)[:, :, si * 1024:(si + 1) * 1024]
                    s.dma("pool", lambda e, w1v=w1v, src1=src1: e.dma_start(
                        w1v.rearrange("p (cc f) -> p cc f", f=1024), src1),
                        writes=W1P, tag="w1ld", chan="w")
                    src2 = w2t[ei][si * 1024:(si + 1) * 1024, :].rearrange(
                        "(fc p) d -> p fc d", p=128)
                    s.dma("pool", lambda e, w2v=w2v, src2=src2: e.dma_start(
                        w2v.rearrange("p (fc d) -> p fc d", d=1024), src2),
                        writes=W2P, tag="w2ld", chan="w")
                    b1v = f32(B1S_C + par * 512, 8)
                    B1P = fpg(B1S_C + par * 512, 512)
                    s.dma("sync", lambda e, b1v=b1v, ei=ei, si=si: e.dma_start(
                        b1v, b1c[ei][:, si * 8:(si + 1) * 8]),
                        writes=B1P, tag="b1ld", chan="b1", serial=True)
                    for tp in range(2):
                        tps = slice(tp * 512, (tp + 1) * 512)
                        for fc in range(8):
                            zb, zres = rot_bank("z")
                            for cc in range(NCH):
                                s.pe(lambda e, zb=zb, w1v=w1v, cc=cc, fc=fc, tps=tps:
                                     e.matmul(zb, w1v[:, cc * 1024 + fc * 128:
                                                      cc * 1024 + (fc + 1) * 128],
                                              xn2[cc][:, tps],
                                              start=(cc == 0), stop=(cc == 7)),
                                     reads=W1P + xn2_pg[cc], writes=zres, tag="z")
                            h0v = bf(H0_C + (fc % 2) * 512, 512)
                            s.act(lambda e, h0v=h0v, zb=zb, b1v=b1v, fc=fc:
                                  e.activation(h0v, zb, AF.Relu, bias=b1v[:, fc:fc + 1]),
                                  reads=zres + B1P,
                                  writes=bpg(H0_C + (fc % 2) * 512, 512), tag="relu")
                            hv = bf(H_C + fc * 512, 512)
                            s.dve(lambda e, hv=hv, h0v=h0v, tp=tp: e.tensor_tensor(
                                    hv, h0v, maskb[:, tp * 512:(tp + 1) * 512], ALU.mult),
                                  reads=bpg(H0_C + (fc % 2) * 512, 512) + MKBP,
                                  writes=bpg(H_C + fc * 512, 512), tag="hm")
                        for do in range(NCH):
                            yb, yres = rot_bank("y")
                            for fc in range(8):
                                s.pe(lambda e, yb=yb, w2v=w2v, fc=fc, do=do:
                                     e.matmul(yb, w2v[:, fc * 1024 + do * 128:
                                                      fc * 1024 + (do + 1) * 128],
                                              bf(H_C + fc * 512, 512),
                                              start=(fc == 0), stop=(fc == 7)),
                                     reads=W2P + bpg(H_C + fc * 512, 512),
                                     writes=yres, tag="y")
                            dst = f32(XA_C + do * 1024 + tp * 512, 512)
                            s.dve(lambda e, dst=dst, yb=yb: e.tensor_tensor(
                                    dst, yb, dst, ALU.add),
                                  reads=yres + fpg(XA_C + do * 1024 + tp * 512, 512),
                                  writes=fpg(XA_C + do * 1024 + tp * 512, 512),
                                  tag="yacc")
            for tp in range(2):
                for do in range(NCH):
                    ub, ures = u_bank()
                    s.pe(lambda e, ub=ub, do=do, tp=tp: e.matmul(
                            ub, bf(B2SB_C + do * 128, 128, 0, 8),
                            mk8[:, tp * 512:(tp + 1) * 512], start=True, stop=True),
                         reads=bpg(B2SB_C, 1024) + MK8P, writes=ures, tag="b2")
                    dst = f32(XA_C + do * 1024 + tp * 512, 512)
                    s.dve(lambda e, dst=dst, ub=ub: e.tensor_tensor(dst, ub, dst, ALU.add),
                          reads=ures + fpg(XA_C + do * 1024 + tp * 512, 512),
                          writes=fpg(XA_C + do * 1024 + tp * 512, 512), tag="b2acc")
            for do in range(NCH):
                s.dma("sync", lambda e, do=do, qc=qc: e.dma_start(
                    outT[do * 128:(do + 1) * 128, qc:qc + QT], xacc[do]),
                    reads=xacc_pg[do], writes=[("out", q, do)], tag="st", chan="st")

        fin_reads = [("out", q, do) for q in range(NQ) for do in range(NCH)]
        if mode == "dbg":
            fin_reads += [("dxa", q, cc) for q in range(NQ) for cc in range(NCH)]
            fin_reads += [("dxn", q, cc) for q in range(NQ) for cc in range(NCH)]
            fin_reads += [("dlg", q) for q in range(NQ)]
            fin_reads += [("dmk", q) for q in range(NQ)]
        s.op("sync", lambda e: e.nop(), reads=fin_reads, tag="fin")

        emitters, stats = s.emit(nc, es)
        blk = es.enter_context(nc.Block())
        blk.tensor(emitters["pe"])
        blk.scalar(emitters["act"])
        blk.vector(emitters["dve"])
        blk.gpsimd(emitters["pool"])
        blk.sync(emitters["sync"])
    return nc, stats


def make_host_inputs(L0_attn_batch, L0_attn_w_query, L0_attn_w_key,
                     L0_attn_w_value, L0_attn_out_w, L0_attn_out_b,
                     L0_ffn_Wr, L0_ffn_br, L0_ffn_W1, L0_ffn_b1,
                     L0_ffn_W2, L0_ffn_b2):
    bf16 = ml_dtypes.bfloat16
    x = np.asarray(L0_attn_batch, dtype=np.float32)
    shared = {
        "wq": np.ascontiguousarray(np.asarray(L0_attn_w_query, np.float32)),
        "wk": np.ascontiguousarray(np.asarray(L0_attn_w_key, np.float32)),
        "wv": np.ascontiguousarray(np.asarray(L0_attn_w_value, np.float32)),
        "wo": np.ascontiguousarray(np.asarray(L0_attn_out_w, np.float32)),
        "outb": np.ascontiguousarray(
            np.asarray(L0_attn_out_b, np.float32).reshape(8, 128).T),
        "wrt": np.ascontiguousarray(np.asarray(L0_ffn_Wr, np.float32).T),
        "br8": np.ascontiguousarray(np.asarray(L0_ffn_br, np.float32).reshape(8, 1)),
        "w1t": np.ascontiguousarray(
            np.asarray(L0_ffn_W1, np.float32).transpose(0, 2, 1)),
        "b1c": np.ascontiguousarray(
            np.asarray(L0_ffn_b1, np.float32).reshape(8, 32, 128).transpose(0, 2, 1)),
        "w2t": np.ascontiguousarray(
            np.asarray(L0_ffn_W2, np.float32).transpose(0, 2, 1)),
        "b2e": np.ascontiguousarray(np.asarray(L0_ffn_b2, np.float32)),
    }
    k_idx = np.arange(128)[:, None]
    q_idx = np.arange(512)[None, :]
    tri = np.zeros((128, 2048), dtype=bf16)
    for j in range(4):
        tri[:, j * 512:(j + 1) * 512] = ((128 * j + k_idx) <= q_idx).astype(bf16)
    shared["tri"] = tri
    shared["onesb"] = np.ones((128, 2), dtype=bf16)
    shared["onesr"] = np.ones((1, 128), dtype=bf16)
    in_maps = []
    for i in range(8):
        m = dict(shared)
        m["xT"] = np.ascontiguousarray(x[i].T)
        in_maps.append(m)
    return in_maps


_CACHE = {}


def kernel(**inputs):
    in_maps = make_host_inputs(**inputs)
    if "nc" not in _CACHE:
        _CACHE["nc"] = build_program("full")[0]
    nc = _CACHE["nc"]
    res = run_bass_kernel_spmd(nc, in_maps, core_ids=list(range(8)))
    out = np.stack([np.asarray(r["outT"], dtype=np.float32).T
                    for r in res.results])
    return np.ascontiguousarray(out)
